# version 12
# baseline (speedup 1.0000x reference)
import math
from contextlib import ExitStack

import numpy as np
import ml_dtypes

import concourse.bass as bass
import concourse.mybir as mybir
from concourse.bass_utils import run_bass_kernel_spmd

F32 = mybir.dt.float32
BF16 = mybir.dt.bfloat16
E4 = mybir.dt.float8e4
AF = mybir.ActivationFunctionType
ALU = mybir.AluOpType
AX = mybir.AxisListType
U32 = mybir.dt.uint32
DR = mybir.MatmulPerfMode.DoubleRow

D = 2048
HID = 8192
NH, HD = 16, 128
SCALE = float(HD) ** -0.5
EPS = 1e-5
GA = float(np.float32(ml_dtypes.bfloat16(0.044715)))
GC = float(np.float32(ml_dtypes.bfloat16(math.sqrt(2.0 / math.pi))))
NEG = -1.0e30

ENGS = ("te", "dve", "act", "sp", "actq")
QUEUES = ("sp", "actq")


class Builder:

    def __init__(self):
        self.ops = []

    def op(self, eng, fn, reads=(), writes=(), tag="", fence=False):
        self.ops.append(dict(eng=eng, fn=fn, reads=tuple(reads), writes=tuple(writes),
                             tag=tag, barrier=False, idx=len(self.ops), fence=fence))

    def dma(self, eng, dst, src, reads=(), writes=(), tag="", **kw):
        assert eng in QUEUES
        self.op(eng, (lambda e, d=dst, s=src: e.dma_start(d, s, **kw)), reads, writes, tag)

    def barrier(self):
        self.ops.append(dict(eng=None, fn=None, reads=(), writes=(), tag="bar",
                             barrier=True, idx=len(self.ops)))

    def schedule(self):
        writers = {}
        readers = {}
        nbar = 0
        self._next_te = {}
        last_te = None
        for o in self.ops:
            if o["barrier"]:
                last_te = None
                continue
            if o["eng"] == "te":
                if last_te is not None:
                    self._next_te[last_te["idx"]] = o
                last_te = o
        for o in self.ops:
            if o["barrier"]:
                nbar += 1
                o["barn"] = nbar
                writers.clear()
                readers.clear()
                continue
            deps = []
            for k in o["reads"]:
                deps.extend(writers.get(k, ()))
                if k.startswith("P!"):
                    deps.extend(r for r in readers.get(k, ())
                                if r["eng"] != o["eng"])
            war = []
            for k in o["writes"]:
                deps.extend(writers.get(k, ()))
                war.extend(readers.get(k, ()))
            bumped = []
            for d in war:
                if d["eng"] == "te":
                    nxt = self._next_te.get(d["idx"])
                    if nxt is not None and nxt["idx"] < o["idx"]:
                        bumped.append(nxt)
                    else:
                        bumped.append(d)
                else:
                    bumped.append(d)
            deps.extend(bumped)
            need = {}
            for d in deps:
                if (d["eng"] == o["eng"] and d["eng"] not in QUEUES
                        and not d.get("fence")):
                    continue
                need[d["idx"]] = d
            o["deps"] = list(need.values())
            for d in o["deps"]:
                d["inc"] = True
                assert d["idx"] < o["idx"]
            for k in o["reads"]:
                readers.setdefault(k, []).append(o)
            for k in o["writes"]:
                writers[k] = [o]
                readers[k] = []
        cnt = {e: 0 for e in ENGS}
        for o in self.ops:
            if o["barrier"]:
                continue
            if o["eng"] in QUEUES:
                cnt[o["eng"]] += 16
                o["semval"] = cnt[o["eng"]]
                o["inc"] = True
            elif o.get("inc"):
                cnt[o["eng"]] += 1
                o["semval"] = cnt[o["eng"]]
        self.final_counts = dict(cnt)
        self.nbar = nbar

    def emit_engine(self, eng_names, engine, sems, barsem, nbeng):
        hw = {}
        bar_seen = 0
        for o in self.ops:
            if o["barrier"]:
                bar_seen += 1
                for qn in eng_names:
                    if qn in QUEUES:
                        q = self._qcount_at(o["idx"], qn)
                        if q > hw.get(qn, 0):
                            engine.wait_ge(sems[qn], q)
                            hw[qn] = q
                engine.nop().then_inc(barsem, 1)
                engine.wait_ge(barsem, nbeng * bar_seen)
                continue
            if o["eng"] not in eng_names:
                continue
            for d in o.get("deps", ()):
                sname = d["eng"]
                v = d["semval"]
                if v > hw.get(sname, 0):
                    engine.wait_ge(sems[sname], v)
                    hw[sname] = v
            instr = o["fn"](engine)
            if o.get("inc"):
                if o["eng"] in QUEUES:
                    instr.then_inc(sems[o["eng"]], 16)
                else:
                    instr.then_inc(sems[o["eng"]], 1)

    def _qcount_at(self, idx, qname):
        c = 0
        for o in self.ops:
            if o["idx"] >= idx:
                break
            if (not o["barrier"]) and o["eng"] == qname:
                c += 16
        return c


def ap(t, off, dims):
    return bass.AP(t, off, dims)


def slices(span, step=512):
    return [(s, min(span, s + step)) for s in range(0, span, step)]


def groups4(n):
    return [(g * 4, min(n, g * 4 + 4)) for g in range((n + 3) // 4)]


def build_nc(T, max_barriers=None, max_ops=None, dbg=False):
    NT = T // 128
    ND = D // 128
    skind = dict(kind="ExternalOutput") if dbg else {}
    nc = bass.Bass("TRN2", target_bir_lowering=False, detect_race_conditions=False,
                   dynamic_dma_scratch_size=2048)

    x_h = nc.dram_tensor("x", [T, D], F32, kind="ExternalInput")
    qw_h = nc.dram_tensor("qw", [D, D], F32, kind="ExternalInput")
    kw_h = nc.dram_tensor("kw", [D, D], F32, kind="ExternalInput")
    vw_h = nc.dram_tensor("vw", [D, D], F32, kind="ExternalInput")
    ow_h = nc.dram_tensor("ow", [D, D], F32, kind="ExternalInput")
    qb_h = nc.dram_tensor("qb", [D], F32, kind="ExternalInput")
    kb_h = nc.dram_tensor("kb", [D], F32, kind="ExternalInput")
    vb_h = nc.dram_tensor("vb", [D], F32, kind="ExternalInput")
    ob_h = nc.dram_tensor("ob", [D], F32, kind="ExternalInput")
    wg_h = nc.dram_tensor("wg", [2, D, HID], F32, kind="ExternalInput")
    wl_h = nc.dram_tensor("wl", [HID, D], F32, kind="ExternalInput")
    id32_h = nc.dram_tensor("id32", [128, 128], F32, kind="ExternalInput")
    id16_h = nc.dram_tensor("id16", [128, 128], BF16, kind="ExternalInput")
    tri_h = nc.dram_tensor("tri", [128, 128], F32, kind="ExternalInput")
    out_h = nc.dram_tensor("out", [T, D], F32, kind="ExternalOutput")

    xq_s = nc.dram_tensor("xq_s", [D, T], E4, **skind)
    QH_s = nc.dram_tensor("QH_s", [D, T], BF16, **skind)
    QL_s = nc.dram_tensor("QL_s", [D, T], BF16, **skind)
    KH_s = nc.dram_tensor("KH_s", [D, T], BF16, **skind)
    KL_s = nc.dram_tensor("KL_s", [D, T], BF16, **skind)
    VH_s = nc.dram_tensor("VH_s", [NH, T, HD], BF16, **skind)
    VL_s = nc.dram_tensor("VL_s", [NH, T, HD], BF16, **skind)
    CH_s = nc.dram_tensor("CH_s", [D, T], BF16, **skind)
    CL_s = nc.dram_tensor("CL_s", [D, T], BF16, **skind)
    FF_s = nc.dram_tensor("FF_s", [T, D], BF16, **skind)

    B = Builder()

    with ExitStack() as top:
        e = top.enter_context
        id32 = e(nc.sbuf_tensor("id32s", [128, 128], F32))
        id16 = e(nc.sbuf_tensor("id16s", [128, 128], BF16))
        tri = e(nc.sbuf_tensor("tris", [128, 128], F32))
        qb_s = e(nc.sbuf_tensor("qbs", [128, ND], F32))
        kb_s = e(nc.sbuf_tensor("kbs", [128, ND], F32))
        vb_s = e(nc.sbuf_tensor("vbs", [128, ND], F32))
        ones1 = e(nc.sbuf_tensor("ones1", [1, 128], F32))
        sems = {}
        for s in ENGS:
            sems[s] = e(nc.semaphore("sem_" + s))
        barsem = e(nc.semaphore("sem_bar"))

        B.dma("sp", id32[:, :], id32_h.ap(), writes=["id32"])
        B.dma("sp", id16[:, :], id16_h.ap(), writes=["id16"])
        B.dma("sp", tri[:, :], tri_h.ap(), writes=["tri"])
        for bs, bh, key in ((qb_s, qb_h, "qb"), (kb_s, kb_h, "kb"), (vb_s, vb_h, "vb")):
            B.dma("sp", ap(bs, 0, [[ND, 128], [1, ND]]),
                  ap(bh, 0, [[1, 128], [128, ND]]), writes=[key],
                  allow_slow_non_contiguous=True)
        B.op("dve", lambda v: v.memset(ones1[:, :], 1.0), writes=["ones1"])

        if True:
            with ExitStack() as pab:
              xnh = pab.enter_context(nc.sbuf_tensor("xnh", [128, ND * T], BF16))
              xnl = pab.enter_context(nc.sbuf_tensor("xnl", [128, ND * T], BF16))
              with ExitStack() as pa:
                  ea = pa.enter_context
                  xt = [ea(nc.sbuf_tensor(f"xt{i}", [128, D], F32)) for i in range(2)]
                  dbuf = [ea(nc.sbuf_tensor(f"dbuf{i}", [128, D], F32)) for i in range(2)]
                  sqb = ea(nc.sbuf_tensor("sqb", [128, D], F32))
                  xnt = [ea(nc.sbuf_tensor(f"xnt{i}", [128, D], F32)) for i in range(2)]
                  st_sum = ea(nc.sbuf_tensor("st_sum", [128, 1], F32))
                  st_negm = ea(nc.sbuf_tensor("st_negm", [128, 1], F32))
                  st_vsum = ea(nc.sbuf_tensor("st_vsum", [128, 1], F32))
                  st_ve = ea(nc.sbuf_tensor("st_ve", [128, 1], F32))
                  st_sv = ea(nc.sbuf_tensor("st_sv", [128, 1], F32))
                  st_r1 = ea(nc.sbuf_tensor("st_r1", [128, 1], F32))
                  st_n1 = ea(nc.sbuf_tensor("st_n1", [128, 1], F32))
                  st_n3 = ea(nc.sbuf_tensor("st_n3", [128, 1], F32))
                  st_rs = ea(nc.sbuf_tensor("st_rs", [128, 1], F32))
                  xqst = [ea(nc.sbuf_tensor(f"xqst{i}", [128, 512], E4)) for i in range(2)]
                  ptp = [ea(nc.psum_tensor(f"ptp{i}", [128, 512], F32)) for i in range(2)]

                  for ti in range(NT):
                      p2 = ti % 2
                      B.dma("sp", xt[p2][:, :], x_h.ap()[ti * 128:(ti + 1) * 128, :],
                            writes=[f"xt{p2}"], tag=f"palod{ti}")
                      B.op("dve", lambda v, p=p2: v.tensor_reduce(
                          st_sum[:, :], xt[p][:, :], AX.X, ALU.add),
                          reads=[f"xt{p2}"], writes=["stats"], fence=True)
                      B.op("dve", lambda v: v.tensor_scalar(
                          st_negm[:, :], st_sum[:, :], -1.0 / D, None, ALU.mult),
                          reads=["stats"], writes=["stats"], fence=True)
                      B.op("dve", lambda v, p=p2: v.tensor_scalar(
                          dbuf[p][:, :], xt[p][:, :], st_negm[:, :], None, ALU.add),
                          reads=[f"xt{p2}", "stats"], writes=[f"dbuf{p2}"])
                      B.op("act", lambda a, p=p2: a.activation(
                          sqb[:, :], dbuf[p][:, :], AF.Square),
                          reads=[f"dbuf{p2}"], writes=["sqb"], fence=True)
                      B.op("dve", lambda v: v.tensor_reduce(
                          st_vsum[:, :], sqb[:, :], AX.X, ALU.add),
                          reads=["sqb"], writes=["stats"], fence=True)
                      B.op("act", lambda a: a.activation(
                          st_ve[:, :], st_vsum[:, :], AF.Copy, scale=1.0 / D, bias=EPS),
                          reads=["stats"], writes=["stats"], fence=True)
                      B.op("act", lambda a: a.activation(st_sv[:, :], st_ve[:, :], AF.Sqrt),
                           reads=["stats"], writes=["stats"], fence=True)
                      B.op("dve", lambda v: v.reciprocal(st_r1[:, :], st_sv[:, :]),
                           reads=["stats"], writes=["stats"], fence=True)
                      B.op("dve", lambda v: v.tensor_mul(st_n1[:, :], st_r1[:, :], st_r1[:, :]),
                           reads=["stats"], writes=["stats"], fence=True)
                      B.op("dve", lambda v: v.tensor_mul(st_n3[:, :], st_n1[:, :], st_ve[:, :]),
                           reads=["stats"], writes=["stats"], fence=True)
                      B.op("dve", lambda v: v.tensor_scalar(
                          st_n1[:, :], st_n3[:, :], -0.5, 1.5, ALU.mult, ALU.add),
                          reads=["stats"], writes=["stats"], fence=True)
                      B.op("dve", lambda v: v.tensor_mul(st_rs[:, :], st_r1[:, :], st_n1[:, :]),
                           reads=["stats"], writes=["stats"], fence=True)
                      B.op("dve", lambda v, p=p2: v.tensor_scalar(
                          xnt[p][:, :], dbuf[p][:, :], st_rs[:, :], None, ALU.mult),
                          reads=[f"dbuf{p2}", "stats"], writes=[f"xnt{p2}"])
                      for g in range(4):
                          gp = g % 2
                          def mk_tr(p, g, gp):
                              def f(t):
                                  r = None
                                  for k in range(4):
                                      ci = g * 4 + k
                                      r = t.transpose(ptp[gp][:, k * 128:(k + 1) * 128],
                                                      xnt[p][:, ci * 128:(ci + 1) * 128],
                                                      id32[:, :])
                                  return r
                              return f
                          B.op("te", mk_tr(p2, g, gp), reads=[f"xnt{p2}", "id32"],
                               writes=[f"P!ptp{gp}"], tag=f"patr{ti}_{g}")
                          c0 = (g * 4) * T + ti * 128
                          def mk_hi(gp, c0):
                              def f(v):
                                  return v.tensor_copy(
                                      ap(xnh, c0, [[ND * T, 128], [T, 4], [1, 128]]),
                                      ptp[gp][:, :])
                              return f
                          def mk_lo(gp, c0):
                              def f(v):
                                  return v.tensor_sub(
                                      ap(xnl, c0, [[ND * T, 128], [T, 4], [1, 128]]),
                                      ptp[gp][:, :],
                                      ap(xnh, c0, [[ND * T, 128], [T, 4], [1, 128]]))
                              return f
                          B.op("dve", mk_hi(gp, c0), reads=[f"P!ptp{gp}"],
                               writes=["xnh"], tag=f"pahi{ti}_{g}")
                          B.op("dve", mk_lo(gp, c0), reads=[f"P!ptp{gp}", "xnh"],
                               writes=["xnl"], tag=f"palo{ti}_{g}")
                          B.op("act", lambda a, gp=gp, p=p2: a.activation(
                              xqst[gp][:, :], ptp[gp][:, :], AF.Copy),
                              reads=[f"P!ptp{gp}"], writes=[f"xqst{gp}"])
                          B.dma("actq",
                                ap(xq_s, (g * 4 * 128) * T + ti * 128,
                                   [[T, 128], [128 * T, 4], [1, 128]]),
                                ap(xqst[gp], 0, [[512, 128], [128, 4], [1, 128]]),
                                reads=[f"xqst{gp}"], tag=f"paxq{ti}_{g}")
                  B.barrier()

              with ExitStack() as pb:
                  eb = pb.enter_context
                  wrow = [eb(nc.sbuf_tensor(f"wrow{i}", [128, D], F32)) for i in range(2)]
                  wTh = [eb(nc.sbuf_tensor(f"wTh{i}", [128, D], BF16)) for i in range(2)]
                  wTl = [eb(nc.sbuf_tensor(f"wTl{i}", [128, D], BF16)) for i in range(2)]
                  qf = eb(nc.sbuf_tensor("qf", [128, T], F32))
                  qhi = eb(nc.sbuf_tensor("qhi", [128, T], BF16))
                  qlo = eb(nc.sbuf_tensor("qlo", [128, T], BF16))
                  vth_s = eb(nc.sbuf_tensor("vth_s", [128, T], BF16))
                  vtl_s = eb(nc.sbuf_tensor("vtl_s", [128, T], BF16))
                  wtp = [eb(nc.psum_tensor(f"wtp{i}", [128, 512], F32)) for i in range(2)]
                  pQ = [eb(nc.psum_tensor(f"pQ{i}", [128, min(1024, T)], F32)) for i in range(2)]
                  vtp = eb(nc.psum_tensor("vtp", [128, 512], BF16))

                  HALF = min(1024, T)
                  NHALF = T // HALF
                  for wi, (W_h, bkey, QHd, QLd) in enumerate((
                          (qw_h, "qb", QH_s, QL_s), (kw_h, "kb", KH_s, KL_s),
                          (vw_h, "vb", VH_s, VL_s))):
                      bias_s = (qb_s, kb_s, vb_s)[wi]
                      for m in range(ND):
                          p2 = m % 2
                          B.dma("sp", wrow[p2][:, :], W_h.ap()[m * 128:(m + 1) * 128, :],
                                writes=[f"wrow{p2}"], tag=f"pbw{wi}_{m}")
                          for g in range(4):
                              gp = g % 2
                              def mk_wtr(p, g, gp):
                                  def f(t):
                                      r = None
                                      for k in range(4):
                                          ci = g * 4 + k
                                          r = t.transpose(
                                              wtp[gp][:, k * 128:(k + 1) * 128],
                                              wrow[p][:, ci * 128:(ci + 1) * 128],
                                              id32[:, :])
                                      return r
                                  return f
                              B.op("te", mk_wtr(p2, g, gp),
                                   reads=[f"wrow{p2}", "id32"], writes=[f"P!wtp{gp}"])
                              B.op("dve", lambda v, p=p2, g=g, gp=gp:
                                   v.tensor_copy(wTh[p][:, g * 512:(g + 1) * 512],
                                                 wtp[gp][:, :]),
                                   reads=[f"P!wtp{gp}"], writes=[f"wTh{p2}"])
                              B.op("dve", lambda v, p=p2, g=g, gp=gp:
                                   v.tensor_sub(wTl[p][:, g * 512:(g + 1) * 512],
                                                wtp[gp][:, :],
                                                wTh[p][:, g * 512:(g + 1) * 512]),
                                   reads=[f"P!wtp{gp}", f"wTh{p2}"], writes=[f"wTl{p2}"])
                          postmul = SCALE if wi == 0 else 1.0
                          for hf in range(NHALF):
                              def mk_mm(p, hf, ci):
                                  def f(t):
                                      r = None
                                      for pss in range(3):
                                          lhs = (wTh, wTh, wTl)[pss][p][:, ci * 128:(ci + 1) * 128]
                                          rsrc = (xnh, xnl, xnh)[pss]
                                          for (s0, s1) in slices(HALF):
                                              col = ci * T + hf * HALF + s0
                                              r = t.matmul(
                                                  pQ[hf % 2][:, s0:s1],
                                                  lhs,
                                                  ap(rsrc, col, [[ND * T, 128], [1, s1 - s0]]),
                                                  start=(ci == 0 and pss == 0),
                                                  stop=(ci == ND - 1 and pss == 2))
                                      return r
                                  return f
                              for ci in range(ND):
                                  B.op("te", mk_mm(p2, hf, ci),
                                       reads=[f"wTh{p2}", f"wTl{p2}", "xnh", "xnl"],
                                       writes=[f"P!pQ{hf % 2}"], tag=f"pbmm{wi}_{m}_{hf}_{ci}")
                              B.op("dve", lambda v, hf=hf, m=m, b=bias_s, pm=postmul:
                                  v.tensor_scalar(
                                      qf[:, hf * HALF:(hf + 1) * HALF], pQ[hf % 2][:, :],
                                      b[:, m:m + 1], pm, ALU.add, ALU.mult),
                                  reads=[f"P!pQ{hf % 2}", bkey], writes=["qf"])
                              B.op("dve", lambda v, hf=hf: v.tensor_copy(
                                  qhi[:, hf * HALF:(hf + 1) * HALF],
                                  qf[:, hf * HALF:(hf + 1) * HALF]),
                                  reads=["qf"], writes=["qhi"])
                              B.op("dve", lambda v, hf=hf: v.tensor_sub(
                                  qlo[:, hf * HALF:(hf + 1) * HALF],
                                  qf[:, hf * HALF:(hf + 1) * HALF],
                                  qhi[:, hf * HALF:(hf + 1) * HALF]),
                                  reads=["qf", "qhi"], writes=["qlo"])
                          if wi < 2:
                              B.dma("actq", QHd.ap()[m * 128:(m + 1) * 128, :], qhi[:, :],
                                    reads=["qhi"], tag=f"pbsth{wi}_{m}")
                              B.dma("actq", QLd.ap()[m * 128:(m + 1) * 128, :], qlo[:, :],
                                    reads=["qlo"], tag=f"pbstl{wi}_{m}")
                          else:
                              for hl in range(2):
                                  src = (qhi, qlo)[hl]
                                  stg = (vth_s, vtl_s)[hl]
                                  skey = ("qhi", "qlo")[hl]
                                  tkey = ("vth_s", "vtl_s")[hl]
                                  Vd = (VH_s, VL_s)[hl]
                                  for (t0, t1) in groups4(NT):
                                      def mk_vtr(src, t0, t1):
                                          def f(t):
                                              r = None
                                              for tj in range(t0, t1):
                                                  r = t.transpose(
                                                      vtp[:, (tj - t0) * 128:(tj - t0 + 1) * 128],
                                                      src[:, tj * 128:(tj + 1) * 128],
                                                      id16[:, :])
                                              return r
                                          return f
                                      B.op("te", mk_vtr(src, t0, t1), reads=[skey, "id16"],
                                           writes=["P!vtp"])
                                      B.op("dve", lambda v, stg=stg, t0=t0, t1=t1: v.tensor_copy(
                                          stg[:, t0 * 128:t1 * 128],
                                          vtp[:, 0:(t1 - t0) * 128]),
                                          reads=["P!vtp"], writes=[tkey])
                                  nh2 = (NT + 1) // 2
                                  for half in range(2):
                                      t0 = half * nh2
                                      cnt = min(NT, t0 + nh2) - t0
                                      if cnt <= 0:
                                          continue
                                      B.dma("actq",
                                            ap(Vd, m * T * HD + t0 * 128 * HD,
                                               [[HD, 128], [128 * HD, cnt], [1, HD]]),
                                            ap(stg, t0 * 128, [[T, 128], [128, cnt], [1, HD]]),
                                            reads=[tkey], tag=f"pbstv{hl}_{m}_{half}")
                  B.barrier()

            with ExitStack() as pc:
                ec = pc.enter_context
                qhh = [ec(nc.sbuf_tensor(f"qhh{i}", [128, T], BF16)) for i in range(2)]
                qhl = [ec(nc.sbuf_tensor(f"qhl{i}", [128, T], BF16)) for i in range(2)]
                khh = [ec(nc.sbuf_tensor(f"khh{i}", [128, T], BF16)) for i in range(2)]
                khl = [ec(nc.sbuf_tensor(f"khl{i}", [128, T], BF16)) for i in range(2)]
                vthb = [ec(nc.sbuf_tensor(f"vthb{i}", [128, T], BF16)) for i in range(2)]
                vtlb = [ec(nc.sbuf_tensor(f"vtlb{i}", [128, T], BF16)) for i in range(2)]
                pbuf = [ec(nc.sbuf_tensor(f"pbuf{i}", [128, T], F32)) for i in range(2)]
                phib = [ec(nc.sbuf_tensor(f"phib{i}", [128, T], BF16)) for i in range(2)]
                plob = [ec(nc.sbuf_tensor(f"plob{i}", [128, T], BF16)) for i in range(2)]
                pth = ec(nc.sbuf_tensor("pth", [128, T], BF16))
                ptl = ec(nc.sbuf_tensor("ptl", [128, T], BF16))
                ctxh = ec(nc.sbuf_tensor("ctxh", [128, T], F32))
                cth = [ec(nc.sbuf_tensor(f"cth{i}", [128, T], BF16)) for i in range(2)]
                ctl = [ec(nc.sbuf_tensor(f"ctl{i}", [128, T], BF16)) for i in range(2)]
                a_negm = [ec(nc.sbuf_tensor(f"a_negm{i}", [128, 1], F32)) for i in range(2)]
                a_sum = [ec(nc.sbuf_tensor(f"a_sum{i}", [128, 1], F32)) for i in range(2)]
                a_rec = [ec(nc.sbuf_tensor(f"a_rec{i}", [128, 1], F32)) for i in range(2)]
                Sps = ec(nc.psum_tensor("P!Sps", [128, T if T <= 2048 else 2048], F32))
                ptpH = ec(nc.psum_tensor("P!ptpH", [128, 512], BF16))
                ptpL = ec(nc.psum_tensor("P!ptpL", [128, 512], BF16))
                ctxp = ec(nc.psum_tensor("P!ctxp", [128, 128], F32))

                def emit_head_loads(h):
                    p2 = h % 2
                    for (buf, S_h) in ((qhh[p2], QH_s), (qhl[p2], QL_s),
                                       (khh[p2], KH_s), (khl[p2], KL_s)):
                        B.dma("sp", buf[:, :], S_h.ap()[h * 128:(h + 1) * 128, :],
                              writes=[f"qkv{p2}"], tag=f"pcl{h}")
                    nh2 = (NT + 1) // 2
                    for (buf, S_h) in ((vthb[p2], VH_s), (vtlb[p2], VL_s)):
                        for half in range(2):
                            t0 = half * nh2
                            cnt = min(NT, t0 + nh2) - t0
                            if cnt <= 0:
                                continue
                            B.dma("sp", ap(buf, t0 * 128, [[T, 128], [128, cnt], [1, HD]]),
                                  ap(S_h, h * T * HD + t0 * 128 * HD,
                                     [[HD, 128], [128 * HD, cnt], [1, HD]]),
                                  writes=[f"qkv{p2}"], tag=f"pclv{h}_{half}")

                def emit_scores(h, at):
                    p2 = h % 2
                    bl = (at + 1) * 128
                    nslc = (bl + 511) // 512
                    def mk_sc(p, at, bl, nslc):
                        def f(t):
                            r = None
                            for pss in range(3):
                                lhs = (qhh, qhh, qhl)[pss][p][:, at * 128:(at + 1) * 128]
                                rsrc = (khh, khl, khh)[pss][p]
                                for s in range(nslc):
                                    c1 = min(bl, (s + 1) * 512)
                                    r = t.matmul(Sps[:, s * 512:c1],
                                                 lhs, rsrc[:, s * 512:c1],
                                                 start=(pss == 0), stop=(pss == 2))
                            return r
                        return f
                    B.op("te", mk_sc(p2, at, bl, nslc),
                         reads=[f"qkv{p2}"], writes=["P!Sps"], tag=f"pcsc{h}_{at}")

                def emit_softmax(h, at):
                    bl = (at + 1) * 128
                    q = at % 2
                    B.op("dve", lambda v, at=at: v.tensor_add(
                        Sps[:, at * 128:(at + 1) * 128],
                        Sps[:, at * 128:(at + 1) * 128], tri[:, :]),
                        reads=["P!Sps", "tri"], writes=["P!Sps"])
                    B.op("dve", lambda v, bl=bl, q=q: v.tensor_reduce(
                        a_negm[q][:, :], Sps[:, 0:bl], AX.X, ALU.max, negate=True),
                        reads=["P!Sps"], writes=[f"ast{q}"], fence=True)
                    B.op("act", lambda a, bl=bl, q=q: a.activation(
                        pbuf[q][:, 0:bl], Sps[:, 0:bl], AF.Exp,
                        bias=a_negm[q][:, :], scale=1.0),
                        reads=["P!Sps", f"ast{q}"], writes=[f"pbuf{q}"])
                    B.op("dve", lambda v, bl=bl, q=q: v.tensor_reduce(
                        a_sum[q][:, :], pbuf[q][:, 0:bl], AX.X, ALU.add),
                        reads=[f"pbuf{q}"], writes=[f"ast{q}"], fence=True)
                    B.op("dve", lambda v, q=q: v.reciprocal(a_rec[q][:, :], a_sum[q][:, :]),
                         reads=[f"ast{q}"], writes=[f"ast{q}"], fence=True)
                    B.op("act", lambda a, bl=bl, q=q: a.activation(
                        phib[q][:, 0:bl], pbuf[q][:, 0:bl], AF.Copy, scale=a_rec[q][:, :]),
                        reads=[f"pbuf{q}", f"ast{q}"], writes=[f"phib{q}"])
                    B.op("dve", lambda v, bl=bl, q=q: v.scalar_tensor_tensor(
                        plob[q][:, 0:bl], pbuf[q][:, 0:bl], a_rec[q][:, :], phib[q][:, 0:bl],
                        ALU.mult, ALU.subtract),
                        reads=[f"pbuf{q}", f"ast{q}", f"phib{q}"], writes=[f"plob{q}"])

                def emit_groups(h, at):
                    p2 = h % 2
                    q = at % 2
                    nb = at + 1
                    for g in range((nb + 3) // 4):
                        k0 = g * 4
                        k1 = min(nb, k0 + 4)
                        for hl in range(2):
                            src = (phib[q], plob[q])[hl]
                            skey = (f"phib{q}", f"plob{q}")[hl]
                            pp = (ptpH, ptpL)[hl]
                            pkey = ("P!ptpH", "P!ptpL")[hl]
                            dst = (pth, ptl)[hl]
                            dkey = ("pth", "ptl")[hl]
                            def mk_ptr(src, k0, k1, pp):
                                def f(t):
                                    r = None
                                    for k in range(k0, k1):
                                        r = t.transpose(
                                            pp[:, (k - k0) * 128:(k - k0 + 1) * 128],
                                            src[:, k * 128:(k + 1) * 128], id16[:, :])
                                    return r
                                return f
                            B.op("te", mk_ptr(src, k0, k1, pp),
                                 reads=[skey, "id16"], writes=[pkey])
                            B.op("act", lambda a, pp=pp, dst=dst, k0=k0, k1=k1:
                                 a.activation(dst[:, k0 * 128:k1 * 128],
                                              pp[:, 0:(k1 - k0) * 128], AF.Copy),
                                 reads=[pkey], writes=[dkey])
                        for k in range(k0, k1):
                            def mk_ctx(p, k, at):
                                def f(t):
                                    r = None
                                    for pss in range(3):
                                        lhs = (vthb, vthb, vtlb)[pss][p][:, k * 128:(k + 1) * 128]
                                        rsrc = (pth, ptl, pth)[pss]
                                        r = t.matmul(ctxp[:, :], lhs,
                                                     rsrc[:, k * 128:(k + 1) * 128],
                                                     start=(k == 0 and pss == 0),
                                                     stop=(k == at and pss == 2))
                                    return r
                                return f
                            B.op("te", mk_ctx(p2, k, at),
                                 reads=[f"qkv{p2}", "pth", "ptl"], writes=["P!ctxp"],
                                 tag=f"pcctx{h}_{at}_{k}")
                    B.op("act", lambda a, at=at: a.activation(
                        ctxh[:, at * 128:(at + 1) * 128], ctxp[:, :], AF.Copy),
                        reads=["P!ctxp"], writes=["ctxh"])

                emit_head_loads(0)
                for h in range(NH):
                    p2 = h % 2
                    emit_scores(h, 0)
                    for at in range(NT):
                        emit_softmax(h, at)
                        if at + 1 < NT:
                            emit_scores(h, at + 1)
                        emit_groups(h, at)
                        if h + 1 < NH and at == 0:
                            emit_head_loads(h + 1)
                    B.op("dve", lambda v, p=p2: v.tensor_copy(cth[p][:, :], ctxh[:, :]),
                         reads=["ctxh"], writes=[f"cth{p2}"])
                    B.op("dve", lambda v, p=p2: v.tensor_sub(
                        ctl[p][:, :], ctxh[:, :], cth[p][:, :]),
                        reads=["ctxh", f"cth{p2}"], writes=[f"ctl{p2}"])
                    B.dma("actq", CH_s.ap()[h * 128:(h + 1) * 128, :], cth[p2][:, :],
                          reads=[f"cth{p2}"], tag=f"pcsth{h}")
                    B.dma("actq", CL_s.ap()[h * 128:(h + 1) * 128, :], ctl[p2][:, :],
                          reads=[f"ctl{p2}"], tag=f"pcstl{h}")
                B.barrier()

            with ExitStack() as pe_outer:
              aq = pe_outer.enter_context(nc.sbuf_tensor("aq", [128, (HID // 128) * T], E4))
              with ExitStack() as pe:
                ee = pe.enter_context
                xq_sb = ee(nc.sbuf_tensor("xq_sb", [128, ND * T], E4))
                wgs = [[ee(nc.sbuf_tensor(f"wgs{g}{i}", [128, D], F32)) for i in range(2)]
                       for g in range(2)]
                wgq = [[ee(nc.sbuf_tensor(f"wgq{g}{i}", [128, D], E4)) for i in range(2)]
                       for g in range(2)]
                GH = min(1024, T)
                NGH = T // GH
                gb = [ee(nc.sbuf_tensor(f"gb{i}", [128, GH], BF16)) for i in range(2)]
                fb = [ee(nc.sbuf_tensor(f"fb{i}", [128, GH], BF16)) for i in range(2)]
                sA = [ee(nc.sbuf_tensor(f"sA{i}", [128, GH], BF16)) for i in range(2)]
                sB = [ee(nc.sbuf_tensor(f"sB{i}", [128, GH], BF16)) for i in range(2)]
                pg = [[ee(nc.psum_tensor(f"pg{g}{i}", [128, GH], F32)) for i in range(2)]
                      for g in range(2)]

                for ci in range(ND):
                    B.dma("sp", xq_sb[:, ci * T:(ci + 1) * T],
                          xq_s.ap()[ci * 128:(ci + 1) * 128, :],
                          writes=["xq_sb"], tag=f"pexq{ci}")
                B.op("dve", lambda v: v.tensor_copy(
                    xq_sb[:, :].bitcast(U32), xq_sb[:, :].bitcast(U32)),
                    reads=["xq_sb"], writes=["xq_sb"], tag="lxq")
                for m in range(HID // 128):
                    p2 = m % 2
                    for g in range(2):
                        for q4 in range(4):
                            B.dma("sp",
                                  ap(wgs[g][p2], q4 * 4 * 128,
                                     [[D, 128], [128, 4], [1, 128]]),
                                  ap(wg_h, g * D * HID + (q4 * 4 * 128) * HID + m * 128,
                                     [[HID, 128], [128 * HID, 4], [1, 128]]),
                                  writes=[f"wgs{g}{p2}"], tag=f"pewg{g}_{m}_{q4}")
                        B.op("dve", lambda v, g=g, p=p2: v.tensor_copy(
                            wgq[g][p][:, :], wgs[g][p][:, :]),
                            reads=[f"wgs{g}{p2}"], writes=[f"wgq{g}{p2}"])
                        B.op("dve", lambda v, g=g, p=p2: v.tensor_copy(
                            wgq[g][p][:, :].bitcast(U32), wgq[g][p][:, :].bitcast(U32)),
                            reads=[f"wgq{g}{p2}"], writes=[f"wgq{g}{p2}"], tag="lwg")
                    for hf in range(NGH):
                        for g in range(2):
                            def mk_gmm(g, p, hf):
                                def f(t):
                                    r = None
                                    for ci in range(0, ND, 2):
                                        for (s0, s1) in slices(GH):
                                            col = ci * T + hf * GH + s0
                                            r = t.matmul(
                                                pg[g][hf % 2][:, s0:s1],
                                                ap(wgq[g][p], ci * 128,
                                                   [[D, 128], [128, 2], [1, 128]]),
                                                ap(xq_sb, col,
                                                   [[ND * T, 128], [T, 2], [1, s1 - s0]]),
                                                start=(ci == 0), stop=(ci == ND - 2),
                                                perf_mode=DR)
                                    return r
                                return f
                            B.op("te", mk_gmm(g, p2, hf),
                                 reads=[f"wgq{g}{p2}", "xq_sb"],
                                 writes=[f"P!pg{g}{hf % 2}"], tag=f"pegm{g}_{m}_{hf}")
                        hp = hf % 2
                        pg0, pg1 = pg[0][hp], pg[1][hp]
                        r0, r1 = f"P!pg0{hp}", f"P!pg1{hp}"
                        gbk, fbk = f"gb{hp}", f"fb{hp}"
                        sak, sbk = f"sA{hp}", f"sB{hp}"
                        B.op("act", lambda a, hp=hp, pg0=pg0: a.activation(
                            gb[hp][:, :], pg0[:, :], AF.Copy), reads=[r0], writes=[gbk])
                        B.op("act", lambda a, hp=hp, pg1=pg1: a.activation(
                            fb[hp][:, :], pg1[:, :], AF.Copy), reads=[r1], writes=[fbk])
                        B.op("act", lambda a, hp=hp: a.activation(
                            sA[hp][:, :], gb[hp][:, :], AF.Square),
                            reads=[gbk], writes=[sak])
                        B.op("dve", lambda v, hp=hp: v.tensor_mul(
                            sB[hp][:, :], sA[hp][:, :], gb[hp][:, :]),
                            reads=[sak, gbk], writes=[sbk])
                        B.op("dve", lambda v, hp=hp: v.tensor_scalar(
                            sA[hp][:, :], sB[hp][:, :], GA, None, ALU.mult),
                            reads=[sbk], writes=[sak])
                        B.op("dve", lambda v, hp=hp: v.tensor_add(
                            sB[hp][:, :], sA[hp][:, :], gb[hp][:, :]),
                            reads=[sak, gbk], writes=[sbk])
                        B.op("dve", lambda v, hp=hp: v.tensor_scalar(
                            sA[hp][:, :], sB[hp][:, :], GC, None, ALU.mult),
                            reads=[sbk], writes=[sak])
                        B.op("act", lambda a, hp=hp: a.activation(
                            sB[hp][:, :], sA[hp][:, :], AF.Tanh),
                            reads=[sak], writes=[sbk])
                        B.op("dve", lambda v, hp=hp: v.tensor_scalar(
                            sA[hp][:, :], sB[hp][:, :], 0.5, 0.5, ALU.mult, ALU.add),
                            reads=[sbk], writes=[sak])
                        B.op("dve", lambda v, hp=hp: v.tensor_mul(
                            sB[hp][:, :], gb[hp][:, :], sA[hp][:, :]),
                            reads=[gbk, sak], writes=[sbk])
                        B.op("dve", lambda v, hp=hp: v.tensor_mul(
                            sA[hp][:, :], sB[hp][:, :], fb[hp][:, :]),
                            reads=[sbk, fbk], writes=[sak])
                        B.op("act", lambda a, hp=hp, m=m, hf=hf: a.activation(
                            aq[:, m * T + hf * GH:m * T + (hf + 1) * GH],
                            sA[hp][:, :], AF.Copy),
                            reads=[sak], writes=["aq"], tag=f"peaq{m}_{hf}")
                B.op("dve", lambda v: v.tensor_copy(
                    aq[:, :].bitcast(U32), aq[:, :].bitcast(U32)),
                    reads=["aq"], writes=["aq"], tag="laq")
                B.barrier()

              if dbg:
                  ffdbg_h = nc.dram_tensor("ff_fm_dbg", [D, T], BF16,
                                           kind="ExternalOutput")
                  aq_dbg = nc.dram_tensor("aq_dbg", [128, (HID // 128) * T], E4,
                                          kind="ExternalOutput")
                  B.dma("sp", aq_dbg.ap(), aq[:, :], reads=["aq"], tag="aqdbg")
                  B.barrier()

              with ExitStack() as pf:
                  ef = pf.enter_context
                  NH2 = HID // 128
                  wlt = ef(nc.sbuf_tensor("wlt", [128, NH2 * 128], F32))
                  wlq = [ef(nc.sbuf_tensor(f"wlq{i}", [128, NH2 * 128], E4)) for i in range(2)]
                  ffn16 = ef(nc.sbuf_tensor("ffn16", [128, T], BF16))
                  fts = ef(nc.sbuf_tensor("fts", [128, T], BF16))
                  LH = min(1024, T)
                  NLH = T // LH
                  pL = [[ef(nc.psum_tensor(f"pL{j}{i}", [128, LH], F32)) for i in range(NLH)]
                        for j in range(1)]
                  ftp = ef(nc.psum_tensor("P!ftp", [128, 512], BF16))

                  for j in range(ND):
                      jp = 0
                      wp = j % 2
                      for q4 in range(NH2 // 4):
                          B.dma("sp",
                                ap(wlt, q4 * 4 * 128, [[NH2 * 128, 128], [128, 4], [1, 128]]),
                                ap(wl_h, (q4 * 4 * 128) * D + j * 128,
                                   [[D, 128], [128 * D, 4], [1, 128]]),
                                writes=["wlt"], tag=f"pfw{j}_{q4}")
                      B.op("act", lambda a, wp=wp: a.activation(
                          wlq[wp][:, :], wlt[:, :], AF.Copy),
                          reads=["wlt"], writes=[f"wlq{wp}"])
                      B.op("dve", lambda v, wp=wp: v.tensor_copy(
                          wlq[wp][:, :].bitcast(U32), wlq[wp][:, :].bitcast(U32)),
                          reads=[f"wlq{wp}"], writes=[f"wlq{wp}"], tag="lwl")
                      for h2 in range(0, NH2, 2):
                          def mk_lmm(wp, jp, h2):
                              def f(t):
                                  r = None
                                  for hf in range(NLH):
                                      for (s0, s1) in slices(LH):
                                          col = h2 * T + hf * LH + s0
                                          r = t.matmul(
                                              pL[jp][hf][:, s0:s1],
                                              ap(wlq[wp], h2 * 128,
                                                 [[NH2 * 128, 128], [128, 2], [1, 128]]),
                                              ap(aq, col,
                                                 [[(HID // 128) * T, 128], [T, 2], [1, s1 - s0]]),
                                              start=(h2 == 0), stop=(h2 == NH2 - 2),
                                              perf_mode=DR)
                                  return r
                              return f
                          B.op("te", mk_lmm(wp, jp, h2),
                               reads=[f"wlq{wp}", "aq"], writes=[f"P!pL{jp}"],
                               tag=f"pflm{j}_{h2}")
                      for hf in range(NLH):
                          B.op("dve", lambda v, jp=jp, hf=hf: v.tensor_copy(
                              ffn16[:, hf * LH:(hf + 1) * LH], pL[jp][hf][:, :]),
                              reads=[f"P!pL{jp}"], writes=["ffn16"])
                      B.op("dve", lambda v: v.tensor_copy(
                          ffn16[:, :].bitcast(U32), ffn16[:, :].bitcast(U32)),
                          reads=["ffn16"], writes=["ffn16"], tag="lff")
                      if dbg:
                          B.dma("sp", ffdbg_h.ap()[j * 128:(j + 1) * 128, :],
                                ffn16[:, :], reads=["ffn16"], tag=f"ffdbg{j}")
                      for (t0, t1) in groups4(NT):
                          def mk_ftr(t0, t1):
                              def f(t):
                                  r = None
                                  for tj in range(t0, t1):
                                      r = t.transpose(ftp[:, (tj - t0) * 128:(tj - t0 + 1) * 128],
                                                      ffn16[:, tj * 128:(tj + 1) * 128],
                                                      id16[:, :])
                                  return r
                              return f
                          B.op("te", mk_ftr(t0, t1), reads=["ffn16", "id16"], writes=["P!ftp"])
                          B.op("act", lambda a, t0=t0, t1=t1: a.activation(
                              fts[:, t0 * 128:t1 * 128], ftp[:, 0:(t1 - t0) * 128], AF.Copy),
                              reads=["P!ftp"], writes=["fts"])
                      for (t0, t1) in groups4(NT):
                          B.dma("actq",
                                ap(FF_s, j * 128 + (t0 * 128) * D,
                                   [[D, 128], [128 * D, t1 - t0], [1, 128]]),
                                ap(fts, t0 * 128, [[T, 128], [128, t1 - t0], [1, 128]]),
                                reads=["fts"], tag=f"pfst{j}_{t0}")
                  B.barrier()

            with ExitStack() as pd:
                ed = pd.enter_context
                owth = ed(nc.sbuf_tensor("owth", [128, ND * D], BF16))
                owtl = ed(nc.sbuf_tensor("owtl", [128, ND * D], BF16))
                obb = ed(nc.sbuf_tensor("obb", [128, D], F32))
                with ExitStack() as pp:
                    ep = pp.enter_context
                    ob_row = ep(nc.sbuf_tensor("obrow", [1, D], F32))
                    pbc = ep(nc.psum_tensor("P!pbc", [128, D], F32))
                    B.dma("sp", ap(ob_row, 0, [[D, 1], [1, D]]),
                          ap(ob_h, 0, [[D, 1], [1, D]]), writes=["obrow"])
                    def mk_bc(s):
                        def f(t):
                            return t.matmul(pbc[:, s * 512:(s + 1) * 512], ones1[:, :],
                                            ob_row[:, s * 512:(s + 1) * 512],
                                            start=True, stop=True)
                        return f
                    for s in range(4):
                        B.op("te", mk_bc(s), reads=["ones1", "obrow"], writes=["P!pbc"])
                    B.op("dve", lambda v: v.tensor_copy(obb[:, :], pbc[:, :]),
                         reads=["P!pbc"], writes=["obb"])
                    wrow2 = [ep(nc.sbuf_tensor(f"wrow2{i}", [128, D], F32)) for i in range(2)]
                    wth2 = [ep(nc.sbuf_tensor(f"wth2{i}", [128, D], BF16)) for i in range(2)]
                    wtl2 = [ep(nc.sbuf_tensor(f"wtl2{i}", [128, D], BF16)) for i in range(2)]
                    wtp2 = [ep(nc.psum_tensor(f"wtp2{i}", [128, 512], BF16)) for i in range(2)]
                    for m in range(ND):
                        p2 = m % 2
                        B.dma("sp", wrow2[p2][:, :], ow_h.ap()[m * 128:(m + 1) * 128, :],
                              writes=[f"wrow2{p2}"], tag=f"pdw{m}")
                        B.op("dve", lambda v, p=p2: v.tensor_copy(
                            wth2[p][:, :], wrow2[p][:, :]),
                            reads=[f"wrow2{p2}"], writes=[f"wth2{p2}"])
                        B.op("dve", lambda v, p=p2: v.tensor_sub(
                            wtl2[p][:, :], wrow2[p][:, :], wth2[p][:, :]),
                            reads=[f"wrow2{p2}", f"wth2{p2}"], writes=[f"wtl2{p2}"])
                        for hl in range(2):
                            src = (wth2, wtl2)[hl]
                            dstT = (owth, owtl)[hl]
                            skey = (f"wth2{p2}", f"wtl2{p2}")[hl]
                            dkey = ("owth", "owtl")[hl]
                            for g in range(4):
                                gp = g % 2
                                def mk_otr(src, p, g, gp):
                                    def f(t):
                                        r = None
                                        for k in range(4):
                                            ci = g * 4 + k
                                            r = t.transpose(
                                                wtp2[gp][:, k * 128:(k + 1) * 128],
                                                src[p][:, ci * 128:(ci + 1) * 128],
                                                id16[:, :])
                                        return r
                                    return f
                                B.op("te", mk_otr(src, p2, g, gp),
                                     reads=[skey, "id16"], writes=[f"P!wtp2{gp}"])
                                c0 = (g * 4) * D + m * 128
                                def mk_oev(dstT, c0, gp):
                                    def f(v):
                                        return v.tensor_copy(
                                            ap(dstT, c0, [[ND * D, 128], [D, 4], [1, 128]]),
                                            wtp2[gp][:, :])
                                    return f
                                B.op("dve", mk_oev(dstT, c0, gp),
                                     reads=[f"P!wtp2{gp}"], writes=[dkey])
                    B.barrier()
                with ExitStack() as pm:
                    em = pm.enter_context
                    scth = [em(nc.sbuf_tensor(f"scth{i}", [128, D], BF16)) for i in range(2)]
                    sctl = [em(nc.sbuf_tensor(f"sctl{i}", [128, D], BF16)) for i in range(2)]
                    xt2 = [em(nc.sbuf_tensor(f"xt2{i}", [128, D], F32)) for i in range(2)]
                    ffnt = [em(nc.sbuf_tensor(f"ffnt{i}", [128, D], BF16)) for i in range(2)]
                    o1b = em(nc.sbuf_tensor("o1b", [128, D], F32))
                    o2b = em(nc.sbuf_tensor("o2b", [128, D], F32))
                    outsb = [em(nc.sbuf_tensor(f"outsb{i}", [128, D], F32)) for i in range(2)]
                    pO = [em(nc.psum_tensor(f"pO{i}", [128, D], F32)) for i in range(2)]

                    for ti in range(NT):
                        p2 = ti % 2
                        for hl in range(2):
                            Cs = (CH_s, CL_s)[hl]
                            dst = (scth, sctl)[hl]
                            dkey = (f"scth{p2}", f"sctl{p2}")[hl]
                            for q4 in range(ND // 4):
                                B.dma("actq",
                                      ap(dst[p2], q4 * 4 * 128,
                                         [[D, 128], [128, 4], [1, 128]]),
                                      ap(Cs, ti * 128 + (q4 * 4 * 128) * T,
                                         [[T, 128], [128 * T, 4], [1, 128]]),
                                      writes=[dkey], tag=f"pdc{hl}_{ti}_{q4}")
                        B.dma("sp", xt2[p2][:, :], x_h.ap()[ti * 128:(ti + 1) * 128, :],
                              writes=[f"xt2{p2}"], tag=f"pdx{ti}")
                        B.dma("sp", ffnt[p2][:, :], FF_s.ap()[ti * 128:(ti + 1) * 128, :],
                              writes=[f"ffnt{p2}"], tag=f"pdf{ti}")
                        for ci in range(ND):
                            def mk_omm(p, ci):
                                def f(t):
                                    r = None
                                    for pss in range(3):
                                        lhs = (scth, scth, sctl)[pss][p][:, ci * 128:(ci + 1) * 128]
                                        rsrc = (owth, owtl, owth)[pss]
                                        for s in range(D // 512):
                                            r = t.matmul(
                                                pO[p][:, s * 512:(s + 1) * 512],
                                                lhs,
                                                ap(rsrc, ci * D + s * 512,
                                                   [[ND * D, 128], [1, 512]]),
                                                start=(ci == 0 and pss == 0),
                                                stop=(ci == ND - 1 and pss == 2))
                                    return r
                                return f
                            B.op("te", mk_omm(p2, ci),
                                 reads=[f"scth{p2}", f"sctl{p2}", "owth", "owtl"],
                                 writes=[f"P!pO{p2}"], tag=f"pdmm{ti}_{ci}")
                        B.op("dve", lambda v, p=p2: v.tensor_add(
                            o1b[:, :], pO[p][:, :], obb[:, :]),
                            reads=[f"P!pO{p2}", "obb"], writes=["o1b"])
                        B.op("dve", lambda v, p=p2: v.tensor_add(
                            o2b[:, :], o1b[:, :], xt2[p][:, :]),
                            reads=["o1b", f"xt2{p2}"], writes=["o2b"])
                        B.op("dve", lambda v, p=p2: v.tensor_add(
                            outsb[p][:, :], o2b[:, :], ffnt[p][:, :]),
                            reads=["o2b", f"ffnt{p2}"], writes=[f"outsb{p2}"])
                        B.dma("sp", out_h.ap()[ti * 128:(ti + 1) * 128, :], outsb[p2][:, :],
                              reads=[f"outsb{p2}"], tag=f"pdout{ti}")
                    B.barrier()

        if max_ops is not None:
            kept, n = [], 0
            for o in B.ops:
                if o["barrier"]:
                    continue
                kept.append(o)
                n += 1
                if n >= max_ops:
                    break
            for i, o in enumerate(kept):
                o["idx"] = i
            B.ops = kept
            B.barrier()
        if max_barriers is not None:
            nb, cut = 0, len(B.ops)
            for i, o in enumerate(B.ops):
                if o["barrier"]:
                    nb += 1
                    if nb >= max_barriers:
                        cut = i + 1
                        break
            B.ops = B.ops[:cut]
        B.schedule()
        with nc.Block() as block:
            block.tensor(lambda t: B.emit_engine(("te",), t, sems, barsem, 4))
            block.vector(lambda v: B.emit_engine(("dve",), v, sems, barsem, 4))
            block.scalar(lambda a: B.emit_engine(("act", "actq"), a, sems, barsem, 4))
            block.sync(lambda s: B.emit_engine(("sp",), s, sems, barsem, 4))
    nops = len(B.ops)
    return nc, nops


def _consts():
    id32 = np.eye(128, dtype=np.float32)
    id16 = np.eye(128, dtype=np.float32).astype(ml_dtypes.bfloat16)
    tri = np.triu(np.full((128, 128), NEG, dtype=np.float32), 1)
    return id32, id16, tri


def kernel(L0_attn_x, L0_attn_q_w, L0_attn_q_b, L0_attn_k_w, L0_attn_k_b,
           L0_attn_v_w, L0_attn_v_b, L0_attn_o_w, L0_attn_o_b,
           L0_ffn_w_gating, L0_ffn_w_linear):
    x = np.ascontiguousarray(np.asarray(L0_attn_x, dtype=np.float32))
    nbatch, T, _ = x.shape
    id32, id16, tri = _consts()
    nc, _ = build_nc(T)
    common = dict(
        qw=np.ascontiguousarray(np.asarray(L0_attn_q_w, np.float32)),
        kw=np.ascontiguousarray(np.asarray(L0_attn_k_w, np.float32)),
        vw=np.ascontiguousarray(np.asarray(L0_attn_v_w, np.float32)),
        ow=np.ascontiguousarray(np.asarray(L0_attn_o_w, np.float32)),
        qb=np.ascontiguousarray(np.asarray(L0_attn_q_b, np.float32)),
        kb=np.ascontiguousarray(np.asarray(L0_attn_k_b, np.float32)),
        vb=np.ascontiguousarray(np.asarray(L0_attn_v_b, np.float32)),
        ob=np.ascontiguousarray(np.asarray(L0_attn_o_b, np.float32)),
        wg=np.ascontiguousarray(np.asarray(L0_ffn_w_gating, np.float32)),
        wl=np.ascontiguousarray(np.asarray(L0_ffn_w_linear, np.float32)),
        id32=id32, id16=id16, tri=tri,
    )
    in_maps = [dict(common, x=np.ascontiguousarray(x[b])) for b in range(nbatch)]
    res = run_bass_kernel_spmd(nc, in_maps, core_ids=list(range(nbatch)))
    return np.stack([res.results[b]["out"] for b in range(nbatch)], axis=0)


# version 13
# speedup vs baseline: 1.0280x; 1.0280x over previous
import math
from contextlib import ExitStack

import numpy as np
import ml_dtypes

import concourse.bass as bass
import concourse.mybir as mybir
from concourse.bass_utils import run_bass_kernel_spmd

F32 = mybir.dt.float32
BF16 = mybir.dt.bfloat16
E4 = mybir.dt.float8e4
AF = mybir.ActivationFunctionType
ALU = mybir.AluOpType
AX = mybir.AxisListType
U32 = mybir.dt.uint32
DR = mybir.MatmulPerfMode.DoubleRow

D = 2048
HID = 8192
NH, HD = 16, 128
SCALE = float(HD) ** -0.5
EPS = 1e-5
GA = float(np.float32(ml_dtypes.bfloat16(0.044715)))
GC = float(np.float32(ml_dtypes.bfloat16(math.sqrt(2.0 / math.pi))))
NEG = -1.0e30

ENGS = ("te", "dve", "act", "sp", "actq")
QUEUES = ("sp", "actq")


class Builder:

    def __init__(self):
        self.ops = []

    def op(self, eng, fn, reads=(), writes=(), tag="", fence=False):
        self.ops.append(dict(eng=eng, fn=fn, reads=tuple(reads), writes=tuple(writes),
                             tag=tag, barrier=False, idx=len(self.ops), fence=fence))

    def dma(self, eng, dst, src, reads=(), writes=(), tag="", **kw):
        assert eng in QUEUES
        self.op(eng, (lambda e, d=dst, s=src: e.dma_start(d, s, **kw)), reads, writes, tag)

    def barrier(self):
        self.ops.append(dict(eng=None, fn=None, reads=(), writes=(), tag="bar",
                             barrier=True, idx=len(self.ops)))

    def schedule(self):
        writers = {}
        readers = {}
        nbar = 0
        self._next_te = {}
        last_te = None
        for o in self.ops:
            if o["barrier"]:
                last_te = None
                continue
            if o["eng"] == "te":
                if last_te is not None:
                    self._next_te[last_te["idx"]] = o
                last_te = o
        for o in self.ops:
            if o["barrier"]:
                nbar += 1
                o["barn"] = nbar
                writers.clear()
                readers.clear()
                continue
            deps = []
            for k in o["reads"]:
                deps.extend(writers.get(k, ()))
                if k.startswith("P!"):
                    deps.extend(r for r in readers.get(k, ())
                                if r["eng"] != o["eng"])
            war = []
            for k in o["writes"]:
                deps.extend(writers.get(k, ()))
                war.extend(readers.get(k, ()))
            bumped = []
            for d in war:
                if d["eng"] == "te":
                    nxt = self._next_te.get(d["idx"])
                    if nxt is not None and nxt["idx"] < o["idx"]:
                        bumped.append(nxt)
                    else:
                        bumped.append(d)
                else:
                    bumped.append(d)
            deps.extend(bumped)
            need = {}
            for d in deps:
                if (d["eng"] == o["eng"] and d["eng"] not in QUEUES
                        and not d.get("fence")):
                    continue
                need[d["idx"]] = d
            o["deps"] = list(need.values())
            for d in o["deps"]:
                d["inc"] = True
                assert d["idx"] < o["idx"]
            for k in o["reads"]:
                readers.setdefault(k, []).append(o)
            for k in o["writes"]:
                writers[k] = [o]
                readers[k] = []
        cnt = {e: 0 for e in ENGS}
        for o in self.ops:
            if o["barrier"]:
                continue
            if o["eng"] in QUEUES:
                cnt[o["eng"]] += 16
                o["semval"] = cnt[o["eng"]]
                o["inc"] = True
            elif o.get("inc"):
                cnt[o["eng"]] += 1
                o["semval"] = cnt[o["eng"]]
        self.final_counts = dict(cnt)
        self.nbar = nbar

    def emit_engine(self, eng_names, engine, sems, barsem, nbeng):
        hw = {}
        bar_seen = 0
        for o in self.ops:
            if o["barrier"]:
                bar_seen += 1
                for qn in eng_names:
                    if qn in QUEUES:
                        q = self._qcount_at(o["idx"], qn)
                        if q > hw.get(qn, 0):
                            engine.wait_ge(sems[qn], q)
                            hw[qn] = q
                engine.nop().then_inc(barsem, 1)
                engine.wait_ge(barsem, nbeng * bar_seen)
                continue
            if o["eng"] not in eng_names:
                continue
            for d in o.get("deps", ()):
                sname = d["eng"]
                v = d["semval"]
                if v > hw.get(sname, 0):
                    engine.wait_ge(sems[sname], v)
                    hw[sname] = v
            instr = o["fn"](engine)
            if o.get("inc"):
                if o["eng"] in QUEUES:
                    instr.then_inc(sems[o["eng"]], 16)
                else:
                    instr.then_inc(sems[o["eng"]], 1)

    def _qcount_at(self, idx, qname):
        c = 0
        for o in self.ops:
            if o["idx"] >= idx:
                break
            if (not o["barrier"]) and o["eng"] == qname:
                c += 16
        return c


def ap(t, off, dims):
    return bass.AP(t, off, dims)


def slices(span, step=512):
    return [(s, min(span, s + step)) for s in range(0, span, step)]


def groups4(n):
    return [(g * 4, min(n, g * 4 + 4)) for g in range((n + 3) // 4)]


def build_nc(T, max_barriers=None, max_ops=None, dbg=False):
    NT = T // 128
    ND = D // 128
    skind = dict(kind="ExternalOutput") if dbg else {}
    nc = bass.Bass("TRN2", target_bir_lowering=False, detect_race_conditions=False,
                   dynamic_dma_scratch_size=2048)

    x_h = nc.dram_tensor("x", [T, D], F32, kind="ExternalInput")
    qw_h = nc.dram_tensor("qw", [D, D], F32, kind="ExternalInput")
    kw_h = nc.dram_tensor("kw", [D, D], F32, kind="ExternalInput")
    vw_h = nc.dram_tensor("vw", [D, D], F32, kind="ExternalInput")
    ow_h = nc.dram_tensor("ow", [D, D], F32, kind="ExternalInput")
    qb_h = nc.dram_tensor("qb", [D], F32, kind="ExternalInput")
    kb_h = nc.dram_tensor("kb", [D], F32, kind="ExternalInput")
    vb_h = nc.dram_tensor("vb", [D], F32, kind="ExternalInput")
    ob_h = nc.dram_tensor("ob", [D], F32, kind="ExternalInput")
    wg_h = nc.dram_tensor("wg", [2, D, HID], F32, kind="ExternalInput")
    wl_h = nc.dram_tensor("wl", [HID, D], F32, kind="ExternalInput")
    id32_h = nc.dram_tensor("id32", [128, 128], F32, kind="ExternalInput")
    id16_h = nc.dram_tensor("id16", [128, 128], BF16, kind="ExternalInput")
    tri_h = nc.dram_tensor("tri", [128, 128], F32, kind="ExternalInput")
    out_h = nc.dram_tensor("out", [T, D], F32, kind="ExternalOutput")

    xq_s = nc.dram_tensor("xq_s", [D, T], E4, **skind)
    QH_s = nc.dram_tensor("QH_s", [D, T], BF16, **skind)
    QL_s = nc.dram_tensor("QL_s", [D, T], BF16, **skind)
    KH_s = nc.dram_tensor("KH_s", [D, T], BF16, **skind)
    KL_s = nc.dram_tensor("KL_s", [D, T], BF16, **skind)
    VH_s = nc.dram_tensor("VH_s", [NH, T, HD], BF16, **skind)
    VL_s = nc.dram_tensor("VL_s", [NH, T, HD], BF16, **skind)
    CH_s = nc.dram_tensor("CH_s", [D, T], BF16, **skind)
    CL_s = nc.dram_tensor("CL_s", [D, T], BF16, **skind)
    FF_s = nc.dram_tensor("FF_s", [T, D], BF16, **skind)

    B = Builder()

    with ExitStack() as top:
        e = top.enter_context
        id32 = e(nc.sbuf_tensor("id32s", [128, 128], F32))
        id16 = e(nc.sbuf_tensor("id16s", [128, 128], BF16))
        tri = e(nc.sbuf_tensor("tris", [128, 128], F32))
        qb_s = e(nc.sbuf_tensor("qbs", [128, ND], F32))
        kb_s = e(nc.sbuf_tensor("kbs", [128, ND], F32))
        vb_s = e(nc.sbuf_tensor("vbs", [128, ND], F32))
        ones1 = e(nc.sbuf_tensor("ones1", [1, 128], F32))
        sems = {}
        for s in ENGS:
            sems[s] = e(nc.semaphore("sem_" + s))
        barsem = e(nc.semaphore("sem_bar"))

        B.dma("sp", id32[:, :], id32_h.ap(), writes=["id32"])
        B.dma("sp", id16[:, :], id16_h.ap(), writes=["id16"])
        B.dma("sp", tri[:, :], tri_h.ap(), writes=["tri"])
        for bs, bh, key in ((qb_s, qb_h, "qb"), (kb_s, kb_h, "kb"), (vb_s, vb_h, "vb")):
            B.dma("sp", ap(bs, 0, [[ND, 128], [1, ND]]),
                  ap(bh, 0, [[1, 128], [128, ND]]), writes=[key],
                  allow_slow_non_contiguous=True)
        B.op("dve", lambda v: v.memset(ones1[:, :], 1.0), writes=["ones1"])

        if True:
            with ExitStack() as pab:
              xnh = pab.enter_context(nc.sbuf_tensor("xnh", [128, ND * T], BF16))
              xnl = pab.enter_context(nc.sbuf_tensor("xnl", [128, ND * T], BF16))
              with ExitStack() as pa:
                  ea = pa.enter_context
                  xt = [ea(nc.sbuf_tensor(f"xt{i}", [128, D], F32)) for i in range(2)]
                  dbuf = [ea(nc.sbuf_tensor(f"dbuf{i}", [128, D], F32)) for i in range(2)]
                  sqb = ea(nc.sbuf_tensor("sqb", [128, D], F32))
                  xnt = [ea(nc.sbuf_tensor(f"xnt{i}", [128, D], F32)) for i in range(2)]
                  st_sum = ea(nc.sbuf_tensor("st_sum", [128, 1], F32))
                  st_negm = ea(nc.sbuf_tensor("st_negm", [128, 1], F32))
                  st_vsum = ea(nc.sbuf_tensor("st_vsum", [128, 1], F32))
                  st_ve = ea(nc.sbuf_tensor("st_ve", [128, 1], F32))
                  st_sv = ea(nc.sbuf_tensor("st_sv", [128, 1], F32))
                  st_r1 = ea(nc.sbuf_tensor("st_r1", [128, 1], F32))
                  st_n1 = ea(nc.sbuf_tensor("st_n1", [128, 1], F32))
                  st_n3 = ea(nc.sbuf_tensor("st_n3", [128, 1], F32))
                  st_rs = ea(nc.sbuf_tensor("st_rs", [128, 1], F32))
                  xqst = [ea(nc.sbuf_tensor(f"xqst{i}", [128, 512], E4)) for i in range(2)]
                  ptp = [ea(nc.psum_tensor(f"ptp{i}", [128, 512], F32)) for i in range(2)]

                  for ti in range(NT):
                      p2 = ti % 2
                      B.dma("sp", xt[p2][:, :], x_h.ap()[ti * 128:(ti + 1) * 128, :],
                            writes=[f"xt{p2}"], tag=f"palod{ti}")
                      B.op("dve", lambda v, p=p2: v.tensor_reduce(
                          st_sum[:, :], xt[p][:, :], AX.X, ALU.add),
                          reads=[f"xt{p2}"], writes=["stats"], fence=True)
                      B.op("dve", lambda v: v.tensor_scalar(
                          st_negm[:, :], st_sum[:, :], -1.0 / D, None, ALU.mult),
                          reads=["stats"], writes=["stats"], fence=True)
                      B.op("dve", lambda v, p=p2: v.tensor_scalar(
                          dbuf[p][:, :], xt[p][:, :], st_negm[:, :], None, ALU.add),
                          reads=[f"xt{p2}", "stats"], writes=[f"dbuf{p2}"])
                      B.op("act", lambda a, p=p2: a.activation(
                          sqb[:, :], dbuf[p][:, :], AF.Square),
                          reads=[f"dbuf{p2}"], writes=["sqb"], fence=True)
                      B.op("dve", lambda v: v.tensor_reduce(
                          st_vsum[:, :], sqb[:, :], AX.X, ALU.add),
                          reads=["sqb"], writes=["stats"], fence=True)
                      B.op("act", lambda a: a.activation(
                          st_ve[:, :], st_vsum[:, :], AF.Copy, scale=1.0 / D, bias=EPS),
                          reads=["stats"], writes=["stats"], fence=True)
                      B.op("act", lambda a: a.activation(st_sv[:, :], st_ve[:, :], AF.Sqrt),
                           reads=["stats"], writes=["stats"], fence=True)
                      B.op("dve", lambda v: v.reciprocal(st_r1[:, :], st_sv[:, :]),
                           reads=["stats"], writes=["stats"], fence=True)
                      B.op("dve", lambda v: v.tensor_mul(st_n1[:, :], st_r1[:, :], st_r1[:, :]),
                           reads=["stats"], writes=["stats"], fence=True)
                      B.op("dve", lambda v: v.tensor_mul(st_n3[:, :], st_n1[:, :], st_ve[:, :]),
                           reads=["stats"], writes=["stats"], fence=True)
                      B.op("dve", lambda v: v.tensor_scalar(
                          st_n1[:, :], st_n3[:, :], -0.5, 1.5, ALU.mult, ALU.add),
                          reads=["stats"], writes=["stats"], fence=True)
                      B.op("dve", lambda v: v.tensor_mul(st_rs[:, :], st_r1[:, :], st_n1[:, :]),
                           reads=["stats"], writes=["stats"], fence=True)
                      B.op("dve", lambda v, p=p2: v.tensor_scalar(
                          xnt[p][:, :], dbuf[p][:, :], st_rs[:, :], None, ALU.mult),
                          reads=[f"dbuf{p2}", "stats"], writes=[f"xnt{p2}"])
                      for g in range(4):
                          gp = g % 2
                          def mk_tr(p, g, gp):
                              def f(t):
                                  r = None
                                  for k in range(4):
                                      ci = g * 4 + k
                                      r = t.transpose(ptp[gp][:, k * 128:(k + 1) * 128],
                                                      xnt[p][:, ci * 128:(ci + 1) * 128],
                                                      id32[:, :])
                                  return r
                              return f
                          B.op("te", mk_tr(p2, g, gp), reads=[f"xnt{p2}", "id32"],
                               writes=[f"P!ptp{gp}"], tag=f"patr{ti}_{g}")
                          c0 = (g * 4) * T + ti * 128
                          def mk_hi(gp, c0):
                              def f(v):
                                  return v.tensor_copy(
                                      ap(xnh, c0, [[ND * T, 128], [T, 4], [1, 128]]),
                                      ptp[gp][:, :])
                              return f
                          def mk_lo(gp, c0):
                              def f(v):
                                  return v.tensor_sub(
                                      ap(xnl, c0, [[ND * T, 128], [T, 4], [1, 128]]),
                                      ptp[gp][:, :],
                                      ap(xnh, c0, [[ND * T, 128], [T, 4], [1, 128]]))
                              return f
                          B.op("dve", mk_hi(gp, c0), reads=[f"P!ptp{gp}"],
                               writes=["xnh"], tag=f"pahi{ti}_{g}")
                          B.op("dve", mk_lo(gp, c0), reads=[f"P!ptp{gp}", "xnh"],
                               writes=["xnl"], tag=f"palo{ti}_{g}")
                          B.op("act", lambda a, gp=gp, p=p2: a.activation(
                              xqst[gp][:, :], ptp[gp][:, :], AF.Copy),
                              reads=[f"P!ptp{gp}"], writes=[f"xqst{gp}"])
                          B.dma("actq",
                                ap(xq_s, (g * 4 * 128) * T + ti * 128,
                                   [[T, 128], [128 * T, 4], [1, 128]]),
                                ap(xqst[gp], 0, [[512, 128], [128, 4], [1, 128]]),
                                reads=[f"xqst{gp}"], tag=f"paxq{ti}_{g}")
                  B.barrier()

              with ExitStack() as pb:
                  eb = pb.enter_context
                  wrow = [eb(nc.sbuf_tensor(f"wrow{i}", [128, D], F32)) for i in range(2)]
                  wTh = [eb(nc.sbuf_tensor(f"wTh{i}", [128, D], BF16)) for i in range(2)]
                  wTl = [eb(nc.sbuf_tensor(f"wTl{i}", [128, D], BF16)) for i in range(2)]
                  qf = eb(nc.sbuf_tensor("qf", [128, T], F32))
                  qhi = eb(nc.sbuf_tensor("qhi", [128, T], BF16))
                  qlo = eb(nc.sbuf_tensor("qlo", [128, T], BF16))
                  vth_s = eb(nc.sbuf_tensor("vth_s", [128, T], BF16))
                  vtl_s = eb(nc.sbuf_tensor("vtl_s", [128, T], BF16))
                  wtp = [eb(nc.psum_tensor(f"wtp{i}", [128, 512], F32)) for i in range(2)]
                  pQ = [eb(nc.psum_tensor(f"pQ{i}", [128, min(1024, T)], F32)) for i in range(2)]
                  vtp = eb(nc.psum_tensor("vtp", [128, 512], BF16))

                  HALF = min(1024, T)
                  NHALF = T // HALF
                  for wi, (W_h, bkey, QHd, QLd) in enumerate((
                          (qw_h, "qb", QH_s, QL_s), (kw_h, "kb", KH_s, KL_s),
                          (vw_h, "vb", VH_s, VL_s))):
                      bias_s = (qb_s, kb_s, vb_s)[wi]
                      for m in range(ND):
                          p2 = m % 2
                          B.dma("sp", wrow[p2][:, :], W_h.ap()[m * 128:(m + 1) * 128, :],
                                writes=[f"wrow{p2}"], tag=f"pbw{wi}_{m}")
                          for g in range(4):
                              gp = g % 2
                              def mk_wtr(p, g, gp):
                                  def f(t):
                                      r = None
                                      for k in range(4):
                                          ci = g * 4 + k
                                          r = t.transpose(
                                              wtp[gp][:, k * 128:(k + 1) * 128],
                                              wrow[p][:, ci * 128:(ci + 1) * 128],
                                              id32[:, :])
                                      return r
                                  return f
                              B.op("te", mk_wtr(p2, g, gp),
                                   reads=[f"wrow{p2}", "id32"], writes=[f"P!wtp{gp}"])
                              B.op("dve", lambda v, p=p2, g=g, gp=gp:
                                   v.tensor_copy(wTh[p][:, g * 512:(g + 1) * 512],
                                                 wtp[gp][:, :]),
                                   reads=[f"P!wtp{gp}"], writes=[f"wTh{p2}"])
                              B.op("dve", lambda v, p=p2, g=g, gp=gp:
                                   v.tensor_sub(wTl[p][:, g * 512:(g + 1) * 512],
                                                wtp[gp][:, :],
                                                wTh[p][:, g * 512:(g + 1) * 512]),
                                   reads=[f"P!wtp{gp}", f"wTh{p2}"], writes=[f"wTl{p2}"])
                          postmul = SCALE if wi == 0 else 1.0
                          for hf in range(NHALF):
                              def mk_mm(p, hf, ci):
                                  def f(t):
                                      r = None
                                      for pss in range(3):
                                          lhs = (wTh, wTh, wTl)[pss][p][:, ci * 128:(ci + 1) * 128]
                                          rsrc = (xnh, xnl, xnh)[pss]
                                          for (s0, s1) in slices(HALF):
                                              col = ci * T + hf * HALF + s0
                                              r = t.matmul(
                                                  pQ[hf % 2][:, s0:s1],
                                                  lhs,
                                                  ap(rsrc, col, [[ND * T, 128], [1, s1 - s0]]),
                                                  start=(ci == 0 and pss == 0),
                                                  stop=(ci == ND - 1 and pss == 2))
                                      return r
                                  return f
                              for ci in range(ND):
                                  B.op("te", mk_mm(p2, hf, ci),
                                       reads=[f"wTh{p2}", f"wTl{p2}", "xnh", "xnl"],
                                       writes=[f"P!pQ{hf % 2}"], tag=f"pbmm{wi}_{m}_{hf}_{ci}")
                              B.op("dve", lambda v, hf=hf, m=m, b=bias_s, pm=postmul:
                                  v.tensor_scalar(
                                      qf[:, hf * HALF:(hf + 1) * HALF], pQ[hf % 2][:, :],
                                      b[:, m:m + 1], pm, ALU.add, ALU.mult),
                                  reads=[f"P!pQ{hf % 2}", bkey], writes=["qf"])
                              B.op("dve", lambda v, hf=hf: v.tensor_copy(
                                  qhi[:, hf * HALF:(hf + 1) * HALF],
                                  qf[:, hf * HALF:(hf + 1) * HALF]),
                                  reads=["qf"], writes=["qhi"])
                              B.op("dve", lambda v, hf=hf: v.tensor_sub(
                                  qlo[:, hf * HALF:(hf + 1) * HALF],
                                  qf[:, hf * HALF:(hf + 1) * HALF],
                                  qhi[:, hf * HALF:(hf + 1) * HALF]),
                                  reads=["qf", "qhi"], writes=["qlo"])
                          if wi < 2:
                              B.dma("actq", QHd.ap()[m * 128:(m + 1) * 128, :], qhi[:, :],
                                    reads=["qhi"], tag=f"pbsth{wi}_{m}")
                              B.dma("actq", QLd.ap()[m * 128:(m + 1) * 128, :], qlo[:, :],
                                    reads=["qlo"], tag=f"pbstl{wi}_{m}")
                          else:
                              for hl in range(2):
                                  src = (qhi, qlo)[hl]
                                  stg = (vth_s, vtl_s)[hl]
                                  skey = ("qhi", "qlo")[hl]
                                  tkey = ("vth_s", "vtl_s")[hl]
                                  Vd = (VH_s, VL_s)[hl]
                                  for (t0, t1) in groups4(NT):
                                      def mk_vtr(src, t0, t1):
                                          def f(t):
                                              r = None
                                              for tj in range(t0, t1):
                                                  r = t.transpose(
                                                      vtp[:, (tj - t0) * 128:(tj - t0 + 1) * 128],
                                                      src[:, tj * 128:(tj + 1) * 128],
                                                      id16[:, :])
                                              return r
                                          return f
                                      B.op("te", mk_vtr(src, t0, t1), reads=[skey, "id16"],
                                           writes=["P!vtp"])
                                      B.op("dve", lambda v, stg=stg, t0=t0, t1=t1: v.tensor_copy(
                                          stg[:, t0 * 128:t1 * 128],
                                          vtp[:, 0:(t1 - t0) * 128]),
                                          reads=["P!vtp"], writes=[tkey])
                                  nh2 = (NT + 1) // 2
                                  for half in range(2):
                                      t0 = half * nh2
                                      cnt = min(NT, t0 + nh2) - t0
                                      if cnt <= 0:
                                          continue
                                      B.dma("actq",
                                            ap(Vd, m * T * HD + t0 * 128 * HD,
                                               [[HD, 128], [128 * HD, cnt], [1, HD]]),
                                            ap(stg, t0 * 128, [[T, 128], [128, cnt], [1, HD]]),
                                            reads=[tkey], tag=f"pbstv{hl}_{m}_{half}")
                  B.barrier()

            with ExitStack() as pc:
                ec = pc.enter_context
                qhh = [ec(nc.sbuf_tensor(f"qhh{i}", [128, T], BF16)) for i in range(2)]
                qhl = [ec(nc.sbuf_tensor(f"qhl{i}", [128, T], BF16)) for i in range(2)]
                khh = [ec(nc.sbuf_tensor(f"khh{i}", [128, T], BF16)) for i in range(2)]
                khl = [ec(nc.sbuf_tensor(f"khl{i}", [128, T], BF16)) for i in range(2)]
                vthb = [ec(nc.sbuf_tensor(f"vthb{i}", [128, T], BF16)) for i in range(2)]
                vtlb = [ec(nc.sbuf_tensor(f"vtlb{i}", [128, T], BF16)) for i in range(2)]
                pbuf = [ec(nc.sbuf_tensor(f"pbuf{i}", [128, T], F32)) for i in range(2)]
                phib = [ec(nc.sbuf_tensor(f"phib{i}", [128, T], BF16)) for i in range(2)]
                plob = [ec(nc.sbuf_tensor(f"plob{i}", [128, T], BF16)) for i in range(2)]
                ptb = ec(nc.sbuf_tensor("ptb", [128, 2 * T], BF16))
                ctxh = ec(nc.sbuf_tensor("ctxh", [128, T], F32))
                cth = [ec(nc.sbuf_tensor(f"cth{i}", [128, T], BF16)) for i in range(2)]
                ctl = [ec(nc.sbuf_tensor(f"ctl{i}", [128, T], BF16)) for i in range(2)]
                a_negm = [ec(nc.sbuf_tensor(f"a_negm{i}", [128, 1], F32)) for i in range(2)]
                a_sum = [ec(nc.sbuf_tensor(f"a_sum{i}", [128, 1], F32)) for i in range(2)]
                a_rec = [ec(nc.sbuf_tensor(f"a_rec{i}", [128, 1], F32)) for i in range(2)]
                Sps = ec(nc.psum_tensor("P!Sps", [128, T if T <= 2048 else 2048], F32))
                ptpc = [ec(nc.psum_tensor(f"P!ptpc{i}", [128, 1024], BF16)) for i in range(2)]
                ctxp = ec(nc.psum_tensor("P!ctxp", [128, 128], F32))

                def emit_head_loads(h):
                    p2 = h % 2
                    for (buf, S_h) in ((qhh[p2], QH_s), (qhl[p2], QL_s),
                                       (khh[p2], KH_s), (khl[p2], KL_s)):
                        B.dma("sp", buf[:, :], S_h.ap()[h * 128:(h + 1) * 128, :],
                              writes=[f"qkv{p2}"], tag=f"pcl{h}")
                    nh2 = (NT + 1) // 2
                    for (buf, S_h) in ((vthb[p2], VH_s), (vtlb[p2], VL_s)):
                        for half in range(2):
                            t0 = half * nh2
                            cnt = min(NT, t0 + nh2) - t0
                            if cnt <= 0:
                                continue
                            B.dma("sp", ap(buf, t0 * 128, [[T, 128], [128, cnt], [1, HD]]),
                                  ap(S_h, h * T * HD + t0 * 128 * HD,
                                     [[HD, 128], [128 * HD, cnt], [1, HD]]),
                                  writes=[f"qkv{p2}"], tag=f"pclv{h}_{half}")

                def emit_scores(h, at):
                    p2 = h % 2
                    bl = (at + 1) * 128
                    nslc = (bl + 511) // 512
                    def mk_sc(p, at, bl, nslc):
                        def f(t):
                            r = None
                            for pss in range(3):
                                lhs = (qhh, qhh, qhl)[pss][p][:, at * 128:(at + 1) * 128]
                                rsrc = (khh, khl, khh)[pss][p]
                                for s in range(nslc):
                                    c1 = min(bl, (s + 1) * 512)
                                    r = t.matmul(Sps[:, s * 512:c1],
                                                 lhs, rsrc[:, s * 512:c1],
                                                 start=(pss == 0), stop=(pss == 2))
                            return r
                        return f
                    B.op("te", mk_sc(p2, at, bl, nslc),
                         reads=[f"qkv{p2}"], writes=["P!Sps"], tag=f"pcsc{h}_{at}")

                def emit_softmax(h, at):
                    bl = (at + 1) * 128
                    q = at % 2
                    B.op("dve", lambda v, at=at: v.tensor_add(
                        Sps[:, at * 128:(at + 1) * 128],
                        Sps[:, at * 128:(at + 1) * 128], tri[:, :]),
                        reads=["P!Sps", "tri"], writes=["P!Sps"])
                    B.op("dve", lambda v, bl=bl, q=q: v.tensor_reduce(
                        a_negm[q][:, :], Sps[:, 0:bl], AX.X, ALU.max, negate=True),
                        reads=["P!Sps"], writes=[f"ast{q}"], fence=True)
                    B.op("act", lambda a, bl=bl, q=q: a.activation(
                        pbuf[q][:, 0:bl], Sps[:, 0:bl], AF.Exp,
                        bias=a_negm[q][:, :], scale=1.0),
                        reads=["P!Sps", f"ast{q}"], writes=[f"pbuf{q}"])
                    B.op("dve", lambda v, bl=bl, q=q: v.tensor_reduce(
                        a_sum[q][:, :], pbuf[q][:, 0:bl], AX.X, ALU.add),
                        reads=[f"pbuf{q}"], writes=[f"ast{q}"], fence=True)
                    B.op("dve", lambda v, q=q: v.reciprocal(a_rec[q][:, :], a_sum[q][:, :]),
                         reads=[f"ast{q}"], writes=[f"ast{q}"], fence=True)
                    B.op("act", lambda a, bl=bl, q=q: a.activation(
                        phib[q][:, 0:bl], pbuf[q][:, 0:bl], AF.Copy, scale=a_rec[q][:, :]),
                        reads=[f"pbuf{q}", f"ast{q}"], writes=[f"phib{q}"])
                    B.op("dve", lambda v, bl=bl, q=q: v.scalar_tensor_tensor(
                        plob[q][:, 0:bl], pbuf[q][:, 0:bl], a_rec[q][:, :], phib[q][:, 0:bl],
                        ALU.mult, ALU.subtract),
                        reads=[f"pbuf{q}", f"ast{q}", f"phib{q}"], writes=[f"plob{q}"])

                def emit_groups(h, at):
                    p2 = h % 2
                    q = at % 2
                    nb = at + 1
                    for g in range((nb + 3) // 4):
                        k0 = g * 4
                        k1 = min(nb, k0 + 4)
                        gp = g % 2
                        def mk_ptr(q, k0, k1, gp):
                            def f(t):
                                r = None
                                for hl in range(2):
                                    src = (phib, plob)[hl][q]
                                    for k in range(k0, k1):
                                        r = t.transpose(
                                            ptpc[gp][:, hl * 512 + (k - k0) * 128:
                                                     hl * 512 + (k - k0 + 1) * 128],
                                            src[:, k * 128:(k + 1) * 128], id16[:, :])
                                return r
                            return f
                        B.op("te", mk_ptr(q, k0, k1, gp),
                             reads=[f"phib{q}", f"plob{q}", "id16"], writes=[f"P!ptpc{gp}"])
                        def mk_pev(k0, k1, gp):
                            def f(a):
                                return a.activation(
                                    ap(ptb, k0 * 128, [[2 * T, 128], [T, 2],
                                                       [1, (k1 - k0) * 128]]),
                                    ap(ptpc[gp], 0, [[1024, 128], [512, 2],
                                                     [1, (k1 - k0) * 128]]),
                                    AF.Copy)
                            return f
                        B.op("act", mk_pev(k0, k1, gp),
                             reads=[f"P!ptpc{gp}"], writes=["ptb"])
                        for k in range(k0, k1):
                            def mk_ctx(p, k, at):
                                def f(t):
                                    r = None
                                    for pss in range(3):
                                        lhs = (vthb, vthb, vtlb)[pss][p][:, k * 128:(k + 1) * 128]
                                        off = (0, T, 0)[pss]
                                        r = t.matmul(ctxp[:, :], lhs,
                                                     ptb[:, off + k * 128:off + (k + 1) * 128],
                                                     start=(k == 0 and pss == 0),
                                                     stop=(k == at and pss == 2))
                                    return r
                                return f
                            B.op("te", mk_ctx(p2, k, at),
                                 reads=[f"qkv{p2}", "ptb"], writes=["P!ctxp"],
                                 tag=f"pcctx{h}_{at}_{k}")

                def emit_ctx_evac(h, at):
                    B.op("act", lambda a, at=at: a.activation(
                        ctxh[:, at * 128:(at + 1) * 128], ctxp[:, :], AF.Copy),
                        reads=["P!ctxp"], writes=["ctxh"])

                emit_head_loads(0)
                for h in range(NH):
                    p2 = h % 2
                    emit_scores(h, 0)
                    for at in range(NT):
                        emit_softmax(h, at)
                        if at + 1 < NT:
                            emit_scores(h, at + 1)
                        if at > 0:
                            emit_ctx_evac(h, at - 1)
                        emit_groups(h, at)
                        if h + 1 < NH and at == 0:
                            emit_head_loads(h + 1)
                    emit_ctx_evac(h, NT - 1)
                    B.op("dve", lambda v, p=p2: v.tensor_copy(cth[p][:, :], ctxh[:, :]),
                         reads=["ctxh"], writes=[f"cth{p2}"])
                    B.op("dve", lambda v, p=p2: v.tensor_sub(
                        ctl[p][:, :], ctxh[:, :], cth[p][:, :]),
                        reads=["ctxh", f"cth{p2}"], writes=[f"ctl{p2}"])
                    B.dma("actq", CH_s.ap()[h * 128:(h + 1) * 128, :], cth[p2][:, :],
                          reads=[f"cth{p2}"], tag=f"pcsth{h}")
                    B.dma("actq", CL_s.ap()[h * 128:(h + 1) * 128, :], ctl[p2][:, :],
                          reads=[f"ctl{p2}"], tag=f"pcstl{h}")
                B.barrier()

            with ExitStack() as pe_outer:
              aq = pe_outer.enter_context(nc.sbuf_tensor("aq", [128, (HID // 128) * T], E4))
              with ExitStack() as pe:
                ee = pe.enter_context
                xq_sb = ee(nc.sbuf_tensor("xq_sb", [128, ND * T], E4))
                wgs = [[ee(nc.sbuf_tensor(f"wgs{g}{i}", [128, D], F32)) for i in range(2)]
                       for g in range(2)]
                wgq = [[ee(nc.sbuf_tensor(f"wgq{g}{i}", [128, D], E4)) for i in range(2)]
                       for g in range(2)]
                GH = min(1024, T)
                NGH = T // GH
                gb = [ee(nc.sbuf_tensor(f"gb{i}", [128, GH], BF16)) for i in range(2)]
                fb = [ee(nc.sbuf_tensor(f"fb{i}", [128, GH], BF16)) for i in range(2)]
                sA = [ee(nc.sbuf_tensor(f"sA{i}", [128, GH], BF16)) for i in range(2)]
                sB = [ee(nc.sbuf_tensor(f"sB{i}", [128, GH], BF16)) for i in range(2)]
                pg = [[ee(nc.psum_tensor(f"pg{g}{i}", [128, GH], F32)) for i in range(2)]
                      for g in range(2)]

                for ci in range(ND):
                    B.dma("sp", xq_sb[:, ci * T:(ci + 1) * T],
                          xq_s.ap()[ci * 128:(ci + 1) * 128, :],
                          writes=["xq_sb"], tag=f"pexq{ci}")
                B.op("dve", lambda v: v.tensor_copy(
                    xq_sb[:, :].bitcast(U32), xq_sb[:, :].bitcast(U32)),
                    reads=["xq_sb"], writes=["xq_sb"], tag="lxq")
                for m in range(HID // 128):
                    p2 = m % 2
                    for g in range(2):
                        for q4 in range(4):
                            B.dma("sp",
                                  ap(wgs[g][p2], q4 * 4 * 128,
                                     [[D, 128], [128, 4], [1, 128]]),
                                  ap(wg_h, g * D * HID + (q4 * 4 * 128) * HID + m * 128,
                                     [[HID, 128], [128 * HID, 4], [1, 128]]),
                                  writes=[f"wgs{g}{p2}"], tag=f"pewg{g}_{m}_{q4}")
                        B.op("dve", lambda v, g=g, p=p2: v.tensor_copy(
                            wgq[g][p][:, :], wgs[g][p][:, :]),
                            reads=[f"wgs{g}{p2}"], writes=[f"wgq{g}{p2}"])
                        B.op("dve", lambda v, g=g, p=p2: v.tensor_copy(
                            wgq[g][p][:, :].bitcast(U32), wgq[g][p][:, :].bitcast(U32)),
                            reads=[f"wgq{g}{p2}"], writes=[f"wgq{g}{p2}"], tag="lwg")
                    for hf in range(NGH):
                        for g in range(2):
                            def mk_gmm(g, p, hf):
                                def f(t):
                                    r = None
                                    for ci in range(0, ND, 2):
                                        for (s0, s1) in slices(GH):
                                            col = ci * T + hf * GH + s0
                                            r = t.matmul(
                                                pg[g][hf % 2][:, s0:s1],
                                                ap(wgq[g][p], ci * 128,
                                                   [[D, 128], [128, 2], [1, 128]]),
                                                ap(xq_sb, col,
                                                   [[ND * T, 128], [T, 2], [1, s1 - s0]]),
                                                start=(ci == 0), stop=(ci == ND - 2),
                                                perf_mode=DR)
                                    return r
                                return f
                            B.op("te", mk_gmm(g, p2, hf),
                                 reads=[f"wgq{g}{p2}", "xq_sb"],
                                 writes=[f"P!pg{g}{hf % 2}"], tag=f"pegm{g}_{m}_{hf}")
                        hp = hf % 2
                        pg0, pg1 = pg[0][hp], pg[1][hp]
                        r0, r1 = f"P!pg0{hp}", f"P!pg1{hp}"
                        gbk, fbk = f"gb{hp}", f"fb{hp}"
                        sak, sbk = f"sA{hp}", f"sB{hp}"
                        B.op("dve", lambda v, hp=hp, pg0=pg0: v.tensor_copy(
                            gb[hp][:, :], pg0[:, :]), reads=[r0], writes=[gbk])
                        B.op("dve", lambda v, hp=hp, pg1=pg1: v.tensor_copy(
                            fb[hp][:, :], pg1[:, :]), reads=[r1], writes=[fbk])
                        B.op("act", lambda a, hp=hp: a.activation(
                            sA[hp][:, :], gb[hp][:, :], AF.Square),
                            reads=[gbk], writes=[sak])
                        B.op("dve", lambda v, hp=hp: v.tensor_mul(
                            sB[hp][:, :], sA[hp][:, :], gb[hp][:, :]),
                            reads=[sak, gbk], writes=[sbk])
                        B.op("dve", lambda v, hp=hp: v.tensor_scalar(
                            sA[hp][:, :], sB[hp][:, :], GA, None, ALU.mult),
                            reads=[sbk], writes=[sak])
                        B.op("dve", lambda v, hp=hp: v.tensor_add(
                            sB[hp][:, :], sA[hp][:, :], gb[hp][:, :]),
                            reads=[sak, gbk], writes=[sbk])
                        B.op("dve", lambda v, hp=hp: v.tensor_scalar(
                            sA[hp][:, :], sB[hp][:, :], GC, None, ALU.mult),
                            reads=[sbk], writes=[sak])
                        B.op("act", lambda a, hp=hp: a.activation(
                            sB[hp][:, :], sA[hp][:, :], AF.Tanh),
                            reads=[sak], writes=[sbk])
                        B.op("dve", lambda v, hp=hp: v.tensor_scalar(
                            sA[hp][:, :], sB[hp][:, :], 0.5, 0.5, ALU.mult, ALU.add),
                            reads=[sbk], writes=[sak])
                        B.op("dve", lambda v, hp=hp: v.tensor_mul(
                            sB[hp][:, :], gb[hp][:, :], sA[hp][:, :]),
                            reads=[gbk, sak], writes=[sbk])
                        B.op("dve", lambda v, hp=hp: v.tensor_mul(
                            sA[hp][:, :], sB[hp][:, :], fb[hp][:, :]),
                            reads=[sbk, fbk], writes=[sak])
                        B.op("act", lambda a, hp=hp, m=m, hf=hf: a.activation(
                            aq[:, m * T + hf * GH:m * T + (hf + 1) * GH],
                            sA[hp][:, :], AF.Copy),
                            reads=[sak], writes=["aq"], tag=f"peaq{m}_{hf}")
                B.op("dve", lambda v: v.tensor_copy(
                    aq[:, :].bitcast(U32), aq[:, :].bitcast(U32)),
                    reads=["aq"], writes=["aq"], tag="laq")
                B.barrier()

              if dbg:
                  ffdbg_h = nc.dram_tensor("ff_fm_dbg", [D, T], BF16,
                                           kind="ExternalOutput")
                  aq_dbg = nc.dram_tensor("aq_dbg", [128, (HID // 128) * T], E4,
                                          kind="ExternalOutput")
                  B.dma("sp", aq_dbg.ap(), aq[:, :], reads=["aq"], tag="aqdbg")
                  B.barrier()

              with ExitStack() as pf:
                  ef = pf.enter_context
                  NH2 = HID // 128
                  wlt = ef(nc.sbuf_tensor("wlt", [128, NH2 * 128], F32))
                  wlq = [ef(nc.sbuf_tensor(f"wlq{i}", [128, NH2 * 128], E4)) for i in range(2)]
                  ffn16 = ef(nc.sbuf_tensor("ffn16", [128, T], BF16))
                  fts = ef(nc.sbuf_tensor("fts", [128, T], BF16))
                  LH = min(1024, T)
                  NLH = T // LH
                  pL = [[ef(nc.psum_tensor(f"pL{j}{i}", [128, LH], F32)) for i in range(NLH)]
                        for j in range(1)]
                  ftp = ef(nc.psum_tensor("P!ftp", [128, 512], BF16))

                  for j in range(ND):
                      jp = 0
                      wp = j % 2
                      for q4 in range(NH2 // 4):
                          B.dma("sp",
                                ap(wlt, q4 * 4 * 128, [[NH2 * 128, 128], [128, 4], [1, 128]]),
                                ap(wl_h, (q4 * 4 * 128) * D + j * 128,
                                   [[D, 128], [128 * D, 4], [1, 128]]),
                                writes=["wlt"], tag=f"pfw{j}_{q4}")
                      B.op("act", lambda a, wp=wp: a.activation(
                          wlq[wp][:, :], wlt[:, :], AF.Copy),
                          reads=["wlt"], writes=[f"wlq{wp}"])
                      B.op("dve", lambda v, wp=wp: v.tensor_copy(
                          wlq[wp][:, :].bitcast(U32), wlq[wp][:, :].bitcast(U32)),
                          reads=[f"wlq{wp}"], writes=[f"wlq{wp}"], tag="lwl")
                      for h2 in range(0, NH2, 2):
                          def mk_lmm(wp, jp, h2):
                              def f(t):
                                  r = None
                                  for hf in range(NLH):
                                      for (s0, s1) in slices(LH):
                                          col = h2 * T + hf * LH + s0
                                          r = t.matmul(
                                              pL[jp][hf][:, s0:s1],
                                              ap(wlq[wp], h2 * 128,
                                                 [[NH2 * 128, 128], [128, 2], [1, 128]]),
                                              ap(aq, col,
                                                 [[(HID // 128) * T, 128], [T, 2], [1, s1 - s0]]),
                                              start=(h2 == 0), stop=(h2 == NH2 - 2),
                                              perf_mode=DR)
                                  return r
                              return f
                          B.op("te", mk_lmm(wp, jp, h2),
                               reads=[f"wlq{wp}", "aq"], writes=[f"P!pL{jp}"],
                               tag=f"pflm{j}_{h2}")
                      for hf in range(NLH):
                          B.op("dve", lambda v, jp=jp, hf=hf: v.tensor_copy(
                              ffn16[:, hf * LH:(hf + 1) * LH], pL[jp][hf][:, :]),
                              reads=[f"P!pL{jp}"], writes=["ffn16"])
                      B.op("dve", lambda v: v.tensor_copy(
                          ffn16[:, :].bitcast(U32), ffn16[:, :].bitcast(U32)),
                          reads=["ffn16"], writes=["ffn16"], tag="lff")
                      if dbg:
                          B.dma("sp", ffdbg_h.ap()[j * 128:(j + 1) * 128, :],
                                ffn16[:, :], reads=["ffn16"], tag=f"ffdbg{j}")
                      for (t0, t1) in groups4(NT):
                          def mk_ftr(t0, t1):
                              def f(t):
                                  r = None
                                  for tj in range(t0, t1):
                                      r = t.transpose(ftp[:, (tj - t0) * 128:(tj - t0 + 1) * 128],
                                                      ffn16[:, tj * 128:(tj + 1) * 128],
                                                      id16[:, :])
                                  return r
                              return f
                          B.op("te", mk_ftr(t0, t1), reads=["ffn16", "id16"], writes=["P!ftp"])
                          B.op("act", lambda a, t0=t0, t1=t1: a.activation(
                              fts[:, t0 * 128:t1 * 128], ftp[:, 0:(t1 - t0) * 128], AF.Copy),
                              reads=["P!ftp"], writes=["fts"])
                      for (t0, t1) in groups4(NT):
                          B.dma("actq",
                                ap(FF_s, j * 128 + (t0 * 128) * D,
                                   [[D, 128], [128 * D, t1 - t0], [1, 128]]),
                                ap(fts, t0 * 128, [[T, 128], [128, t1 - t0], [1, 128]]),
                                reads=["fts"], tag=f"pfst{j}_{t0}")
                  B.barrier()

            with ExitStack() as pd:
                ed = pd.enter_context
                owth = ed(nc.sbuf_tensor("owth", [128, ND * D], BF16))
                owtl = ed(nc.sbuf_tensor("owtl", [128, ND * D], BF16))
                obb = ed(nc.sbuf_tensor("obb", [128, D], F32))
                with ExitStack() as pp:
                    ep = pp.enter_context
                    ob_row = ep(nc.sbuf_tensor("obrow", [1, D], F32))
                    pbc = ep(nc.psum_tensor("P!pbc", [128, D], F32))
                    B.dma("sp", ap(ob_row, 0, [[D, 1], [1, D]]),
                          ap(ob_h, 0, [[D, 1], [1, D]]), writes=["obrow"])
                    def mk_bc(s):
                        def f(t):
                            return t.matmul(pbc[:, s * 512:(s + 1) * 512], ones1[:, :],
                                            ob_row[:, s * 512:(s + 1) * 512],
                                            start=True, stop=True)
                        return f
                    for s in range(4):
                        B.op("te", mk_bc(s), reads=["ones1", "obrow"], writes=["P!pbc"])
                    B.op("dve", lambda v: v.tensor_copy(obb[:, :], pbc[:, :]),
                         reads=["P!pbc"], writes=["obb"])
                    wrow2 = [ep(nc.sbuf_tensor(f"wrow2{i}", [128, D], F32)) for i in range(2)]
                    wth2 = [ep(nc.sbuf_tensor(f"wth2{i}", [128, D], BF16)) for i in range(2)]
                    wtl2 = [ep(nc.sbuf_tensor(f"wtl2{i}", [128, D], BF16)) for i in range(2)]
                    wtp2 = [ep(nc.psum_tensor(f"wtp2{i}", [128, 512], BF16)) for i in range(2)]
                    for m in range(ND):
                        p2 = m % 2
                        B.dma("sp", wrow2[p2][:, :], ow_h.ap()[m * 128:(m + 1) * 128, :],
                              writes=[f"wrow2{p2}"], tag=f"pdw{m}")
                        B.op("dve", lambda v, p=p2: v.tensor_copy(
                            wth2[p][:, :], wrow2[p][:, :]),
                            reads=[f"wrow2{p2}"], writes=[f"wth2{p2}"])
                        B.op("dve", lambda v, p=p2: v.tensor_sub(
                            wtl2[p][:, :], wrow2[p][:, :], wth2[p][:, :]),
                            reads=[f"wrow2{p2}", f"wth2{p2}"], writes=[f"wtl2{p2}"])
                        for hl in range(2):
                            src = (wth2, wtl2)[hl]
                            dstT = (owth, owtl)[hl]
                            skey = (f"wth2{p2}", f"wtl2{p2}")[hl]
                            dkey = ("owth", "owtl")[hl]
                            for g in range(4):
                                gp = g % 2
                                def mk_otr(src, p, g, gp):
                                    def f(t):
                                        r = None
                                        for k in range(4):
                                            ci = g * 4 + k
                                            r = t.transpose(
                                                wtp2[gp][:, k * 128:(k + 1) * 128],
                                                src[p][:, ci * 128:(ci + 1) * 128],
                                                id16[:, :])
                                        return r
                                    return f
                                B.op("te", mk_otr(src, p2, g, gp),
                                     reads=[skey, "id16"], writes=[f"P!wtp2{gp}"])
                                c0 = (g * 4) * D + m * 128
                                def mk_oev(dstT, c0, gp):
                                    def f(v):
                                        return v.tensor_copy(
                                            ap(dstT, c0, [[ND * D, 128], [D, 4], [1, 128]]),
                                            wtp2[gp][:, :])
                                    return f
                                B.op("dve", mk_oev(dstT, c0, gp),
                                     reads=[f"P!wtp2{gp}"], writes=[dkey])
                    B.barrier()
                with ExitStack() as pm:
                    em = pm.enter_context
                    scth = [em(nc.sbuf_tensor(f"scth{i}", [128, D], BF16)) for i in range(2)]
                    sctl = [em(nc.sbuf_tensor(f"sctl{i}", [128, D], BF16)) for i in range(2)]
                    xt2 = [em(nc.sbuf_tensor(f"xt2{i}", [128, D], F32)) for i in range(2)]
                    ffnt = [em(nc.sbuf_tensor(f"ffnt{i}", [128, D], BF16)) for i in range(2)]
                    o1b = em(nc.sbuf_tensor("o1b", [128, D], F32))
                    o2b = em(nc.sbuf_tensor("o2b", [128, D], F32))
                    outsb = [em(nc.sbuf_tensor(f"outsb{i}", [128, D], F32)) for i in range(2)]
                    pO = [em(nc.psum_tensor(f"pO{i}", [128, D], F32)) for i in range(2)]

                    for ti in range(NT):
                        p2 = ti % 2
                        for hl in range(2):
                            Cs = (CH_s, CL_s)[hl]
                            dst = (scth, sctl)[hl]
                            dkey = (f"scth{p2}", f"sctl{p2}")[hl]
                            for q4 in range(ND // 4):
                                B.dma("actq",
                                      ap(dst[p2], q4 * 4 * 128,
                                         [[D, 128], [128, 4], [1, 128]]),
                                      ap(Cs, ti * 128 + (q4 * 4 * 128) * T,
                                         [[T, 128], [128 * T, 4], [1, 128]]),
                                      writes=[dkey], tag=f"pdc{hl}_{ti}_{q4}")
                        B.dma("sp", xt2[p2][:, :], x_h.ap()[ti * 128:(ti + 1) * 128, :],
                              writes=[f"xt2{p2}"], tag=f"pdx{ti}")
                        B.dma("sp", ffnt[p2][:, :], FF_s.ap()[ti * 128:(ti + 1) * 128, :],
                              writes=[f"ffnt{p2}"], tag=f"pdf{ti}")
                        for ci in range(ND):
                            def mk_omm(p, ci):
                                def f(t):
                                    r = None
                                    for pss in range(3):
                                        lhs = (scth, scth, sctl)[pss][p][:, ci * 128:(ci + 1) * 128]
                                        rsrc = (owth, owtl, owth)[pss]
                                        for s in range(D // 512):
                                            r = t.matmul(
                                                pO[p][:, s * 512:(s + 1) * 512],
                                                lhs,
                                                ap(rsrc, ci * D + s * 512,
                                                   [[ND * D, 128], [1, 512]]),
                                                start=(ci == 0 and pss == 0),
                                                stop=(ci == ND - 1 and pss == 2))
                                    return r
                                return f
                            B.op("te", mk_omm(p2, ci),
                                 reads=[f"scth{p2}", f"sctl{p2}", "owth", "owtl"],
                                 writes=[f"P!pO{p2}"], tag=f"pdmm{ti}_{ci}")
                        B.op("dve", lambda v, p=p2: v.tensor_add(
                            o1b[:, :], pO[p][:, :], obb[:, :]),
                            reads=[f"P!pO{p2}", "obb"], writes=["o1b"])
                        B.op("dve", lambda v, p=p2: v.tensor_add(
                            o2b[:, :], o1b[:, :], xt2[p][:, :]),
                            reads=["o1b", f"xt2{p2}"], writes=["o2b"])
                        B.op("dve", lambda v, p=p2: v.tensor_add(
                            outsb[p][:, :], o2b[:, :], ffnt[p][:, :]),
                            reads=["o2b", f"ffnt{p2}"], writes=[f"outsb{p2}"])
                        B.dma("sp", out_h.ap()[ti * 128:(ti + 1) * 128, :], outsb[p2][:, :],
                              reads=[f"outsb{p2}"], tag=f"pdout{ti}")
                    B.barrier()

        if max_ops is not None:
            kept, n = [], 0
            for o in B.ops:
                if o["barrier"]:
                    continue
                kept.append(o)
                n += 1
                if n >= max_ops:
                    break
            for i, o in enumerate(kept):
                o["idx"] = i
            B.ops = kept
            B.barrier()
        if max_barriers is not None:
            nb, cut = 0, len(B.ops)
            for i, o in enumerate(B.ops):
                if o["barrier"]:
                    nb += 1
                    if nb >= max_barriers:
                        cut = i + 1
                        break
            B.ops = B.ops[:cut]
        B.schedule()
        with nc.Block() as block:
            block.tensor(lambda t: B.emit_engine(("te",), t, sems, barsem, 4))
            block.vector(lambda v: B.emit_engine(("dve",), v, sems, barsem, 4))
            block.scalar(lambda a: B.emit_engine(("act", "actq"), a, sems, barsem, 4))
            block.sync(lambda s: B.emit_engine(("sp",), s, sems, barsem, 4))
    nops = len(B.ops)
    return nc, nops


def _consts():
    id32 = np.eye(128, dtype=np.float32)
    id16 = np.eye(128, dtype=np.float32).astype(ml_dtypes.bfloat16)
    tri = np.triu(np.full((128, 128), NEG, dtype=np.float32), 1)
    return id32, id16, tri


def kernel(L0_attn_x, L0_attn_q_w, L0_attn_q_b, L0_attn_k_w, L0_attn_k_b,
           L0_attn_v_w, L0_attn_v_b, L0_attn_o_w, L0_attn_o_b,
           L0_ffn_w_gating, L0_ffn_w_linear):
    x = np.ascontiguousarray(np.asarray(L0_attn_x, dtype=np.float32))
    nbatch, T, _ = x.shape
    id32, id16, tri = _consts()
    nc, _ = build_nc(T)
    common = dict(
        qw=np.ascontiguousarray(np.asarray(L0_attn_q_w, np.float32)),
        kw=np.ascontiguousarray(np.asarray(L0_attn_k_w, np.float32)),
        vw=np.ascontiguousarray(np.asarray(L0_attn_v_w, np.float32)),
        ow=np.ascontiguousarray(np.asarray(L0_attn_o_w, np.float32)),
        qb=np.ascontiguousarray(np.asarray(L0_attn_q_b, np.float32)),
        kb=np.ascontiguousarray(np.asarray(L0_attn_k_b, np.float32)),
        vb=np.ascontiguousarray(np.asarray(L0_attn_v_b, np.float32)),
        ob=np.ascontiguousarray(np.asarray(L0_attn_o_b, np.float32)),
        wg=np.ascontiguousarray(np.asarray(L0_ffn_w_gating, np.float32)),
        wl=np.ascontiguousarray(np.asarray(L0_ffn_w_linear, np.float32)),
        id32=id32, id16=id16, tri=tri,
    )
    in_maps = [dict(common, x=np.ascontiguousarray(x[b])) for b in range(nbatch)]
    res = run_bass_kernel_spmd(nc, in_maps, core_ids=list(range(nbatch)))
    return np.stack([res.results[b]["out"] for b in range(nbatch)], axis=0)


# version 14 (final)
# speedup vs baseline: 1.0806x; 1.0511x over previous
import math
from contextlib import ExitStack

import numpy as np
import ml_dtypes

import concourse.bass as bass
import concourse.mybir as mybir
from concourse.bass_utils import run_bass_kernel_spmd

F32 = mybir.dt.float32
BF16 = mybir.dt.bfloat16
E4 = mybir.dt.float8e4
AF = mybir.ActivationFunctionType
ALU = mybir.AluOpType
AX = mybir.AxisListType
U32 = mybir.dt.uint32
DR = mybir.MatmulPerfMode.DoubleRow

D = 2048
HID = 8192
NH, HD = 16, 128
SCALE = float(HD) ** -0.5
EPS = 1e-5
GA = float(np.float32(ml_dtypes.bfloat16(0.044715)))
GC = float(np.float32(ml_dtypes.bfloat16(math.sqrt(2.0 / math.pi))))
NEG = -1.0e30

ENGS = ("te", "dve", "act", "sp", "actq")
QUEUES = ("sp", "actq")


class Builder:

    def __init__(self):
        self.ops = []

    def op(self, eng, fn, reads=(), writes=(), tag="", fence=False):
        self.ops.append(dict(eng=eng, fn=fn, reads=tuple(reads), writes=tuple(writes),
                             tag=tag, barrier=False, idx=len(self.ops), fence=fence))

    def dma(self, eng, dst, src, reads=(), writes=(), tag="", **kw):
        assert eng in QUEUES
        self.op(eng, (lambda e, d=dst, s=src: e.dma_start(d, s, **kw)), reads, writes, tag)

    def barrier(self):
        self.ops.append(dict(eng=None, fn=None, reads=(), writes=(), tag="bar",
                             barrier=True, idx=len(self.ops)))

    def schedule(self):
        writers = {}
        readers = {}
        nbar = 0
        self._next_te = {}
        last_te = None
        for o in self.ops:
            if o["barrier"]:
                last_te = None
                continue
            if o["eng"] == "te":
                if last_te is not None:
                    self._next_te[last_te["idx"]] = o
                last_te = o
        for o in self.ops:
            if o["barrier"]:
                nbar += 1
                o["barn"] = nbar
                writers.clear()
                readers.clear()
                continue
            deps = []
            for k in o["reads"]:
                deps.extend(writers.get(k, ()))
                if k.startswith("P!"):
                    deps.extend(r for r in readers.get(k, ())
                                if r["eng"] != o["eng"])
            war = []
            for k in o["writes"]:
                deps.extend(writers.get(k, ()))
                war.extend(readers.get(k, ()))
            bumped = []
            for d in war:
                if d["eng"] == "te":
                    nxt = self._next_te.get(d["idx"])
                    if nxt is not None and nxt["idx"] < o["idx"]:
                        bumped.append(nxt)
                    else:
                        bumped.append(d)
                else:
                    bumped.append(d)
            deps.extend(bumped)
            need = {}
            for d in deps:
                if (d["eng"] == o["eng"] and d["eng"] not in QUEUES
                        and not d.get("fence")):
                    continue
                need[d["idx"]] = d
            o["deps"] = list(need.values())
            for d in o["deps"]:
                d["inc"] = True
                assert d["idx"] < o["idx"]
            for k in o["reads"]:
                readers.setdefault(k, []).append(o)
            for k in o["writes"]:
                writers[k] = [o]
                readers[k] = []
        cnt = {e: 0 for e in ENGS}
        for o in self.ops:
            if o["barrier"]:
                continue
            if o["eng"] in QUEUES:
                cnt[o["eng"]] += 16
                o["semval"] = cnt[o["eng"]]
                o["inc"] = True
            elif o.get("inc"):
                cnt[o["eng"]] += 1
                o["semval"] = cnt[o["eng"]]
        self.final_counts = dict(cnt)
        self.nbar = nbar

    def emit_engine(self, eng_names, engine, sems, barsem, nbeng):
        hw = {}
        bar_seen = 0
        for o in self.ops:
            if o["barrier"]:
                bar_seen += 1
                for qn in eng_names:
                    if qn in QUEUES:
                        q = self._qcount_at(o["idx"], qn)
                        if q > hw.get(qn, 0):
                            engine.wait_ge(sems[qn], q)
                            hw[qn] = q
                engine.nop().then_inc(barsem, 1)
                engine.wait_ge(barsem, nbeng * bar_seen)
                continue
            if o["eng"] not in eng_names:
                continue
            for d in o.get("deps", ()):
                sname = d["eng"]
                v = d["semval"]
                if v > hw.get(sname, 0):
                    engine.wait_ge(sems[sname], v)
                    hw[sname] = v
            instr = o["fn"](engine)
            if o.get("inc"):
                if o["eng"] in QUEUES:
                    instr.then_inc(sems[o["eng"]], 16)
                else:
                    instr.then_inc(sems[o["eng"]], 1)

    def _qcount_at(self, idx, qname):
        c = 0
        for o in self.ops:
            if o["idx"] >= idx:
                break
            if (not o["barrier"]) and o["eng"] == qname:
                c += 16
        return c


def ap(t, off, dims):
    return bass.AP(t, off, dims)


def slices(span, step=512):
    return [(s, min(span, s + step)) for s in range(0, span, step)]


def groups4(n):
    return [(g * 4, min(n, g * 4 + 4)) for g in range((n + 3) // 4)]


def build_nc(T, max_barriers=None, max_ops=None, dbg=False):
    NT = T // 128
    ND = D // 128
    skind = dict(kind="ExternalOutput") if dbg else {}
    nc = bass.Bass("TRN2", target_bir_lowering=False, detect_race_conditions=False,
                   dynamic_dma_scratch_size=2048)

    x_h = nc.dram_tensor("x", [T, D], F32, kind="ExternalInput")
    qw_h = nc.dram_tensor("qw", [D, D], F32, kind="ExternalInput")
    kw_h = nc.dram_tensor("kw", [D, D], F32, kind="ExternalInput")
    vw_h = nc.dram_tensor("vw", [D, D], F32, kind="ExternalInput")
    ow_h = nc.dram_tensor("ow", [D, D], F32, kind="ExternalInput")
    qb_h = nc.dram_tensor("qb", [D], F32, kind="ExternalInput")
    kb_h = nc.dram_tensor("kb", [D], F32, kind="ExternalInput")
    vb_h = nc.dram_tensor("vb", [D], F32, kind="ExternalInput")
    ob_h = nc.dram_tensor("ob", [D], F32, kind="ExternalInput")
    wg_h = nc.dram_tensor("wg", [2, D, HID], F32, kind="ExternalInput")
    wl_h = nc.dram_tensor("wl", [HID, D], F32, kind="ExternalInput")
    id32_h = nc.dram_tensor("id32", [128, 128], F32, kind="ExternalInput")
    id16_h = nc.dram_tensor("id16", [128, 128], BF16, kind="ExternalInput")
    tri_h = nc.dram_tensor("tri", [128, 128], F32, kind="ExternalInput")
    out_h = nc.dram_tensor("out", [T, D], F32, kind="ExternalOutput")

    xq_s = nc.dram_tensor("xq_s", [D, T], E4, **skind)
    QH_s = nc.dram_tensor("QH_s", [D, T], BF16, **skind)
    QL_s = nc.dram_tensor("QL_s", [D, T], BF16, **skind)
    KH_s = nc.dram_tensor("KH_s", [D, T], BF16, **skind)
    KL_s = nc.dram_tensor("KL_s", [D, T], BF16, **skind)
    VH_s = nc.dram_tensor("VH_s", [NH, T, HD], BF16, **skind)
    VL_s = nc.dram_tensor("VL_s", [NH, T, HD], BF16, **skind)
    CH_s = nc.dram_tensor("CH_s", [D, T], BF16, **skind)
    CL_s = nc.dram_tensor("CL_s", [D, T], BF16, **skind)
    FF_s = nc.dram_tensor("FF_s", [T, D], BF16, **skind)

    B = Builder()

    with ExitStack() as top:
        e = top.enter_context
        id32 = e(nc.sbuf_tensor("id32s", [128, 128], F32))
        id16 = e(nc.sbuf_tensor("id16s", [128, 128], BF16))
        tri = e(nc.sbuf_tensor("tris", [128, 128], F32))
        qb_s = e(nc.sbuf_tensor("qbs", [128, ND], F32))
        kb_s = e(nc.sbuf_tensor("kbs", [128, ND], F32))
        vb_s = e(nc.sbuf_tensor("vbs", [128, ND], F32))
        ones1 = e(nc.sbuf_tensor("ones1", [1, 128], F32))
        sems = {}
        for s in ENGS:
            sems[s] = e(nc.semaphore("sem_" + s))
        barsem = e(nc.semaphore("sem_bar"))

        B.dma("sp", id32[:, :], id32_h.ap(), writes=["id32"])
        B.dma("sp", id16[:, :], id16_h.ap(), writes=["id16"])
        B.dma("sp", tri[:, :], tri_h.ap(), writes=["tri"])
        for bs, bh, key in ((qb_s, qb_h, "qb"), (kb_s, kb_h, "kb"), (vb_s, vb_h, "vb")):
            B.dma("sp", ap(bs, 0, [[ND, 128], [1, ND]]),
                  ap(bh, 0, [[1, 128], [128, ND]]), writes=[key],
                  allow_slow_non_contiguous=True)
        B.op("dve", lambda v: v.memset(ones1[:, :], 1.0), writes=["ones1"])

        if True:
            with ExitStack() as pab:
              xnh = pab.enter_context(nc.sbuf_tensor("xnh", [128, ND * T], BF16))
              xnl = pab.enter_context(nc.sbuf_tensor("xnl", [128, ND * T], BF16))
              with ExitStack() as pa:
                  ea = pa.enter_context
                  xt = [ea(nc.sbuf_tensor(f"xt{i}", [128, D], F32)) for i in range(2)]
                  dbuf = [ea(nc.sbuf_tensor(f"dbuf{i}", [128, D], F32)) for i in range(2)]
                  sqb = ea(nc.sbuf_tensor("sqb", [128, D], F32))
                  xnt = [ea(nc.sbuf_tensor(f"xnt{i}", [128, D], F32)) for i in range(2)]
                  st_sum = ea(nc.sbuf_tensor("st_sum", [128, 1], F32))
                  st_negm = ea(nc.sbuf_tensor("st_negm", [128, 1], F32))
                  st_vsum = ea(nc.sbuf_tensor("st_vsum", [128, 1], F32))
                  st_ve = ea(nc.sbuf_tensor("st_ve", [128, 1], F32))
                  st_sv = ea(nc.sbuf_tensor("st_sv", [128, 1], F32))
                  st_r1 = ea(nc.sbuf_tensor("st_r1", [128, 1], F32))
                  st_n1 = ea(nc.sbuf_tensor("st_n1", [128, 1], F32))
                  st_n3 = ea(nc.sbuf_tensor("st_n3", [128, 1], F32))
                  st_rs = ea(nc.sbuf_tensor("st_rs", [128, 1], F32))
                  xqst = [ea(nc.sbuf_tensor(f"xqst{i}", [128, 512], E4)) for i in range(2)]
                  ptp = [ea(nc.psum_tensor(f"ptp{i}", [128, 512], F32)) for i in range(2)]

                  for ti in range(NT):
                      p2 = ti % 2
                      B.dma("sp", xt[p2][:, :], x_h.ap()[ti * 128:(ti + 1) * 128, :],
                            writes=[f"xt{p2}"], tag=f"palod{ti}")
                      B.op("dve", lambda v, p=p2: v.tensor_reduce(
                          st_sum[:, :], xt[p][:, :], AX.X, ALU.add),
                          reads=[f"xt{p2}"], writes=["stats"], fence=True)
                      B.op("dve", lambda v: v.tensor_scalar(
                          st_negm[:, :], st_sum[:, :], -1.0 / D, None, ALU.mult),
                          reads=["stats"], writes=["stats"], fence=True)
                      B.op("dve", lambda v, p=p2: v.tensor_scalar(
                          dbuf[p][:, :], xt[p][:, :], st_negm[:, :], None, ALU.add),
                          reads=[f"xt{p2}", "stats"], writes=[f"dbuf{p2}"])
                      B.op("act", lambda a, p=p2: a.activation(
                          sqb[:, :], dbuf[p][:, :], AF.Square),
                          reads=[f"dbuf{p2}"], writes=["sqb"], fence=True)
                      B.op("dve", lambda v: v.tensor_reduce(
                          st_vsum[:, :], sqb[:, :], AX.X, ALU.add),
                          reads=["sqb"], writes=["stats"], fence=True)
                      B.op("act", lambda a: a.activation(
                          st_ve[:, :], st_vsum[:, :], AF.Copy, scale=1.0 / D, bias=EPS),
                          reads=["stats"], writes=["stats"], fence=True)
                      B.op("act", lambda a: a.activation(st_sv[:, :], st_ve[:, :], AF.Sqrt),
                           reads=["stats"], writes=["stats"], fence=True)
                      B.op("dve", lambda v: v.reciprocal(st_r1[:, :], st_sv[:, :]),
                           reads=["stats"], writes=["stats"], fence=True)
                      B.op("dve", lambda v: v.tensor_mul(st_n1[:, :], st_r1[:, :], st_r1[:, :]),
                           reads=["stats"], writes=["stats"], fence=True)
                      B.op("dve", lambda v: v.tensor_mul(st_n3[:, :], st_n1[:, :], st_ve[:, :]),
                           reads=["stats"], writes=["stats"], fence=True)
                      B.op("dve", lambda v: v.tensor_scalar(
                          st_n1[:, :], st_n3[:, :], -0.5, 1.5, ALU.mult, ALU.add),
                          reads=["stats"], writes=["stats"], fence=True)
                      B.op("dve", lambda v: v.tensor_mul(st_rs[:, :], st_r1[:, :], st_n1[:, :]),
                           reads=["stats"], writes=["stats"], fence=True)
                      B.op("dve", lambda v, p=p2: v.tensor_scalar(
                          xnt[p][:, :], dbuf[p][:, :], st_rs[:, :], None, ALU.mult),
                          reads=[f"dbuf{p2}", "stats"], writes=[f"xnt{p2}"])
                      for g in range(4):
                          gp = g % 2
                          def mk_tr(p, g, gp):
                              def f(t):
                                  r = None
                                  for k in range(4):
                                      ci = g * 4 + k
                                      r = t.transpose(ptp[gp][:, k * 128:(k + 1) * 128],
                                                      xnt[p][:, ci * 128:(ci + 1) * 128],
                                                      id32[:, :])
                                  return r
                              return f
                          B.op("te", mk_tr(p2, g, gp), reads=[f"xnt{p2}", "id32"],
                               writes=[f"P!ptp{gp}"], tag=f"patr{ti}_{g}")
                          c0 = (g * 4) * T + ti * 128
                          def mk_hi(gp, c0):
                              def f(v):
                                  return v.tensor_copy(
                                      ap(xnh, c0, [[ND * T, 128], [T, 4], [1, 128]]),
                                      ptp[gp][:, :])
                              return f
                          def mk_lo(gp, c0):
                              def f(v):
                                  return v.tensor_sub(
                                      ap(xnl, c0, [[ND * T, 128], [T, 4], [1, 128]]),
                                      ptp[gp][:, :],
                                      ap(xnh, c0, [[ND * T, 128], [T, 4], [1, 128]]))
                              return f
                          B.op("dve", mk_hi(gp, c0), reads=[f"P!ptp{gp}"],
                               writes=["xnh"], tag=f"pahi{ti}_{g}")
                          B.op("dve", mk_lo(gp, c0), reads=[f"P!ptp{gp}", "xnh"],
                               writes=["xnl"], tag=f"palo{ti}_{g}")
                          B.op("act", lambda a, gp=gp, p=p2: a.activation(
                              xqst[gp][:, :], ptp[gp][:, :], AF.Copy),
                              reads=[f"P!ptp{gp}"], writes=[f"xqst{gp}"])
                          B.dma("actq",
                                ap(xq_s, (g * 4 * 128) * T + ti * 128,
                                   [[T, 128], [128 * T, 4], [1, 128]]),
                                ap(xqst[gp], 0, [[512, 128], [128, 4], [1, 128]]),
                                reads=[f"xqst{gp}"], tag=f"paxq{ti}_{g}")
                  B.barrier()

              with ExitStack() as pb:
                  eb = pb.enter_context
                  wrow = [eb(nc.sbuf_tensor(f"wrow{i}", [128, D], F32)) for i in range(2)]
                  wTh = [eb(nc.sbuf_tensor(f"wTh{i}", [128, D], BF16)) for i in range(2)]
                  wTl = [eb(nc.sbuf_tensor(f"wTl{i}", [128, D], BF16)) for i in range(2)]
                  qf = eb(nc.sbuf_tensor("qf", [128, T], F32))
                  qhi = eb(nc.sbuf_tensor("qhi", [128, T], BF16))
                  qlo = eb(nc.sbuf_tensor("qlo", [128, T], BF16))
                  vth_s = eb(nc.sbuf_tensor("vth_s", [128, T], BF16))
                  vtl_s = eb(nc.sbuf_tensor("vtl_s", [128, T], BF16))
                  wtp = [eb(nc.psum_tensor(f"wtp{i}", [128, 512], F32)) for i in range(2)]
                  pQ = [eb(nc.psum_tensor(f"pQ{i}", [128, min(1024, T)], F32)) for i in range(2)]
                  vtp = eb(nc.psum_tensor("vtp", [128, 512], BF16))

                  HALF = min(1024, T)
                  NHALF = T // HALF
                  for wi, (W_h, bkey, QHd, QLd) in enumerate((
                          (qw_h, "qb", QH_s, QL_s), (kw_h, "kb", KH_s, KL_s),
                          (vw_h, "vb", VH_s, VL_s))):
                      bias_s = (qb_s, kb_s, vb_s)[wi]
                      for m in range(ND):
                          p2 = m % 2
                          B.dma("sp", wrow[p2][:, :], W_h.ap()[m * 128:(m + 1) * 128, :],
                                writes=[f"wrow{p2}"], tag=f"pbw{wi}_{m}")
                          for g in range(4):
                              gp = g % 2
                              def mk_wtr(p, g, gp):
                                  def f(t):
                                      r = None
                                      for k in range(4):
                                          ci = g * 4 + k
                                          r = t.transpose(
                                              wtp[gp][:, k * 128:(k + 1) * 128],
                                              wrow[p][:, ci * 128:(ci + 1) * 128],
                                              id32[:, :])
                                      return r
                                  return f
                              B.op("te", mk_wtr(p2, g, gp),
                                   reads=[f"wrow{p2}", "id32"], writes=[f"P!wtp{gp}"])
                              B.op("dve", lambda v, p=p2, g=g, gp=gp:
                                   v.tensor_copy(wTh[p][:, g * 512:(g + 1) * 512],
                                                 wtp[gp][:, :]),
                                   reads=[f"P!wtp{gp}"], writes=[f"wTh{p2}"])
                              B.op("dve", lambda v, p=p2, g=g, gp=gp:
                                   v.tensor_sub(wTl[p][:, g * 512:(g + 1) * 512],
                                                wtp[gp][:, :],
                                                wTh[p][:, g * 512:(g + 1) * 512]),
                                   reads=[f"P!wtp{gp}", f"wTh{p2}"], writes=[f"wTl{p2}"])
                          postmul = SCALE if wi == 0 else 1.0
                          for hf in range(NHALF):
                              def mk_mm(p, hf, ci):
                                  def f(t):
                                      r = None
                                      for pss in range(3):
                                          lhs = (wTh, wTh, wTl)[pss][p][:, ci * 128:(ci + 1) * 128]
                                          rsrc = (xnh, xnl, xnh)[pss]
                                          for (s0, s1) in slices(HALF):
                                              col = ci * T + hf * HALF + s0
                                              r = t.matmul(
                                                  pQ[hf % 2][:, s0:s1],
                                                  lhs,
                                                  ap(rsrc, col, [[ND * T, 128], [1, s1 - s0]]),
                                                  start=(ci == 0 and pss == 0),
                                                  stop=(ci == ND - 1 and pss == 2))
                                      return r
                                  return f
                              for ci in range(ND):
                                  B.op("te", mk_mm(p2, hf, ci),
                                       reads=[f"wTh{p2}", f"wTl{p2}", "xnh", "xnl"],
                                       writes=[f"P!pQ{hf % 2}"], tag=f"pbmm{wi}_{m}_{hf}_{ci}")
                              B.op("dve", lambda v, hf=hf, m=m, b=bias_s, pm=postmul:
                                  v.tensor_scalar(
                                      qf[:, hf * HALF:(hf + 1) * HALF], pQ[hf % 2][:, :],
                                      b[:, m:m + 1], pm, ALU.add, ALU.mult),
                                  reads=[f"P!pQ{hf % 2}", bkey], writes=["qf"])
                              B.op("dve", lambda v, hf=hf: v.tensor_copy(
                                  qhi[:, hf * HALF:(hf + 1) * HALF],
                                  qf[:, hf * HALF:(hf + 1) * HALF]),
                                  reads=["qf"], writes=["qhi"])
                              B.op("dve", lambda v, hf=hf: v.tensor_sub(
                                  qlo[:, hf * HALF:(hf + 1) * HALF],
                                  qf[:, hf * HALF:(hf + 1) * HALF],
                                  qhi[:, hf * HALF:(hf + 1) * HALF]),
                                  reads=["qf", "qhi"], writes=["qlo"])
                          if wi < 2:
                              B.dma("actq", QHd.ap()[m * 128:(m + 1) * 128, :], qhi[:, :],
                                    reads=["qhi"], tag=f"pbsth{wi}_{m}")
                              B.dma("actq", QLd.ap()[m * 128:(m + 1) * 128, :], qlo[:, :],
                                    reads=["qlo"], tag=f"pbstl{wi}_{m}")
                          else:
                              for hl in range(2):
                                  src = (qhi, qlo)[hl]
                                  stg = (vth_s, vtl_s)[hl]
                                  skey = ("qhi", "qlo")[hl]
                                  tkey = ("vth_s", "vtl_s")[hl]
                                  Vd = (VH_s, VL_s)[hl]
                                  for (t0, t1) in groups4(NT):
                                      def mk_vtr(src, t0, t1):
                                          def f(t):
                                              r = None
                                              for tj in range(t0, t1):
                                                  r = t.transpose(
                                                      vtp[:, (tj - t0) * 128:(tj - t0 + 1) * 128],
                                                      src[:, tj * 128:(tj + 1) * 128],
                                                      id16[:, :])
                                              return r
                                          return f
                                      B.op("te", mk_vtr(src, t0, t1), reads=[skey, "id16"],
                                           writes=["P!vtp"])
                                      B.op("dve", lambda v, stg=stg, t0=t0, t1=t1: v.tensor_copy(
                                          stg[:, t0 * 128:t1 * 128],
                                          vtp[:, 0:(t1 - t0) * 128]),
                                          reads=["P!vtp"], writes=[tkey])
                                  nh2 = (NT + 1) // 2
                                  for half in range(2):
                                      t0 = half * nh2
                                      cnt = min(NT, t0 + nh2) - t0
                                      if cnt <= 0:
                                          continue
                                      B.dma("actq",
                                            ap(Vd, m * T * HD + t0 * 128 * HD,
                                               [[HD, 128], [128 * HD, cnt], [1, HD]]),
                                            ap(stg, t0 * 128, [[T, 128], [128, cnt], [1, HD]]),
                                            reads=[tkey], tag=f"pbstv{hl}_{m}_{half}")
                  B.barrier()

            with ExitStack() as pc:
                ec = pc.enter_context
                qhh = [ec(nc.sbuf_tensor(f"qhh{i}", [128, T], BF16)) for i in range(2)]
                qhl = [ec(nc.sbuf_tensor(f"qhl{i}", [128, T], BF16)) for i in range(2)]
                khh = [ec(nc.sbuf_tensor(f"khh{i}", [128, T], BF16)) for i in range(2)]
                khl = [ec(nc.sbuf_tensor(f"khl{i}", [128, T], BF16)) for i in range(2)]
                vthb = [ec(nc.sbuf_tensor(f"vthb{i}", [128, T], BF16)) for i in range(2)]
                vtlb = [ec(nc.sbuf_tensor(f"vtlb{i}", [128, T], BF16)) for i in range(2)]
                pbuf = [ec(nc.sbuf_tensor(f"pbuf{i}", [128, T], F32)) for i in range(2)]
                phib = [ec(nc.sbuf_tensor(f"phib{i}", [128, T], BF16)) for i in range(2)]
                plob = [ec(nc.sbuf_tensor(f"plob{i}", [128, T], BF16)) for i in range(2)]
                ptb = ec(nc.sbuf_tensor("ptb", [128, 2 * T], BF16))
                ctxh = ec(nc.sbuf_tensor("ctxh", [128, T], F32))
                cth = [ec(nc.sbuf_tensor(f"cth{i}", [128, T], BF16)) for i in range(2)]
                ctl = [ec(nc.sbuf_tensor(f"ctl{i}", [128, T], BF16)) for i in range(2)]
                a_negm = [ec(nc.sbuf_tensor(f"a_negm{i}", [128, 1], F32)) for i in range(2)]
                a_sum = [ec(nc.sbuf_tensor(f"a_sum{i}", [128, 1], F32)) for i in range(2)]
                a_rec = [ec(nc.sbuf_tensor(f"a_rec{i}", [128, 1], F32)) for i in range(2)]
                Sps = ec(nc.psum_tensor("P!Sps", [128, T if T <= 2048 else 2048], F32))
                ptpc = [ec(nc.psum_tensor(f"P!ptpc{i}", [128, 1024], BF16)) for i in range(2)]
                ctxp = ec(nc.psum_tensor("P!ctxp", [128, 128], F32))

                def emit_head_loads(h):
                    p2 = h % 2
                    for (buf, S_h) in ((qhh[p2], QH_s), (qhl[p2], QL_s),
                                       (khh[p2], KH_s), (khl[p2], KL_s)):
                        B.dma("sp", buf[:, :], S_h.ap()[h * 128:(h + 1) * 128, :],
                              writes=[f"qkv{p2}"], tag=f"pcl{h}")
                    nh2 = (NT + 1) // 2
                    for (buf, S_h) in ((vthb[p2], VH_s), (vtlb[p2], VL_s)):
                        for half in range(2):
                            t0 = half * nh2
                            cnt = min(NT, t0 + nh2) - t0
                            if cnt <= 0:
                                continue
                            B.dma("sp", ap(buf, t0 * 128, [[T, 128], [128, cnt], [1, HD]]),
                                  ap(S_h, h * T * HD + t0 * 128 * HD,
                                     [[HD, 128], [128 * HD, cnt], [1, HD]]),
                                  writes=[f"qkv{p2}"], tag=f"pclv{h}_{half}")

                def emit_scores(h, at):
                    p2 = h % 2
                    bl = (at + 1) * 128
                    nslc = (bl + 511) // 512
                    def mk_sc(p, at, bl, nslc):
                        def f(t):
                            r = None
                            for pss in range(3):
                                lhs = (qhh, qhh, qhl)[pss][p][:, at * 128:(at + 1) * 128]
                                rsrc = (khh, khl, khh)[pss][p]
                                for s in range(nslc):
                                    c1 = min(bl, (s + 1) * 512)
                                    r = t.matmul(Sps[:, s * 512:c1],
                                                 lhs, rsrc[:, s * 512:c1],
                                                 start=(pss == 0), stop=(pss == 2))
                            return r
                        return f
                    B.op("te", mk_sc(p2, at, bl, nslc),
                         reads=[f"qkv{p2}"], writes=["P!Sps"], tag=f"pcsc{h}_{at}")

                def emit_softmax(h, at):
                    bl = (at + 1) * 128
                    q = at % 2
                    B.op("dve", lambda v, at=at: v.tensor_add(
                        Sps[:, at * 128:(at + 1) * 128],
                        Sps[:, at * 128:(at + 1) * 128], tri[:, :]),
                        reads=["P!Sps", "tri"], writes=["P!Sps"])
                    B.op("dve", lambda v, bl=bl, q=q: v.tensor_reduce(
                        a_negm[q][:, :], Sps[:, 0:bl], AX.X, ALU.max, negate=True),
                        reads=["P!Sps"], writes=[f"ast{q}"], fence=True)
                    B.op("act", lambda a, bl=bl, q=q: a.activation(
                        pbuf[q][:, 0:bl], Sps[:, 0:bl], AF.Exp,
                        bias=a_negm[q][:, :], scale=1.0,
                        accum_out=a_sum[q][:, :]),
                        reads=["P!Sps", f"ast{q}"], writes=[f"pbuf{q}", f"ast{q}"],
                        fence=True)
                    B.op("dve", lambda v, q=q: v.reciprocal(a_rec[q][:, :], a_sum[q][:, :]),
                         reads=[f"ast{q}"], writes=[f"ast{q}"], fence=True)
                    B.op("act", lambda a, bl=bl, q=q: a.activation(
                        phib[q][:, 0:bl], pbuf[q][:, 0:bl], AF.Copy, scale=a_rec[q][:, :]),
                        reads=[f"pbuf{q}", f"ast{q}"], writes=[f"phib{q}"])
                    B.op("dve", lambda v, bl=bl, q=q: v.scalar_tensor_tensor(
                        plob[q][:, 0:bl], pbuf[q][:, 0:bl], a_rec[q][:, :], phib[q][:, 0:bl],
                        ALU.mult, ALU.subtract),
                        reads=[f"pbuf{q}", f"ast{q}", f"phib{q}"], writes=[f"plob{q}"])

                def emit_groups(h, at):
                    p2 = h % 2
                    q = at % 2
                    nb = at + 1
                    for g in range((nb + 3) // 4):
                        k0 = g * 4
                        k1 = min(nb, k0 + 4)
                        gp = g % 2
                        def mk_ptr(q, k0, k1, gp):
                            def f(t):
                                r = None
                                for hl in range(2):
                                    src = (phib, plob)[hl][q]
                                    for k in range(k0, k1):
                                        r = t.transpose(
                                            ptpc[gp][:, hl * 512 + (k - k0) * 128:
                                                     hl * 512 + (k - k0 + 1) * 128],
                                            src[:, k * 128:(k + 1) * 128], id16[:, :])
                                return r
                            return f
                        B.op("te", mk_ptr(q, k0, k1, gp),
                             reads=[f"phib{q}", f"plob{q}", "id16"], writes=[f"P!ptpc{gp}"])
                        def mk_pev(k0, k1, gp):
                            def f(a):
                                return a.activation(
                                    ap(ptb, k0 * 128, [[2 * T, 128], [T, 2],
                                                       [1, (k1 - k0) * 128]]),
                                    ap(ptpc[gp], 0, [[1024, 128], [512, 2],
                                                     [1, (k1 - k0) * 128]]),
                                    AF.Copy)
                            return f
                        B.op("act", mk_pev(k0, k1, gp),
                             reads=[f"P!ptpc{gp}"], writes=["ptb"])
                        for k in range(k0, k1):
                            def mk_ctx(p, k, at):
                                def f(t):
                                    r = None
                                    for pss in range(3):
                                        lhs = (vthb, vthb, vtlb)[pss][p][:, k * 128:(k + 1) * 128]
                                        off = (0, T, 0)[pss]
                                        r = t.matmul(ctxp[:, :], lhs,
                                                     ptb[:, off + k * 128:off + (k + 1) * 128],
                                                     start=(k == 0 and pss == 0),
                                                     stop=(k == at and pss == 2))
                                    return r
                                return f
                            B.op("te", mk_ctx(p2, k, at),
                                 reads=[f"qkv{p2}", "ptb"], writes=["P!ctxp"],
                                 tag=f"pcctx{h}_{at}_{k}")

                def emit_ctx_evac(h, at):
                    B.op("act", lambda a, at=at: a.activation(
                        ctxh[:, at * 128:(at + 1) * 128], ctxp[:, :], AF.Copy),
                        reads=["P!ctxp"], writes=["ctxh"])

                emit_head_loads(0)
                for h in range(NH):
                    p2 = h % 2
                    emit_scores(h, 0)
                    for at in range(NT):
                        emit_softmax(h, at)
                        if at + 1 < NT:
                            emit_scores(h, at + 1)
                        if at > 0:
                            emit_ctx_evac(h, at - 1)
                        emit_groups(h, at)
                        if h + 1 < NH and at == 0:
                            emit_head_loads(h + 1)
                    emit_ctx_evac(h, NT - 1)
                    B.op("dve", lambda v, p=p2: v.tensor_copy(cth[p][:, :], ctxh[:, :]),
                         reads=["ctxh"], writes=[f"cth{p2}"])
                    B.op("dve", lambda v, p=p2: v.tensor_sub(
                        ctl[p][:, :], ctxh[:, :], cth[p][:, :]),
                        reads=["ctxh", f"cth{p2}"], writes=[f"ctl{p2}"])
                    B.dma("actq", CH_s.ap()[h * 128:(h + 1) * 128, :], cth[p2][:, :],
                          reads=[f"cth{p2}"], tag=f"pcsth{h}")
                    B.dma("actq", CL_s.ap()[h * 128:(h + 1) * 128, :], ctl[p2][:, :],
                          reads=[f"ctl{p2}"], tag=f"pcstl{h}")
                B.barrier()

            with ExitStack() as pe_outer:
              aq = pe_outer.enter_context(nc.sbuf_tensor("aq", [128, (HID // 128) * T], E4))
              with ExitStack() as pe:
                ee = pe.enter_context
                xq_sb = ee(nc.sbuf_tensor("xq_sb", [128, ND * T], E4))
                wgs = [[ee(nc.sbuf_tensor(f"wgs{g}{i}", [128, D], F32)) for i in range(2)]
                       for g in range(2)]
                wgq = [[ee(nc.sbuf_tensor(f"wgq{g}{i}", [128, D], E4)) for i in range(2)]
                       for g in range(2)]
                GH = min(1024, T)
                NGH = T // GH
                gb = [ee(nc.sbuf_tensor(f"gb{i}", [128, GH], BF16)) for i in range(2)]
                fb = [ee(nc.sbuf_tensor(f"fb{i}", [128, GH], BF16)) for i in range(2)]
                sA = [ee(nc.sbuf_tensor(f"sA{i}", [128, GH], BF16)) for i in range(2)]
                sB = [ee(nc.sbuf_tensor(f"sB{i}", [128, GH], BF16)) for i in range(2)]
                pg = [[ee(nc.psum_tensor(f"pg{g}{i}", [128, GH], F32)) for i in range(2)]
                      for g in range(2)]

                for ci in range(ND):
                    B.dma("sp", xq_sb[:, ci * T:(ci + 1) * T],
                          xq_s.ap()[ci * 128:(ci + 1) * 128, :],
                          writes=["xq_sb"], tag=f"pexq{ci}")
                B.op("dve", lambda v: v.tensor_copy(
                    xq_sb[:, :].bitcast(U32), xq_sb[:, :].bitcast(U32)),
                    reads=["xq_sb"], writes=["xq_sb"], tag="lxq")
                for m in range(HID // 128):
                    p2 = m % 2
                    for g in range(2):
                        for q4 in range(4):
                            B.dma("sp",
                                  ap(wgs[g][p2], q4 * 4 * 128,
                                     [[D, 128], [128, 4], [1, 128]]),
                                  ap(wg_h, g * D * HID + (q4 * 4 * 128) * HID + m * 128,
                                     [[HID, 128], [128 * HID, 4], [1, 128]]),
                                  writes=[f"wgs{g}{p2}"], tag=f"pewg{g}_{m}_{q4}")
                        B.op("dve", lambda v, g=g, p=p2: v.tensor_copy(
                            wgq[g][p][:, :], wgs[g][p][:, :]),
                            reads=[f"wgs{g}{p2}"], writes=[f"wgq{g}{p2}"])
                        B.op("dve", lambda v, g=g, p=p2: v.tensor_copy(
                            wgq[g][p][:, :].bitcast(U32), wgq[g][p][:, :].bitcast(U32)),
                            reads=[f"wgq{g}{p2}"], writes=[f"wgq{g}{p2}"], tag="lwg")
                    for hf in range(NGH):
                        for g in range(2):
                            def mk_gmm(g, p, hf):
                                def f(t):
                                    r = None
                                    for ci in range(0, ND, 2):
                                        for (s0, s1) in slices(GH):
                                            col = ci * T + hf * GH + s0
                                            r = t.matmul(
                                                pg[g][hf % 2][:, s0:s1],
                                                ap(wgq[g][p], ci * 128,
                                                   [[D, 128], [128, 2], [1, 128]]),
                                                ap(xq_sb, col,
                                                   [[ND * T, 128], [T, 2], [1, s1 - s0]]),
                                                start=(ci == 0), stop=(ci == ND - 2),
                                                perf_mode=DR)
                                    return r
                                return f
                            B.op("te", mk_gmm(g, p2, hf),
                                 reads=[f"wgq{g}{p2}", "xq_sb"],
                                 writes=[f"P!pg{g}{hf % 2}"], tag=f"pegm{g}_{m}_{hf}")
                        hp = hf % 2
                        pg0, pg1 = pg[0][hp], pg[1][hp]
                        r0, r1 = f"P!pg0{hp}", f"P!pg1{hp}"
                        gbk, fbk = f"gb{hp}", f"fb{hp}"
                        sak, sbk = f"sA{hp}", f"sB{hp}"
                        B.op("dve", lambda v, hp=hp, pg0=pg0: v.tensor_copy(
                            gb[hp][:, :], pg0[:, :]), reads=[r0], writes=[gbk])
                        B.op("dve", lambda v, hp=hp, pg1=pg1: v.tensor_copy(
                            fb[hp][:, :], pg1[:, :]), reads=[r1], writes=[fbk])
                        B.op("act", lambda a, hp=hp: a.activation(
                            sA[hp][:, :], gb[hp][:, :], AF.Square),
                            reads=[gbk], writes=[sak])
                        B.op("dve", lambda v, hp=hp: v.tensor_mul(
                            sB[hp][:, :], sA[hp][:, :], gb[hp][:, :]),
                            reads=[sak, gbk], writes=[sbk])
                        B.op("dve", lambda v, hp=hp: v.tensor_scalar(
                            sA[hp][:, :], sB[hp][:, :], GA, None, ALU.mult),
                            reads=[sbk], writes=[sak])
                        B.op("dve", lambda v, hp=hp: v.tensor_add(
                            sB[hp][:, :], sA[hp][:, :], gb[hp][:, :]),
                            reads=[sak, gbk], writes=[sbk])
                        B.op("dve", lambda v, hp=hp: v.tensor_scalar(
                            sA[hp][:, :], sB[hp][:, :], GC, None, ALU.mult),
                            reads=[sbk], writes=[sak])
                        B.op("act", lambda a, hp=hp: a.activation(
                            sB[hp][:, :], sA[hp][:, :], AF.Tanh),
                            reads=[sak], writes=[sbk])
                        B.op("dve", lambda v, hp=hp: v.tensor_scalar(
                            sA[hp][:, :], sB[hp][:, :], 0.5, 0.5, ALU.mult, ALU.add),
                            reads=[sbk], writes=[sak])
                        B.op("dve", lambda v, hp=hp: v.tensor_mul(
                            sB[hp][:, :], gb[hp][:, :], sA[hp][:, :]),
                            reads=[gbk, sak], writes=[sbk])
                        B.op("dve", lambda v, hp=hp: v.tensor_mul(
                            sA[hp][:, :], sB[hp][:, :], fb[hp][:, :]),
                            reads=[sbk, fbk], writes=[sak])
                        B.op("act", lambda a, hp=hp, m=m, hf=hf: a.activation(
                            aq[:, m * T + hf * GH:m * T + (hf + 1) * GH],
                            sA[hp][:, :], AF.Copy),
                            reads=[sak], writes=["aq"], tag=f"peaq{m}_{hf}")
                B.op("dve", lambda v: v.tensor_copy(
                    aq[:, :].bitcast(U32), aq[:, :].bitcast(U32)),
                    reads=["aq"], writes=["aq"], tag="laq")
                B.barrier()

              if dbg:
                  ffdbg_h = nc.dram_tensor("ff_fm_dbg", [D, T], BF16,
                                           kind="ExternalOutput")
                  aq_dbg = nc.dram_tensor("aq_dbg", [128, (HID // 128) * T], E4,
                                          kind="ExternalOutput")
                  B.dma("sp", aq_dbg.ap(), aq[:, :], reads=["aq"], tag="aqdbg")
                  B.barrier()

              with ExitStack() as pf:
                  ef = pf.enter_context
                  NH2 = HID // 128
                  wlt = ef(nc.sbuf_tensor("wlt", [128, NH2 * 128], F32))
                  wlq = [ef(nc.sbuf_tensor(f"wlq{i}", [128, NH2 * 128], E4)) for i in range(2)]
                  ffn16 = ef(nc.sbuf_tensor("ffn16", [128, T], BF16))
                  fts = ef(nc.sbuf_tensor("fts", [128, T], BF16))
                  LH = min(1024, T)
                  NLH = T // LH
                  pL = [[ef(nc.psum_tensor(f"pL{j}{i}", [128, LH], F32)) for i in range(NLH)]
                        for j in range(1)]
                  ftp = ef(nc.psum_tensor("P!ftp", [128, 512], BF16))

                  for j in range(ND):
                      jp = 0
                      wp = j % 2
                      for q4 in range(NH2 // 4):
                          B.dma("sp",
                                ap(wlt, q4 * 4 * 128, [[NH2 * 128, 128], [128, 4], [1, 128]]),
                                ap(wl_h, (q4 * 4 * 128) * D + j * 128,
                                   [[D, 128], [128 * D, 4], [1, 128]]),
                                writes=["wlt"], tag=f"pfw{j}_{q4}")
                      B.op("act", lambda a, wp=wp: a.activation(
                          wlq[wp][:, :], wlt[:, :], AF.Copy),
                          reads=["wlt"], writes=[f"wlq{wp}"])
                      B.op("dve", lambda v, wp=wp: v.tensor_copy(
                          wlq[wp][:, :].bitcast(U32), wlq[wp][:, :].bitcast(U32)),
                          reads=[f"wlq{wp}"], writes=[f"wlq{wp}"], tag="lwl")
                      for h2 in range(0, NH2, 2):
                          def mk_lmm(wp, jp, h2):
                              def f(t):
                                  r = None
                                  for hf in range(NLH):
                                      for (s0, s1) in slices(LH):
                                          col = h2 * T + hf * LH + s0
                                          r = t.matmul(
                                              pL[jp][hf][:, s0:s1],
                                              ap(wlq[wp], h2 * 128,
                                                 [[NH2 * 128, 128], [128, 2], [1, 128]]),
                                              ap(aq, col,
                                                 [[(HID // 128) * T, 128], [T, 2], [1, s1 - s0]]),
                                              start=(h2 == 0), stop=(h2 == NH2 - 2),
                                              perf_mode=DR)
                                  return r
                              return f
                          B.op("te", mk_lmm(wp, jp, h2),
                               reads=[f"wlq{wp}", "aq"], writes=[f"P!pL{jp}"],
                               tag=f"pflm{j}_{h2}")
                      for hf in range(NLH):
                          B.op("dve", lambda v, jp=jp, hf=hf: v.tensor_copy(
                              ffn16[:, hf * LH:(hf + 1) * LH], pL[jp][hf][:, :]),
                              reads=[f"P!pL{jp}"], writes=["ffn16"])
                      B.op("dve", lambda v: v.tensor_copy(
                          ffn16[:, :].bitcast(U32), ffn16[:, :].bitcast(U32)),
                          reads=["ffn16"], writes=["ffn16"], tag="lff")
                      if dbg:
                          B.dma("sp", ffdbg_h.ap()[j * 128:(j + 1) * 128, :],
                                ffn16[:, :], reads=["ffn16"], tag=f"ffdbg{j}")
                      for (t0, t1) in groups4(NT):
                          def mk_ftr(t0, t1):
                              def f(t):
                                  r = None
                                  for tj in range(t0, t1):
                                      r = t.transpose(ftp[:, (tj - t0) * 128:(tj - t0 + 1) * 128],
                                                      ffn16[:, tj * 128:(tj + 1) * 128],
                                                      id16[:, :])
                                  return r
                              return f
                          B.op("te", mk_ftr(t0, t1), reads=["ffn16", "id16"], writes=["P!ftp"])
                          B.op("act", lambda a, t0=t0, t1=t1: a.activation(
                              fts[:, t0 * 128:t1 * 128], ftp[:, 0:(t1 - t0) * 128], AF.Copy),
                              reads=["P!ftp"], writes=["fts"])
                      for (t0, t1) in groups4(NT):
                          B.dma("actq",
                                ap(FF_s, j * 128 + (t0 * 128) * D,
                                   [[D, 128], [128 * D, t1 - t0], [1, 128]]),
                                ap(fts, t0 * 128, [[T, 128], [128, t1 - t0], [1, 128]]),
                                reads=["fts"], tag=f"pfst{j}_{t0}")
                  B.barrier()

            with ExitStack() as pd:
                ed = pd.enter_context
                owth = ed(nc.sbuf_tensor("owth", [128, ND * D], BF16))
                owtl = ed(nc.sbuf_tensor("owtl", [128, ND * D], BF16))
                obb = ed(nc.sbuf_tensor("obb", [128, D], F32))
                with ExitStack() as pp:
                    ep = pp.enter_context
                    ob_row = ep(nc.sbuf_tensor("obrow", [1, D], F32))
                    pbc = ep(nc.psum_tensor("P!pbc", [128, D], F32))
                    B.dma("sp", ap(ob_row, 0, [[D, 1], [1, D]]),
                          ap(ob_h, 0, [[D, 1], [1, D]]), writes=["obrow"])
                    def mk_bc(s):
                        def f(t):
                            return t.matmul(pbc[:, s * 512:(s + 1) * 512], ones1[:, :],
                                            ob_row[:, s * 512:(s + 1) * 512],
                                            start=True, stop=True)
                        return f
                    for s in range(4):
                        B.op("te", mk_bc(s), reads=["ones1", "obrow"], writes=["P!pbc"])
                    B.op("dve", lambda v: v.tensor_copy(obb[:, :], pbc[:, :]),
                         reads=["P!pbc"], writes=["obb"])
                    wrow2 = [ep(nc.sbuf_tensor(f"wrow2{i}", [128, D], F32)) for i in range(2)]
                    wth2 = [ep(nc.sbuf_tensor(f"wth2{i}", [128, D], BF16)) for i in range(2)]
                    wtl2 = [ep(nc.sbuf_tensor(f"wtl2{i}", [128, D], BF16)) for i in range(2)]
                    wtp2 = [ep(nc.psum_tensor(f"wtp2{i}", [128, 512], BF16)) for i in range(2)]
                    for m in range(ND):
                        p2 = m % 2
                        B.dma("sp", wrow2[p2][:, :], ow_h.ap()[m * 128:(m + 1) * 128, :],
                              writes=[f"wrow2{p2}"], tag=f"pdw{m}")
                        B.op("dve", lambda v, p=p2: v.tensor_copy(
                            wth2[p][:, :], wrow2[p][:, :]),
                            reads=[f"wrow2{p2}"], writes=[f"wth2{p2}"])
                        B.op("dve", lambda v, p=p2: v.tensor_sub(
                            wtl2[p][:, :], wrow2[p][:, :], wth2[p][:, :]),
                            reads=[f"wrow2{p2}", f"wth2{p2}"], writes=[f"wtl2{p2}"])
                        for hl in range(2):
                            src = (wth2, wtl2)[hl]
                            dstT = (owth, owtl)[hl]
                            skey = (f"wth2{p2}", f"wtl2{p2}")[hl]
                            dkey = ("owth", "owtl")[hl]
                            for g in range(4):
                                gp = g % 2
                                def mk_otr(src, p, g, gp):
                                    def f(t):
                                        r = None
                                        for k in range(4):
                                            ci = g * 4 + k
                                            r = t.transpose(
                                                wtp2[gp][:, k * 128:(k + 1) * 128],
                                                src[p][:, ci * 128:(ci + 1) * 128],
                                                id16[:, :])
                                        return r
                                    return f
                                B.op("te", mk_otr(src, p2, g, gp),
                                     reads=[skey, "id16"], writes=[f"P!wtp2{gp}"])
                                c0 = (g * 4) * D + m * 128
                                def mk_oev(dstT, c0, gp):
                                    def f(v):
                                        return v.tensor_copy(
                                            ap(dstT, c0, [[ND * D, 128], [D, 4], [1, 128]]),
                                            wtp2[gp][:, :])
                                    return f
                                B.op("dve", mk_oev(dstT, c0, gp),
                                     reads=[f"P!wtp2{gp}"], writes=[dkey])
                    B.barrier()
                with ExitStack() as pm:
                    em = pm.enter_context
                    scth = [em(nc.sbuf_tensor(f"scth{i}", [128, D], BF16)) for i in range(2)]
                    sctl = [em(nc.sbuf_tensor(f"sctl{i}", [128, D], BF16)) for i in range(2)]
                    xt2 = [em(nc.sbuf_tensor(f"xt2{i}", [128, D], F32)) for i in range(2)]
                    ffnt = [em(nc.sbuf_tensor(f"ffnt{i}", [128, D], BF16)) for i in range(2)]
                    o1b = em(nc.sbuf_tensor("o1b", [128, D], F32))
                    o2b = em(nc.sbuf_tensor("o2b", [128, D], F32))
                    outsb = [em(nc.sbuf_tensor(f"outsb{i}", [128, D], F32)) for i in range(2)]
                    pO = [em(nc.psum_tensor(f"pO{i}", [128, D], F32)) for i in range(2)]

                    for ti in range(NT):
                        p2 = ti % 2
                        for hl in range(2):
                            Cs = (CH_s, CL_s)[hl]
                            dst = (scth, sctl)[hl]
                            dkey = (f"scth{p2}", f"sctl{p2}")[hl]
                            for q4 in range(ND // 4):
                                B.dma("actq",
                                      ap(dst[p2], q4 * 4 * 128,
                                         [[D, 128], [128, 4], [1, 128]]),
                                      ap(Cs, ti * 128 + (q4 * 4 * 128) * T,
                                         [[T, 128], [128 * T, 4], [1, 128]]),
                                      writes=[dkey], tag=f"pdc{hl}_{ti}_{q4}")
                        B.dma("sp", xt2[p2][:, :], x_h.ap()[ti * 128:(ti + 1) * 128, :],
                              writes=[f"xt2{p2}"], tag=f"pdx{ti}")
                        B.dma("sp", ffnt[p2][:, :], FF_s.ap()[ti * 128:(ti + 1) * 128, :],
                              writes=[f"ffnt{p2}"], tag=f"pdf{ti}")
                        for ci in range(ND):
                            def mk_omm(p, ci):
                                def f(t):
                                    r = None
                                    for pss in range(3):
                                        lhs = (scth, scth, sctl)[pss][p][:, ci * 128:(ci + 1) * 128]
                                        rsrc = (owth, owtl, owth)[pss]
                                        for s in range(D // 512):
                                            r = t.matmul(
                                                pO[p][:, s * 512:(s + 1) * 512],
                                                lhs,
                                                ap(rsrc, ci * D + s * 512,
                                                   [[ND * D, 128], [1, 512]]),
                                                start=(ci == 0 and pss == 0),
                                                stop=(ci == ND - 1 and pss == 2))
                                    return r
                                return f
                            B.op("te", mk_omm(p2, ci),
                                 reads=[f"scth{p2}", f"sctl{p2}", "owth", "owtl"],
                                 writes=[f"P!pO{p2}"], tag=f"pdmm{ti}_{ci}")
                        B.op("dve", lambda v, p=p2: v.tensor_add(
                            o1b[:, :], pO[p][:, :], obb[:, :]),
                            reads=[f"P!pO{p2}", "obb"], writes=["o1b"])
                        B.op("dve", lambda v, p=p2: v.tensor_add(
                            o2b[:, :], o1b[:, :], xt2[p][:, :]),
                            reads=["o1b", f"xt2{p2}"], writes=["o2b"])
                        B.op("dve", lambda v, p=p2: v.tensor_add(
                            outsb[p][:, :], o2b[:, :], ffnt[p][:, :]),
                            reads=["o2b", f"ffnt{p2}"], writes=[f"outsb{p2}"])
                        B.dma("sp", out_h.ap()[ti * 128:(ti + 1) * 128, :], outsb[p2][:, :],
                              reads=[f"outsb{p2}"], tag=f"pdout{ti}")
                    B.barrier()

        if max_ops is not None:
            kept, n = [], 0
            for o in B.ops:
                if o["barrier"]:
                    continue
                kept.append(o)
                n += 1
                if n >= max_ops:
                    break
            for i, o in enumerate(kept):
                o["idx"] = i
            B.ops = kept
            B.barrier()
        if max_barriers is not None:
            nb, cut = 0, len(B.ops)
            for i, o in enumerate(B.ops):
                if o["barrier"]:
                    nb += 1
                    if nb >= max_barriers:
                        cut = i + 1
                        break
            B.ops = B.ops[:cut]
        B.schedule()
        with nc.Block() as block:
            block.tensor(lambda t: B.emit_engine(("te",), t, sems, barsem, 4))
            block.vector(lambda v: B.emit_engine(("dve",), v, sems, barsem, 4))
            block.scalar(lambda a: B.emit_engine(("act", "actq"), a, sems, barsem, 4))
            block.sync(lambda s: B.emit_engine(("sp",), s, sems, barsem, 4))
    nops = len(B.ops)
    return nc, nops


def _consts():
    id32 = np.eye(128, dtype=np.float32)
    id16 = np.eye(128, dtype=np.float32).astype(ml_dtypes.bfloat16)
    tri = np.triu(np.full((128, 128), NEG, dtype=np.float32), 1)
    return id32, id16, tri


def kernel(L0_attn_x, L0_attn_q_w, L0_attn_q_b, L0_attn_k_w, L0_attn_k_b,
           L0_attn_v_w, L0_attn_v_b, L0_attn_o_w, L0_attn_o_b,
           L0_ffn_w_gating, L0_ffn_w_linear):
    x = np.ascontiguousarray(np.asarray(L0_attn_x, dtype=np.float32))
    nbatch, T, _ = x.shape
    id32, id16, tri = _consts()
    nc, _ = build_nc(T)
    common = dict(
        qw=np.ascontiguousarray(np.asarray(L0_attn_q_w, np.float32)),
        kw=np.ascontiguousarray(np.asarray(L0_attn_k_w, np.float32)),
        vw=np.ascontiguousarray(np.asarray(L0_attn_v_w, np.float32)),
        ow=np.ascontiguousarray(np.asarray(L0_attn_o_w, np.float32)),
        qb=np.ascontiguousarray(np.asarray(L0_attn_q_b, np.float32)),
        kb=np.ascontiguousarray(np.asarray(L0_attn_k_b, np.float32)),
        vb=np.ascontiguousarray(np.asarray(L0_attn_v_b, np.float32)),
        ob=np.ascontiguousarray(np.asarray(L0_attn_o_b, np.float32)),
        wg=np.ascontiguousarray(np.asarray(L0_ffn_w_gating, np.float32)),
        wl=np.ascontiguousarray(np.asarray(L0_ffn_w_linear, np.float32)),
        id32=id32, id16=id16, tri=tri,
    )
    in_maps = [dict(common, x=np.ascontiguousarray(x[b])) for b in range(nbatch)]
    res = run_bass_kernel_spmd(nc, in_maps, core_ids=list(range(nbatch)))
    return np.stack([res.results[b]["out"] for b in range(nbatch)], axis=0)


# version 15
# speedup vs baseline: 1.0812x; 1.0005x over previous
import math
from contextlib import ExitStack

import numpy as np
import ml_dtypes

import concourse.bass as bass
import concourse.mybir as mybir
from concourse.bass_utils import run_bass_kernel_spmd

F32 = mybir.dt.float32
BF16 = mybir.dt.bfloat16
E4 = mybir.dt.float8e4
AF = mybir.ActivationFunctionType
ALU = mybir.AluOpType
AX = mybir.AxisListType
U32 = mybir.dt.uint32
DR = mybir.MatmulPerfMode.DoubleRow

D = 2048
HID = 8192
NH, HD = 16, 128
SCALE = float(HD) ** -0.5
EPS = 1e-5
GA = float(np.float32(ml_dtypes.bfloat16(0.044715)))
GC = float(np.float32(ml_dtypes.bfloat16(math.sqrt(2.0 / math.pi))))
NEG = -1.0e30

ENGS = ("te", "dve", "act", "sp", "actq")
QUEUES = ("sp", "actq")


class Builder:

    def __init__(self):
        self.ops = []

    def op(self, eng, fn, reads=(), writes=(), tag="", fence=False):
        self.ops.append(dict(eng=eng, fn=fn, reads=tuple(reads), writes=tuple(writes),
                             tag=tag, barrier=False, idx=len(self.ops), fence=fence))

    def dma(self, eng, dst, src, reads=(), writes=(), tag="", **kw):
        assert eng in QUEUES
        self.op(eng, (lambda e, d=dst, s=src: e.dma_start(d, s, **kw)), reads, writes, tag)

    def barrier(self):
        self.ops.append(dict(eng=None, fn=None, reads=(), writes=(), tag="bar",
                             barrier=True, idx=len(self.ops)))

    def schedule(self):
        writers = {}
        readers = {}
        nbar = 0
        self._next_te = {}
        last_te = None
        for o in self.ops:
            if o["barrier"]:
                last_te = None
                continue
            if o["eng"] == "te":
                if last_te is not None:
                    self._next_te[last_te["idx"]] = o
                last_te = o
        for o in self.ops:
            if o["barrier"]:
                nbar += 1
                o["barn"] = nbar
                writers.clear()
                readers.clear()
                continue
            deps = []
            for k in o["reads"]:
                deps.extend(writers.get(k, ()))
                if k.startswith("P!"):
                    deps.extend(r for r in readers.get(k, ())
                                if r["eng"] != o["eng"])
            war = []
            for k in o["writes"]:
                deps.extend(writers.get(k, ()))
                war.extend(readers.get(k, ()))
            bumped = []
            for d in war:
                if d["eng"] == "te":
                    nxt = self._next_te.get(d["idx"])
                    if nxt is not None and nxt["idx"] < o["idx"]:
                        bumped.append(nxt)
                    else:
                        bumped.append(d)
                else:
                    bumped.append(d)
            deps.extend(bumped)
            need = {}
            for d in deps:
                if (d["eng"] == o["eng"] and d["eng"] not in QUEUES
                        and not d.get("fence")):
                    continue
                need[d["idx"]] = d
            o["deps"] = list(need.values())
            for d in o["deps"]:
                d["inc"] = True
                assert d["idx"] < o["idx"]
            for k in o["reads"]:
                readers.setdefault(k, []).append(o)
            for k in o["writes"]:
                writers[k] = [o]
                readers[k] = []
        cnt = {e: 0 for e in ENGS}
        for o in self.ops:
            if o["barrier"]:
                continue
            if o["eng"] in QUEUES:
                cnt[o["eng"]] += 16
                o["semval"] = cnt[o["eng"]]
                o["inc"] = True
            elif o.get("inc"):
                cnt[o["eng"]] += 1
                o["semval"] = cnt[o["eng"]]
        self.final_counts = dict(cnt)
        self.nbar = nbar

    def emit_engine(self, eng_names, engine, sems, barsem, nbeng):
        hw = {}
        bar_seen = 0
        for o in self.ops:
            if o["barrier"]:
                bar_seen += 1
                for qn in eng_names:
                    if qn in QUEUES:
                        q = self._qcount_at(o["idx"], qn)
                        if q > hw.get(qn, 0):
                            engine.wait_ge(sems[qn], q)
                            hw[qn] = q
                engine.nop().then_inc(barsem, 1)
                engine.wait_ge(barsem, nbeng * bar_seen)
                continue
            if o["eng"] not in eng_names:
                continue
            for d in o.get("deps", ()):
                sname = d["eng"]
                v = d["semval"]
                if v > hw.get(sname, 0):
                    engine.wait_ge(sems[sname], v)
                    hw[sname] = v
            instr = o["fn"](engine)
            if o.get("inc"):
                if o["eng"] in QUEUES:
                    instr.then_inc(sems[o["eng"]], 16)
                else:
                    instr.then_inc(sems[o["eng"]], 1)

    def _qcount_at(self, idx, qname):
        c = 0
        for o in self.ops:
            if o["idx"] >= idx:
                break
            if (not o["barrier"]) and o["eng"] == qname:
                c += 16
        return c


def ap(t, off, dims):
    return bass.AP(t, off, dims)


def slices(span, step=512):
    return [(s, min(span, s + step)) for s in range(0, span, step)]


def groups4(n):
    return [(g * 4, min(n, g * 4 + 4)) for g in range((n + 3) // 4)]


def build_nc(T, max_barriers=None, max_ops=None, dbg=False):
    NT = T // 128
    ND = D // 128
    skind = dict(kind="ExternalOutput") if dbg else {}
    nc = bass.Bass("TRN2", target_bir_lowering=False, detect_race_conditions=False,
                   dynamic_dma_scratch_size=2048)

    x_h = nc.dram_tensor("x", [T, D], F32, kind="ExternalInput")
    qw_h = nc.dram_tensor("qw", [D, D], F32, kind="ExternalInput")
    kw_h = nc.dram_tensor("kw", [D, D], F32, kind="ExternalInput")
    vw_h = nc.dram_tensor("vw", [D, D], F32, kind="ExternalInput")
    ow_h = nc.dram_tensor("ow", [D, D], F32, kind="ExternalInput")
    qb_h = nc.dram_tensor("qb", [D], F32, kind="ExternalInput")
    kb_h = nc.dram_tensor("kb", [D], F32, kind="ExternalInput")
    vb_h = nc.dram_tensor("vb", [D], F32, kind="ExternalInput")
    ob_h = nc.dram_tensor("ob", [D], F32, kind="ExternalInput")
    wg_h = nc.dram_tensor("wg", [2, D, HID], F32, kind="ExternalInput")
    wl_h = nc.dram_tensor("wl", [HID, D], F32, kind="ExternalInput")
    id32_h = nc.dram_tensor("id32", [128, 128], F32, kind="ExternalInput")
    id16_h = nc.dram_tensor("id16", [128, 128], BF16, kind="ExternalInput")
    tri_h = nc.dram_tensor("tri", [128, 128], F32, kind="ExternalInput")
    out_h = nc.dram_tensor("out", [T, D], F32, kind="ExternalOutput")

    xq_s = nc.dram_tensor("xq_s", [D, T], E4, **skind)
    QH_s = nc.dram_tensor("QH_s", [D, T], BF16, **skind)
    QL_s = nc.dram_tensor("QL_s", [D, T], BF16, **skind)
    KH_s = nc.dram_tensor("KH_s", [D, T], BF16, **skind)
    KL_s = nc.dram_tensor("KL_s", [D, T], BF16, **skind)
    VH_s = nc.dram_tensor("VH_s", [NH, T, HD], BF16, **skind)
    VL_s = nc.dram_tensor("VL_s", [NH, T, HD], BF16, **skind)
    CH_s = nc.dram_tensor("CH_s", [D, T], BF16, **skind)
    CL_s = nc.dram_tensor("CL_s", [D, T], BF16, **skind)
    FF_s = nc.dram_tensor("FF_s", [T, D], BF16, **skind)

    B = Builder()

    with ExitStack() as top:
        e = top.enter_context
        id32 = e(nc.sbuf_tensor("id32s", [128, 128], F32))
        id16 = e(nc.sbuf_tensor("id16s", [128, 128], BF16))
        tri = e(nc.sbuf_tensor("tris", [128, 128], F32))
        qb_s = e(nc.sbuf_tensor("qbs", [128, ND], F32))
        kb_s = e(nc.sbuf_tensor("kbs", [128, ND], F32))
        vb_s = e(nc.sbuf_tensor("vbs", [128, ND], F32))
        ones1 = e(nc.sbuf_tensor("ones1", [1, 128], F32))
        sems = {}
        for s in ENGS:
            sems[s] = e(nc.semaphore("sem_" + s))
        barsem = e(nc.semaphore("sem_bar"))

        B.dma("sp", id32[:, :], id32_h.ap(), writes=["id32"])
        B.dma("sp", id16[:, :], id16_h.ap(), writes=["id16"])
        B.dma("sp", tri[:, :], tri_h.ap(), writes=["tri"])
        for bs, bh, key in ((qb_s, qb_h, "qb"), (kb_s, kb_h, "kb"), (vb_s, vb_h, "vb")):
            B.dma("sp", ap(bs, 0, [[ND, 128], [1, ND]]),
                  ap(bh, 0, [[1, 128], [128, ND]]), writes=[key],
                  allow_slow_non_contiguous=True)
        B.op("dve", lambda v: v.memset(ones1[:, :], 1.0), writes=["ones1"])

        if True:
            with ExitStack() as pab:
              xnh = pab.enter_context(nc.sbuf_tensor("xnh", [128, ND * T], BF16))
              xnl = pab.enter_context(nc.sbuf_tensor("xnl", [128, ND * T], BF16))
              with ExitStack() as pa:
                  ea = pa.enter_context
                  xt = [ea(nc.sbuf_tensor(f"xt{i}", [128, D], F32)) for i in range(2)]
                  dbuf = [ea(nc.sbuf_tensor(f"dbuf{i}", [128, D], F32)) for i in range(2)]
                  sqb = ea(nc.sbuf_tensor("sqb", [128, D], F32))
                  xnt = [ea(nc.sbuf_tensor(f"xnt{i}", [128, D], F32)) for i in range(2)]
                  st_sum = ea(nc.sbuf_tensor("st_sum", [128, 1], F32))
                  st_negm = ea(nc.sbuf_tensor("st_negm", [128, 1], F32))
                  st_vsum = ea(nc.sbuf_tensor("st_vsum", [128, 1], F32))
                  st_ve = ea(nc.sbuf_tensor("st_ve", [128, 1], F32))
                  st_sv = ea(nc.sbuf_tensor("st_sv", [128, 1], F32))
                  st_r1 = ea(nc.sbuf_tensor("st_r1", [128, 1], F32))
                  st_n1 = ea(nc.sbuf_tensor("st_n1", [128, 1], F32))
                  st_n3 = ea(nc.sbuf_tensor("st_n3", [128, 1], F32))
                  st_rs = ea(nc.sbuf_tensor("st_rs", [128, 1], F32))
                  xqst = [ea(nc.sbuf_tensor(f"xqst{i}", [128, 512], E4)) for i in range(2)]
                  ptp = [ea(nc.psum_tensor(f"ptp{i}", [128, 512], F32)) for i in range(2)]

                  for ti in range(NT):
                      p2 = ti % 2
                      B.dma("sp", xt[p2][:, :], x_h.ap()[ti * 128:(ti + 1) * 128, :],
                            writes=[f"xt{p2}"], tag=f"palod{ti}")
                      B.op("dve", lambda v, p=p2: v.tensor_reduce(
                          st_sum[:, :], xt[p][:, :], AX.X, ALU.add),
                          reads=[f"xt{p2}"], writes=["stats"], fence=True)
                      B.op("dve", lambda v: v.tensor_scalar(
                          st_negm[:, :], st_sum[:, :], -1.0 / D, None, ALU.mult),
                          reads=["stats"], writes=["stats"], fence=True)
                      B.op("dve", lambda v, p=p2: v.tensor_scalar(
                          dbuf[p][:, :], xt[p][:, :], st_negm[:, :], None, ALU.add),
                          reads=[f"xt{p2}", "stats"], writes=[f"dbuf{p2}"])
                      B.op("act", lambda a, p=p2: a.activation(
                          sqb[:, :], dbuf[p][:, :], AF.Square),
                          reads=[f"dbuf{p2}"], writes=["sqb"], fence=True)
                      B.op("dve", lambda v: v.tensor_reduce(
                          st_vsum[:, :], sqb[:, :], AX.X, ALU.add),
                          reads=["sqb"], writes=["stats"], fence=True)
                      B.op("act", lambda a: a.activation(
                          st_ve[:, :], st_vsum[:, :], AF.Copy, scale=1.0 / D, bias=EPS),
                          reads=["stats"], writes=["stats"], fence=True)
                      B.op("act", lambda a: a.activation(st_sv[:, :], st_ve[:, :], AF.Sqrt),
                           reads=["stats"], writes=["stats"], fence=True)
                      B.op("dve", lambda v: v.reciprocal(st_r1[:, :], st_sv[:, :]),
                           reads=["stats"], writes=["stats"], fence=True)
                      B.op("dve", lambda v: v.tensor_mul(st_n1[:, :], st_r1[:, :], st_r1[:, :]),
                           reads=["stats"], writes=["stats"], fence=True)
                      B.op("dve", lambda v: v.tensor_mul(st_n3[:, :], st_n1[:, :], st_ve[:, :]),
                           reads=["stats"], writes=["stats"], fence=True)
                      B.op("dve", lambda v: v.tensor_scalar(
                          st_n1[:, :], st_n3[:, :], -0.5, 1.5, ALU.mult, ALU.add),
                          reads=["stats"], writes=["stats"], fence=True)
                      B.op("dve", lambda v: v.tensor_mul(st_rs[:, :], st_r1[:, :], st_n1[:, :]),
                           reads=["stats"], writes=["stats"], fence=True)
                      B.op("dve", lambda v, p=p2: v.tensor_scalar(
                          xnt[p][:, :], dbuf[p][:, :], st_rs[:, :], None, ALU.mult),
                          reads=[f"dbuf{p2}", "stats"], writes=[f"xnt{p2}"])
                      for g in range(4):
                          gp = g % 2
                          def mk_tr(p, g, gp):
                              def f(t):
                                  r = None
                                  for k in range(4):
                                      ci = g * 4 + k
                                      r = t.transpose(ptp[gp][:, k * 128:(k + 1) * 128],
                                                      xnt[p][:, ci * 128:(ci + 1) * 128],
                                                      id32[:, :])
                                  return r
                              return f
                          B.op("te", mk_tr(p2, g, gp), reads=[f"xnt{p2}", "id32"],
                               writes=[f"P!ptp{gp}"], tag=f"patr{ti}_{g}")
                          c0 = (g * 4) * T + ti * 128
                          def mk_hi(gp, c0):
                              def f(v):
                                  return v.tensor_copy(
                                      ap(xnh, c0, [[ND * T, 128], [T, 4], [1, 128]]),
                                      ptp[gp][:, :])
                              return f
                          def mk_lo(gp, c0):
                              def f(v):
                                  return v.tensor_sub(
                                      ap(xnl, c0, [[ND * T, 128], [T, 4], [1, 128]]),
                                      ptp[gp][:, :],
                                      ap(xnh, c0, [[ND * T, 128], [T, 4], [1, 128]]))
                              return f
                          B.op("dve", mk_hi(gp, c0), reads=[f"P!ptp{gp}"],
                               writes=["xnh"], tag=f"pahi{ti}_{g}")
                          B.op("dve", mk_lo(gp, c0), reads=[f"P!ptp{gp}", "xnh"],
                               writes=["xnl"], tag=f"palo{ti}_{g}")
                          B.op("act", lambda a, gp=gp, p=p2: a.activation(
                              xqst[gp][:, :], ptp[gp][:, :], AF.Copy),
                              reads=[f"P!ptp{gp}"], writes=[f"xqst{gp}"])
                          B.dma("actq",
                                ap(xq_s, (g * 4 * 128) * T + ti * 128,
                                   [[T, 128], [128 * T, 4], [1, 128]]),
                                ap(xqst[gp], 0, [[512, 128], [128, 4], [1, 128]]),
                                reads=[f"xqst{gp}"], tag=f"paxq{ti}_{g}")
                  B.barrier()

              with ExitStack() as pb:
                  eb = pb.enter_context
                  wrow = [eb(nc.sbuf_tensor(f"wrow{i}", [128, D], F32)) for i in range(2)]
                  wTh = [eb(nc.sbuf_tensor(f"wTh{i}", [128, D], BF16)) for i in range(2)]
                  wTl = [eb(nc.sbuf_tensor(f"wTl{i}", [128, D], BF16)) for i in range(2)]
                  qf = eb(nc.sbuf_tensor("qf", [128, T], F32))
                  qhi = eb(nc.sbuf_tensor("qhi", [128, T], BF16))
                  qlo = eb(nc.sbuf_tensor("qlo", [128, T], BF16))
                  vth_s = eb(nc.sbuf_tensor("vth_s", [128, T], BF16))
                  vtl_s = eb(nc.sbuf_tensor("vtl_s", [128, T], BF16))
                  wtp = [eb(nc.psum_tensor(f"wtp{i}", [128, 512], F32)) for i in range(2)]
                  pQ = [eb(nc.psum_tensor(f"pQ{i}", [128, min(1024, T)], F32)) for i in range(2)]
                  vtp = eb(nc.psum_tensor("vtp", [128, 512], BF16))

                  HALF = min(1024, T)
                  NHALF = T // HALF
                  for wi, (W_h, bkey, QHd, QLd) in enumerate((
                          (qw_h, "qb", QH_s, QL_s), (kw_h, "kb", KH_s, KL_s),
                          (vw_h, "vb", VH_s, VL_s))):
                      bias_s = (qb_s, kb_s, vb_s)[wi]
                      for m in range(ND):
                          p2 = m % 2
                          B.dma("sp", wrow[p2][:, :], W_h.ap()[m * 128:(m + 1) * 128, :],
                                writes=[f"wrow{p2}"], tag=f"pbw{wi}_{m}")
                          for g in range(4):
                              gp = g % 2
                              def mk_wtr(p, g, gp):
                                  def f(t):
                                      r = None
                                      for k in range(4):
                                          ci = g * 4 + k
                                          r = t.transpose(
                                              wtp[gp][:, k * 128:(k + 1) * 128],
                                              wrow[p][:, ci * 128:(ci + 1) * 128],
                                              id32[:, :])
                                      return r
                                  return f
                              B.op("te", mk_wtr(p2, g, gp),
                                   reads=[f"wrow{p2}", "id32"], writes=[f"P!wtp{gp}"])
                              B.op("dve", lambda v, p=p2, g=g, gp=gp:
                                   v.tensor_copy(wTh[p][:, g * 512:(g + 1) * 512],
                                                 wtp[gp][:, :]),
                                   reads=[f"P!wtp{gp}"], writes=[f"wTh{p2}"])
                              B.op("dve", lambda v, p=p2, g=g, gp=gp:
                                   v.tensor_sub(wTl[p][:, g * 512:(g + 1) * 512],
                                                wtp[gp][:, :],
                                                wTh[p][:, g * 512:(g + 1) * 512]),
                                   reads=[f"P!wtp{gp}", f"wTh{p2}"], writes=[f"wTl{p2}"])
                          postmul = SCALE if wi == 0 else 1.0
                          for hf in range(NHALF):
                              def mk_mm(p, hf, ci):
                                  def f(t):
                                      r = None
                                      for pss in range(3):
                                          lhs = (wTh, wTh, wTl)[pss][p][:, ci * 128:(ci + 1) * 128]
                                          rsrc = (xnh, xnl, xnh)[pss]
                                          for (s0, s1) in slices(HALF):
                                              col = ci * T + hf * HALF + s0
                                              r = t.matmul(
                                                  pQ[hf % 2][:, s0:s1],
                                                  lhs,
                                                  ap(rsrc, col, [[ND * T, 128], [1, s1 - s0]]),
                                                  start=(ci == 0 and pss == 0),
                                                  stop=(ci == ND - 1 and pss == 2))
                                      return r
                                  return f
                              for ci in range(ND):
                                  B.op("te", mk_mm(p2, hf, ci),
                                       reads=[f"wTh{p2}", f"wTl{p2}", "xnh", "xnl"],
                                       writes=[f"P!pQ{hf % 2}"], tag=f"pbmm{wi}_{m}_{hf}_{ci}")
                              B.op("dve", lambda v, hf=hf, m=m, b=bias_s, pm=postmul:
                                  v.tensor_scalar(
                                      qf[:, hf * HALF:(hf + 1) * HALF], pQ[hf % 2][:, :],
                                      b[:, m:m + 1], pm, ALU.add, ALU.mult),
                                  reads=[f"P!pQ{hf % 2}", bkey], writes=["qf"])
                              B.op("dve", lambda v, hf=hf: v.tensor_copy(
                                  qhi[:, hf * HALF:(hf + 1) * HALF],
                                  qf[:, hf * HALF:(hf + 1) * HALF]),
                                  reads=["qf"], writes=["qhi"])
                              B.op("dve", lambda v, hf=hf: v.tensor_sub(
                                  qlo[:, hf * HALF:(hf + 1) * HALF],
                                  qf[:, hf * HALF:(hf + 1) * HALF],
                                  qhi[:, hf * HALF:(hf + 1) * HALF]),
                                  reads=["qf", "qhi"], writes=["qlo"])
                          if wi < 2:
                              B.dma("actq", QHd.ap()[m * 128:(m + 1) * 128, :], qhi[:, :],
                                    reads=["qhi"], tag=f"pbsth{wi}_{m}")
                              B.dma("actq", QLd.ap()[m * 128:(m + 1) * 128, :], qlo[:, :],
                                    reads=["qlo"], tag=f"pbstl{wi}_{m}")
                          else:
                              for hl in range(2):
                                  src = (qhi, qlo)[hl]
                                  stg = (vth_s, vtl_s)[hl]
                                  skey = ("qhi", "qlo")[hl]
                                  tkey = ("vth_s", "vtl_s")[hl]
                                  Vd = (VH_s, VL_s)[hl]
                                  for (t0, t1) in groups4(NT):
                                      def mk_vtr(src, t0, t1):
                                          def f(t):
                                              r = None
                                              for tj in range(t0, t1):
                                                  r = t.transpose(
                                                      vtp[:, (tj - t0) * 128:(tj - t0 + 1) * 128],
                                                      src[:, tj * 128:(tj + 1) * 128],
                                                      id16[:, :])
                                              return r
                                          return f
                                      B.op("te", mk_vtr(src, t0, t1), reads=[skey, "id16"],
                                           writes=["P!vtp"])
                                      B.op("dve", lambda v, stg=stg, t0=t0, t1=t1: v.tensor_copy(
                                          stg[:, t0 * 128:t1 * 128],
                                          vtp[:, 0:(t1 - t0) * 128]),
                                          reads=["P!vtp"], writes=[tkey])
                                  nh2 = (NT + 1) // 2
                                  for half in range(2):
                                      t0 = half * nh2
                                      cnt = min(NT, t0 + nh2) - t0
                                      if cnt <= 0:
                                          continue
                                      B.dma("actq",
                                            ap(Vd, m * T * HD + t0 * 128 * HD,
                                               [[HD, 128], [128 * HD, cnt], [1, HD]]),
                                            ap(stg, t0 * 128, [[T, 128], [128, cnt], [1, HD]]),
                                            reads=[tkey], tag=f"pbstv{hl}_{m}_{half}")
                  B.barrier()

            with ExitStack() as pc:
                ec = pc.enter_context
                qhh = [ec(nc.sbuf_tensor(f"qhh{i}", [128, T], BF16)) for i in range(2)]
                qhl = [ec(nc.sbuf_tensor(f"qhl{i}", [128, T], BF16)) for i in range(2)]
                khh = [ec(nc.sbuf_tensor(f"khh{i}", [128, T], BF16)) for i in range(2)]
                khl = [ec(nc.sbuf_tensor(f"khl{i}", [128, T], BF16)) for i in range(2)]
                vthb = [ec(nc.sbuf_tensor(f"vthb{i}", [128, T], BF16)) for i in range(2)]
                vtlb = [ec(nc.sbuf_tensor(f"vtlb{i}", [128, T], BF16)) for i in range(2)]
                pbuf = [ec(nc.sbuf_tensor(f"pbuf{i}", [128, T], F32)) for i in range(2)]
                phib = [ec(nc.sbuf_tensor(f"phib{i}", [128, T], BF16)) for i in range(2)]
                plob = [ec(nc.sbuf_tensor(f"plob{i}", [128, T], BF16)) for i in range(2)]
                ptb = ec(nc.sbuf_tensor("ptb", [128, 2 * T], BF16))
                ctxh = ec(nc.sbuf_tensor("ctxh", [128, T], F32))
                cth = [ec(nc.sbuf_tensor(f"cth{i}", [128, T], BF16)) for i in range(2)]
                ctl = [ec(nc.sbuf_tensor(f"ctl{i}", [128, T], BF16)) for i in range(2)]
                a_negm = [ec(nc.sbuf_tensor(f"a_negm{i}", [128, 1], F32)) for i in range(2)]
                a_sum = [ec(nc.sbuf_tensor(f"a_sum{i}", [128, 1], F32)) for i in range(2)]
                a_rec = [ec(nc.sbuf_tensor(f"a_rec{i}", [128, 1], F32)) for i in range(2)]
                Sps = ec(nc.psum_tensor("P!Sps", [128, T if T <= 2048 else 2048], F32))
                ptpc = [ec(nc.psum_tensor(f"P!ptpc{i}", [128, 1024], BF16)) for i in range(2)]
                ctxp = ec(nc.psum_tensor("P!ctxp", [128, 128], F32))

                def emit_head_loads(h):
                    p2 = h % 2
                    for (buf, S_h) in ((qhh[p2], QH_s), (qhl[p2], QL_s),
                                       (khh[p2], KH_s), (khl[p2], KL_s)):
                        B.dma("sp", buf[:, :], S_h.ap()[h * 128:(h + 1) * 128, :],
                              writes=[f"qkv{p2}"], tag=f"pcl{h}")
                    nh2 = (NT + 1) // 2
                    for (buf, S_h) in ((vthb[p2], VH_s), (vtlb[p2], VL_s)):
                        for half in range(2):
                            t0 = half * nh2
                            cnt = min(NT, t0 + nh2) - t0
                            if cnt <= 0:
                                continue
                            B.dma("sp", ap(buf, t0 * 128, [[T, 128], [128, cnt], [1, HD]]),
                                  ap(S_h, h * T * HD + t0 * 128 * HD,
                                     [[HD, 128], [128 * HD, cnt], [1, HD]]),
                                  writes=[f"qkv{p2}"], tag=f"pclv{h}_{half}")

                def emit_scores(h, at):
                    p2 = h % 2
                    bl = (at + 1) * 128
                    nslc = (bl + 511) // 512
                    def mk_sc(p, at, bl, nslc):
                        def f(t):
                            r = None
                            for pss in range(3):
                                lhs = (qhh, qhh, qhl)[pss][p][:, at * 128:(at + 1) * 128]
                                rsrc = (khh, khl, khh)[pss][p]
                                for s in range(nslc):
                                    c1 = min(bl, (s + 1) * 512)
                                    r = t.matmul(Sps[:, s * 512:c1],
                                                 lhs, rsrc[:, s * 512:c1],
                                                 start=(pss == 0), stop=(pss == 2))
                            return r
                        return f
                    B.op("te", mk_sc(p2, at, bl, nslc),
                         reads=[f"qkv{p2}"], writes=["P!Sps"], tag=f"pcsc{h}_{at}")

                def emit_softmax(h, at):
                    bl = (at + 1) * 128
                    q = at % 2
                    B.op("dve", lambda v, at=at: v.tensor_add(
                        Sps[:, at * 128:(at + 1) * 128],
                        Sps[:, at * 128:(at + 1) * 128], tri[:, :]),
                        reads=["P!Sps", "tri"], writes=["P!Sps"])
                    B.op("dve", lambda v, bl=bl, q=q: v.tensor_reduce(
                        a_negm[q][:, :], Sps[:, 0:bl], AX.X, ALU.max, negate=True),
                        reads=["P!Sps"], writes=[f"ast{q}"], fence=True)
                    B.op("act", lambda a, bl=bl, q=q: a.activation(
                        pbuf[q][:, 0:bl], Sps[:, 0:bl], AF.Exp,
                        bias=a_negm[q][:, :], scale=1.0,
                        accum_out=a_sum[q][:, :]),
                        reads=["P!Sps", f"ast{q}"], writes=[f"pbuf{q}", f"ast{q}"],
                        fence=True)
                    B.op("dve", lambda v, q=q: v.reciprocal(a_rec[q][:, :], a_sum[q][:, :]),
                         reads=[f"ast{q}"], writes=[f"ast{q}"], fence=True)
                    B.op("act", lambda a, bl=bl, q=q: a.activation(
                        phib[q][:, 0:bl], pbuf[q][:, 0:bl], AF.Copy, scale=a_rec[q][:, :]),
                        reads=[f"pbuf{q}", f"ast{q}"], writes=[f"phib{q}"])
                    B.op("dve", lambda v, bl=bl, q=q: v.scalar_tensor_tensor(
                        plob[q][:, 0:bl], pbuf[q][:, 0:bl], a_rec[q][:, :], phib[q][:, 0:bl],
                        ALU.mult, ALU.subtract),
                        reads=[f"pbuf{q}", f"ast{q}", f"phib{q}"], writes=[f"plob{q}"])

                def emit_groups(h, at):
                    p2 = h % 2
                    q = at % 2
                    nb = at + 1
                    for g in range((nb + 3) // 4):
                        k0 = g * 4
                        k1 = min(nb, k0 + 4)
                        gp = g % 2
                        def mk_ptr(q, k0, k1, gp):
                            def f(t):
                                r = None
                                for hl in range(2):
                                    src = (phib, plob)[hl][q]
                                    for k in range(k0, k1):
                                        r = t.transpose(
                                            ptpc[gp][:, hl * 512 + (k - k0) * 128:
                                                     hl * 512 + (k - k0 + 1) * 128],
                                            src[:, k * 128:(k + 1) * 128], id16[:, :])
                                return r
                            return f
                        B.op("te", mk_ptr(q, k0, k1, gp),
                             reads=[f"phib{q}", f"plob{q}", "id16"], writes=[f"P!ptpc{gp}"])
                        def mk_pev(k0, k1, gp):
                            def f(v):
                                return v.tensor_copy(
                                    ap(ptb, k0 * 128, [[2 * T, 128], [T, 2],
                                                       [1, (k1 - k0) * 128]]),
                                    ap(ptpc[gp], 0, [[1024, 128], [512, 2],
                                                     [1, (k1 - k0) * 128]]))
                            return f
                        B.op("dve", mk_pev(k0, k1, gp),
                             reads=[f"P!ptpc{gp}"], writes=["ptb"])
                        for k in range(k0, k1):
                            def mk_ctx(p, k, at):
                                def f(t):
                                    r = None
                                    for pss in range(3):
                                        lhs = (vthb, vthb, vtlb)[pss][p][:, k * 128:(k + 1) * 128]
                                        off = (0, T, 0)[pss]
                                        r = t.matmul(ctxp[:, :], lhs,
                                                     ptb[:, off + k * 128:off + (k + 1) * 128],
                                                     start=(k == 0 and pss == 0),
                                                     stop=(k == at and pss == 2))
                                    return r
                                return f
                            B.op("te", mk_ctx(p2, k, at),
                                 reads=[f"qkv{p2}", "ptb"], writes=["P!ctxp"],
                                 tag=f"pcctx{h}_{at}_{k}")

                def emit_ctx_evac(h, at):
                    B.op("act", lambda a, at=at: a.activation(
                        ctxh[:, at * 128:(at + 1) * 128], ctxp[:, :], AF.Copy),
                        reads=["P!ctxp"], writes=["ctxh"])

                emit_head_loads(0)
                for h in range(NH):
                    p2 = h % 2
                    emit_scores(h, 0)
                    for at in range(NT):
                        emit_softmax(h, at)
                        if at + 1 < NT:
                            emit_scores(h, at + 1)
                        if at > 0:
                            emit_ctx_evac(h, at - 1)
                        emit_groups(h, at)
                        if h + 1 < NH and at == 0:
                            emit_head_loads(h + 1)
                    emit_ctx_evac(h, NT - 1)
                    B.op("dve", lambda v, p=p2: v.tensor_copy(cth[p][:, :], ctxh[:, :]),
                         reads=["ctxh"], writes=[f"cth{p2}"])
                    B.op("dve", lambda v, p=p2: v.tensor_sub(
                        ctl[p][:, :], ctxh[:, :], cth[p][:, :]),
                        reads=["ctxh", f"cth{p2}"], writes=[f"ctl{p2}"])
                    B.dma("actq", CH_s.ap()[h * 128:(h + 1) * 128, :], cth[p2][:, :],
                          reads=[f"cth{p2}"], tag=f"pcsth{h}")
                    B.dma("actq", CL_s.ap()[h * 128:(h + 1) * 128, :], ctl[p2][:, :],
                          reads=[f"ctl{p2}"], tag=f"pcstl{h}")
                B.barrier()

            with ExitStack() as pe_outer:
              aq = pe_outer.enter_context(nc.sbuf_tensor("aq", [128, (HID // 128) * T], E4))
              with ExitStack() as pe:
                ee = pe.enter_context
                xq_sb = ee(nc.sbuf_tensor("xq_sb", [128, ND * T], E4))
                wgs = [[ee(nc.sbuf_tensor(f"wgs{g}{i}", [128, D], F32)) for i in range(2)]
                       for g in range(2)]
                wgq = [[ee(nc.sbuf_tensor(f"wgq{g}{i}", [128, D], E4)) for i in range(2)]
                       for g in range(2)]
                GH = min(1024, T)
                NGH = T // GH
                gb = [ee(nc.sbuf_tensor(f"gb{i}", [128, GH], BF16)) for i in range(2)]
                fb = [ee(nc.sbuf_tensor(f"fb{i}", [128, GH], BF16)) for i in range(2)]
                sA = [ee(nc.sbuf_tensor(f"sA{i}", [128, GH], BF16)) for i in range(2)]
                sB = [ee(nc.sbuf_tensor(f"sB{i}", [128, GH], BF16)) for i in range(2)]
                pg = [[ee(nc.psum_tensor(f"pg{g}{i}", [128, GH], F32)) for i in range(2)]
                      for g in range(2)]

                for ci in range(ND):
                    B.dma("sp", xq_sb[:, ci * T:(ci + 1) * T],
                          xq_s.ap()[ci * 128:(ci + 1) * 128, :],
                          writes=["xq_sb"], tag=f"pexq{ci}")
                B.op("dve", lambda v: v.tensor_copy(
                    xq_sb[:, :].bitcast(U32), xq_sb[:, :].bitcast(U32)),
                    reads=["xq_sb"], writes=["xq_sb"], tag="lxq")
                for m in range(HID // 128):
                    p2 = m % 2
                    for g in range(2):
                        for q4 in range(4):
                            B.dma("sp",
                                  ap(wgs[g][p2], q4 * 4 * 128,
                                     [[D, 128], [128, 4], [1, 128]]),
                                  ap(wg_h, g * D * HID + (q4 * 4 * 128) * HID + m * 128,
                                     [[HID, 128], [128 * HID, 4], [1, 128]]),
                                  writes=[f"wgs{g}{p2}"], tag=f"pewg{g}_{m}_{q4}")
                        B.op("dve", lambda v, g=g, p=p2: v.tensor_copy(
                            wgq[g][p][:, :], wgs[g][p][:, :]),
                            reads=[f"wgs{g}{p2}"], writes=[f"wgq{g}{p2}"])
                        B.op("dve", lambda v, g=g, p=p2: v.tensor_copy(
                            wgq[g][p][:, :].bitcast(U32), wgq[g][p][:, :].bitcast(U32)),
                            reads=[f"wgq{g}{p2}"], writes=[f"wgq{g}{p2}"], tag="lwg")
                    for hf in range(NGH):
                        for g in range(2):
                            def mk_gmm(g, p, hf):
                                def f(t):
                                    r = None
                                    for ci in range(0, ND, 2):
                                        for (s0, s1) in slices(GH):
                                            col = ci * T + hf * GH + s0
                                            r = t.matmul(
                                                pg[g][hf % 2][:, s0:s1],
                                                ap(wgq[g][p], ci * 128,
                                                   [[D, 128], [128, 2], [1, 128]]),
                                                ap(xq_sb, col,
                                                   [[ND * T, 128], [T, 2], [1, s1 - s0]]),
                                                start=(ci == 0), stop=(ci == ND - 2),
                                                perf_mode=DR)
                                    return r
                                return f
                            B.op("te", mk_gmm(g, p2, hf),
                                 reads=[f"wgq{g}{p2}", "xq_sb"],
                                 writes=[f"P!pg{g}{hf % 2}"], tag=f"pegm{g}_{m}_{hf}")
                        hp = hf % 2
                        pg0, pg1 = pg[0][hp], pg[1][hp]
                        r0, r1 = f"P!pg0{hp}", f"P!pg1{hp}"
                        gbk, fbk = f"gb{hp}", f"fb{hp}"
                        sak, sbk = f"sA{hp}", f"sB{hp}"
                        B.op("dve", lambda v, hp=hp, pg0=pg0: v.tensor_copy(
                            gb[hp][:, :], pg0[:, :]), reads=[r0], writes=[gbk])
                        B.op("dve", lambda v, hp=hp, pg1=pg1: v.tensor_copy(
                            fb[hp][:, :], pg1[:, :]), reads=[r1], writes=[fbk])
                        B.op("act", lambda a, hp=hp: a.activation(
                            sA[hp][:, :], gb[hp][:, :], AF.Square),
                            reads=[gbk], writes=[sak])
                        B.op("dve", lambda v, hp=hp: v.tensor_mul(
                            sB[hp][:, :], sA[hp][:, :], gb[hp][:, :]),
                            reads=[sak, gbk], writes=[sbk])
                        B.op("dve", lambda v, hp=hp: v.tensor_scalar(
                            sA[hp][:, :], sB[hp][:, :], GA, None, ALU.mult),
                            reads=[sbk], writes=[sak])
                        B.op("dve", lambda v, hp=hp: v.tensor_add(
                            sB[hp][:, :], sA[hp][:, :], gb[hp][:, :]),
                            reads=[sak, gbk], writes=[sbk])
                        B.op("dve", lambda v, hp=hp: v.tensor_scalar(
                            sA[hp][:, :], sB[hp][:, :], GC, None, ALU.mult),
                            reads=[sbk], writes=[sak])
                        B.op("act", lambda a, hp=hp: a.activation(
                            sB[hp][:, :], sA[hp][:, :], AF.Tanh),
                            reads=[sak], writes=[sbk])
                        B.op("dve", lambda v, hp=hp: v.tensor_scalar(
                            sA[hp][:, :], sB[hp][:, :], 0.5, 0.5, ALU.mult, ALU.add),
                            reads=[sbk], writes=[sak])
                        B.op("dve", lambda v, hp=hp: v.tensor_mul(
                            sB[hp][:, :], gb[hp][:, :], sA[hp][:, :]),
                            reads=[gbk, sak], writes=[sbk])
                        B.op("dve", lambda v, hp=hp: v.tensor_mul(
                            sA[hp][:, :], sB[hp][:, :], fb[hp][:, :]),
                            reads=[sbk, fbk], writes=[sak])
                        B.op("act", lambda a, hp=hp, m=m, hf=hf: a.activation(
                            aq[:, m * T + hf * GH:m * T + (hf + 1) * GH],
                            sA[hp][:, :], AF.Copy),
                            reads=[sak], writes=["aq"], tag=f"peaq{m}_{hf}")
                B.op("dve", lambda v: v.tensor_copy(
                    aq[:, :].bitcast(U32), aq[:, :].bitcast(U32)),
                    reads=["aq"], writes=["aq"], tag="laq")
                B.barrier()

              if dbg:
                  ffdbg_h = nc.dram_tensor("ff_fm_dbg", [D, T], BF16,
                                           kind="ExternalOutput")
                  aq_dbg = nc.dram_tensor("aq_dbg", [128, (HID // 128) * T], E4,
                                          kind="ExternalOutput")
                  B.dma("sp", aq_dbg.ap(), aq[:, :], reads=["aq"], tag="aqdbg")
                  B.barrier()

              with ExitStack() as pf:
                  ef = pf.enter_context
                  NH2 = HID // 128
                  wlt = ef(nc.sbuf_tensor("wlt", [128, NH2 * 128], F32))
                  wlq = [ef(nc.sbuf_tensor(f"wlq{i}", [128, NH2 * 128], E4)) for i in range(2)]
                  ffn16 = ef(nc.sbuf_tensor("ffn16", [128, T], BF16))
                  fts = ef(nc.sbuf_tensor("fts", [128, T], BF16))
                  LH = min(1024, T)
                  NLH = T // LH
                  pL = [[ef(nc.psum_tensor(f"pL{j}{i}", [128, LH], F32)) for i in range(NLH)]
                        for j in range(1)]
                  ftp = ef(nc.psum_tensor("P!ftp", [128, 512], BF16))

                  for j in range(ND):
                      jp = 0
                      wp = j % 2
                      for q4 in range(NH2 // 4):
                          B.dma("sp",
                                ap(wlt, q4 * 4 * 128, [[NH2 * 128, 128], [128, 4], [1, 128]]),
                                ap(wl_h, (q4 * 4 * 128) * D + j * 128,
                                   [[D, 128], [128 * D, 4], [1, 128]]),
                                writes=["wlt"], tag=f"pfw{j}_{q4}")
                      B.op("act", lambda a, wp=wp: a.activation(
                          wlq[wp][:, :], wlt[:, :], AF.Copy),
                          reads=["wlt"], writes=[f"wlq{wp}"])
                      B.op("dve", lambda v, wp=wp: v.tensor_copy(
                          wlq[wp][:, :].bitcast(U32), wlq[wp][:, :].bitcast(U32)),
                          reads=[f"wlq{wp}"], writes=[f"wlq{wp}"], tag="lwl")
                      for h2 in range(0, NH2, 2):
                          def mk_lmm(wp, jp, h2):
                              def f(t):
                                  r = None
                                  for hf in range(NLH):
                                      for (s0, s1) in slices(LH):
                                          col = h2 * T + hf * LH + s0
                                          r = t.matmul(
                                              pL[jp][hf][:, s0:s1],
                                              ap(wlq[wp], h2 * 128,
                                                 [[NH2 * 128, 128], [128, 2], [1, 128]]),
                                              ap(aq, col,
                                                 [[(HID // 128) * T, 128], [T, 2], [1, s1 - s0]]),
                                              start=(h2 == 0), stop=(h2 == NH2 - 2),
                                              perf_mode=DR)
                                  return r
                              return f
                          B.op("te", mk_lmm(wp, jp, h2),
                               reads=[f"wlq{wp}", "aq"], writes=[f"P!pL{jp}"],
                               tag=f"pflm{j}_{h2}")
                      for hf in range(NLH):
                          B.op("dve", lambda v, jp=jp, hf=hf: v.tensor_copy(
                              ffn16[:, hf * LH:(hf + 1) * LH], pL[jp][hf][:, :]),
                              reads=[f"P!pL{jp}"], writes=["ffn16"])
                      B.op("dve", lambda v: v.tensor_copy(
                          ffn16[:, :].bitcast(U32), ffn16[:, :].bitcast(U32)),
                          reads=["ffn16"], writes=["ffn16"], tag="lff")
                      if dbg:
                          B.dma("sp", ffdbg_h.ap()[j * 128:(j + 1) * 128, :],
                                ffn16[:, :], reads=["ffn16"], tag=f"ffdbg{j}")
                      for (t0, t1) in groups4(NT):
                          def mk_ftr(t0, t1):
                              def f(t):
                                  r = None
                                  for tj in range(t0, t1):
                                      r = t.transpose(ftp[:, (tj - t0) * 128:(tj - t0 + 1) * 128],
                                                      ffn16[:, tj * 128:(tj + 1) * 128],
                                                      id16[:, :])
                                  return r
                              return f
                          B.op("te", mk_ftr(t0, t1), reads=["ffn16", "id16"], writes=["P!ftp"])
                          B.op("act", lambda a, t0=t0, t1=t1: a.activation(
                              fts[:, t0 * 128:t1 * 128], ftp[:, 0:(t1 - t0) * 128], AF.Copy),
                              reads=["P!ftp"], writes=["fts"])
                      for (t0, t1) in groups4(NT):
                          B.dma("actq",
                                ap(FF_s, j * 128 + (t0 * 128) * D,
                                   [[D, 128], [128 * D, t1 - t0], [1, 128]]),
                                ap(fts, t0 * 128, [[T, 128], [128, t1 - t0], [1, 128]]),
                                reads=["fts"], tag=f"pfst{j}_{t0}")
                  B.barrier()

            with ExitStack() as pd:
                ed = pd.enter_context
                owth = ed(nc.sbuf_tensor("owth", [128, ND * D], BF16))
                owtl = ed(nc.sbuf_tensor("owtl", [128, ND * D], BF16))
                obb = ed(nc.sbuf_tensor("obb", [128, D], F32))
                with ExitStack() as pp:
                    ep = pp.enter_context
                    ob_row = ep(nc.sbuf_tensor("obrow", [1, D], F32))
                    pbc = ep(nc.psum_tensor("P!pbc", [128, D], F32))
                    B.dma("sp", ap(ob_row, 0, [[D, 1], [1, D]]),
                          ap(ob_h, 0, [[D, 1], [1, D]]), writes=["obrow"])
                    def mk_bc(s):
                        def f(t):
                            return t.matmul(pbc[:, s * 512:(s + 1) * 512], ones1[:, :],
                                            ob_row[:, s * 512:(s + 1) * 512],
                                            start=True, stop=True)
                        return f
                    for s in range(4):
                        B.op("te", mk_bc(s), reads=["ones1", "obrow"], writes=["P!pbc"])
                    B.op("dve", lambda v: v.tensor_copy(obb[:, :], pbc[:, :]),
                         reads=["P!pbc"], writes=["obb"])
                    wrow2 = [ep(nc.sbuf_tensor(f"wrow2{i}", [128, D], F32)) for i in range(2)]
                    wth2 = [ep(nc.sbuf_tensor(f"wth2{i}", [128, D], BF16)) for i in range(2)]
                    wtl2 = [ep(nc.sbuf_tensor(f"wtl2{i}", [128, D], BF16)) for i in range(2)]
                    wtp2 = [ep(nc.psum_tensor(f"wtp2{i}", [128, 512], BF16)) for i in range(2)]
                    for m in range(ND):
                        p2 = m % 2
                        B.dma("sp", wrow2[p2][:, :], ow_h.ap()[m * 128:(m + 1) * 128, :],
                              writes=[f"wrow2{p2}"], tag=f"pdw{m}")
                        B.op("dve", lambda v, p=p2: v.tensor_copy(
                            wth2[p][:, :], wrow2[p][:, :]),
                            reads=[f"wrow2{p2}"], writes=[f"wth2{p2}"])
                        B.op("dve", lambda v, p=p2: v.tensor_sub(
                            wtl2[p][:, :], wrow2[p][:, :], wth2[p][:, :]),
                            reads=[f"wrow2{p2}", f"wth2{p2}"], writes=[f"wtl2{p2}"])
                        for hl in range(2):
                            src = (wth2, wtl2)[hl]
                            dstT = (owth, owtl)[hl]
                            skey = (f"wth2{p2}", f"wtl2{p2}")[hl]
                            dkey = ("owth", "owtl")[hl]
                            for g in range(4):
                                gp = g % 2
                                def mk_otr(src, p, g, gp):
                                    def f(t):
                                        r = None
                                        for k in range(4):
                                            ci = g * 4 + k
                                            r = t.transpose(
                                                wtp2[gp][:, k * 128:(k + 1) * 128],
                                                src[p][:, ci * 128:(ci + 1) * 128],
                                                id16[:, :])
                                        return r
                                    return f
                                B.op("te", mk_otr(src, p2, g, gp),
                                     reads=[skey, "id16"], writes=[f"P!wtp2{gp}"])
                                c0 = (g * 4) * D + m * 128
                                def mk_oev(dstT, c0, gp):
                                    def f(v):
                                        return v.tensor_copy(
                                            ap(dstT, c0, [[ND * D, 128], [D, 4], [1, 128]]),
                                            wtp2[gp][:, :])
                                    return f
                                B.op("dve", mk_oev(dstT, c0, gp),
                                     reads=[f"P!wtp2{gp}"], writes=[dkey])
                    B.barrier()
                with ExitStack() as pm:
                    em = pm.enter_context
                    scth = [em(nc.sbuf_tensor(f"scth{i}", [128, D], BF16)) for i in range(2)]
                    sctl = [em(nc.sbuf_tensor(f"sctl{i}", [128, D], BF16)) for i in range(2)]
                    xt2 = [em(nc.sbuf_tensor(f"xt2{i}", [128, D], F32)) for i in range(2)]
                    ffnt = [em(nc.sbuf_tensor(f"ffnt{i}", [128, D], BF16)) for i in range(2)]
                    o1b = em(nc.sbuf_tensor("o1b", [128, D], F32))
                    o2b = em(nc.sbuf_tensor("o2b", [128, D], F32))
                    outsb = [em(nc.sbuf_tensor(f"outsb{i}", [128, D], F32)) for i in range(2)]
                    pO = [em(nc.psum_tensor(f"pO{i}", [128, D], F32)) for i in range(2)]

                    for ti in range(NT):
                        p2 = ti % 2
                        for hl in range(2):
                            Cs = (CH_s, CL_s)[hl]
                            dst = (scth, sctl)[hl]
                            dkey = (f"scth{p2}", f"sctl{p2}")[hl]
                            for q4 in range(ND // 4):
                                B.dma("actq",
                                      ap(dst[p2], q4 * 4 * 128,
                                         [[D, 128], [128, 4], [1, 128]]),
                                      ap(Cs, ti * 128 + (q4 * 4 * 128) * T,
                                         [[T, 128], [128 * T, 4], [1, 128]]),
                                      writes=[dkey], tag=f"pdc{hl}_{ti}_{q4}")
                        B.dma("sp", xt2[p2][:, :], x_h.ap()[ti * 128:(ti + 1) * 128, :],
                              writes=[f"xt2{p2}"], tag=f"pdx{ti}")
                        B.dma("sp", ffnt[p2][:, :], FF_s.ap()[ti * 128:(ti + 1) * 128, :],
                              writes=[f"ffnt{p2}"], tag=f"pdf{ti}")
                        for ci in range(ND):
                            def mk_omm(p, ci):
                                def f(t):
                                    r = None
                                    for pss in range(3):
                                        lhs = (scth, scth, sctl)[pss][p][:, ci * 128:(ci + 1) * 128]
                                        rsrc = (owth, owtl, owth)[pss]
                                        for s in range(D // 512):
                                            r = t.matmul(
                                                pO[p][:, s * 512:(s + 1) * 512],
                                                lhs,
                                                ap(rsrc, ci * D + s * 512,
                                                   [[ND * D, 128], [1, 512]]),
                                                start=(ci == 0 and pss == 0),
                                                stop=(ci == ND - 1 and pss == 2))
                                    return r
                                return f
                            B.op("te", mk_omm(p2, ci),
                                 reads=[f"scth{p2}", f"sctl{p2}", "owth", "owtl"],
                                 writes=[f"P!pO{p2}"], tag=f"pdmm{ti}_{ci}")
                        B.op("dve", lambda v, p=p2: v.tensor_add(
                            o1b[:, :], pO[p][:, :], obb[:, :]),
                            reads=[f"P!pO{p2}", "obb"], writes=["o1b"])
                        B.op("dve", lambda v, p=p2: v.tensor_add(
                            o2b[:, :], o1b[:, :], xt2[p][:, :]),
                            reads=["o1b", f"xt2{p2}"], writes=["o2b"])
                        B.op("dve", lambda v, p=p2: v.tensor_add(
                            outsb[p][:, :], o2b[:, :], ffnt[p][:, :]),
                            reads=["o2b", f"ffnt{p2}"], writes=[f"outsb{p2}"])
                        B.dma("sp", out_h.ap()[ti * 128:(ti + 1) * 128, :], outsb[p2][:, :],
                              reads=[f"outsb{p2}"], tag=f"pdout{ti}")
                    B.barrier()

        if max_ops is not None:
            kept, n = [], 0
            for o in B.ops:
                if o["barrier"]:
                    continue
                kept.append(o)
                n += 1
                if n >= max_ops:
                    break
            for i, o in enumerate(kept):
                o["idx"] = i
            B.ops = kept
            B.barrier()
        if max_barriers is not None:
            nb, cut = 0, len(B.ops)
            for i, o in enumerate(B.ops):
                if o["barrier"]:
                    nb += 1
                    if nb >= max_barriers:
                        cut = i + 1
                        break
            B.ops = B.ops[:cut]
        B.schedule()
        with nc.Block() as block:
            block.tensor(lambda t: B.emit_engine(("te",), t, sems, barsem, 4))
            block.vector(lambda v: B.emit_engine(("dve",), v, sems, barsem, 4))
            block.scalar(lambda a: B.emit_engine(("act", "actq"), a, sems, barsem, 4))
            block.sync(lambda s: B.emit_engine(("sp",), s, sems, barsem, 4))
    nops = len(B.ops)
    return nc, nops


def _consts():
    id32 = np.eye(128, dtype=np.float32)
    id16 = np.eye(128, dtype=np.float32).astype(ml_dtypes.bfloat16)
    tri = np.triu(np.full((128, 128), NEG, dtype=np.float32), 1)
    return id32, id16, tri


def kernel(L0_attn_x, L0_attn_q_w, L0_attn_q_b, L0_attn_k_w, L0_attn_k_b,
           L0_attn_v_w, L0_attn_v_b, L0_attn_o_w, L0_attn_o_b,
           L0_ffn_w_gating, L0_ffn_w_linear):
    x = np.ascontiguousarray(np.asarray(L0_attn_x, dtype=np.float32))
    nbatch, T, _ = x.shape
    id32, id16, tri = _consts()
    nc, _ = build_nc(T)
    common = dict(
        qw=np.ascontiguousarray(np.asarray(L0_attn_q_w, np.float32)),
        kw=np.ascontiguousarray(np.asarray(L0_attn_k_w, np.float32)),
        vw=np.ascontiguousarray(np.asarray(L0_attn_v_w, np.float32)),
        ow=np.ascontiguousarray(np.asarray(L0_attn_o_w, np.float32)),
        qb=np.ascontiguousarray(np.asarray(L0_attn_q_b, np.float32)),
        kb=np.ascontiguousarray(np.asarray(L0_attn_k_b, np.float32)),
        vb=np.ascontiguousarray(np.asarray(L0_attn_v_b, np.float32)),
        ob=np.ascontiguousarray(np.asarray(L0_attn_o_b, np.float32)),
        wg=np.ascontiguousarray(np.asarray(L0_ffn_w_gating, np.float32)),
        wl=np.ascontiguousarray(np.asarray(L0_ffn_w_linear, np.float32)),
        id32=id32, id16=id16, tri=tri,
    )
    in_maps = [dict(common, x=np.ascontiguousarray(x[b])) for b in range(nbatch)]
    res = run_bass_kernel_spmd(nc, in_maps, core_ids=list(range(nbatch)))
    return np.stack([res.results[b]["out"] for b in range(nbatch)], axis=0)
